# version 03
# baseline (speedup 1.0000x reference)
import contextlib
import math

import numpy as np
import ml_dtypes
import bass_rust
import concourse.bass as bass
import concourse.mybir as mybir
from concourse.bass_utils import run_bass_kernel_spmd

F32 = mybir.dt.float32
BF16 = mybir.dt.bfloat16
AF = mybir.ActivationFunctionType
ALU = mybir.AluOpType
AX = mybir.AxisListType
bf16 = ml_dtypes.bfloat16

B, T, C = 8, 2048, 1024
NH, HD = 16, 64
NCL, GS = 64, 256
NTPC = T // NCL
G = T // GS
NCPG = GS // NTPC
CMAX = (G - 1) * NCPG
NLM = 64
NE = 8

NCHUNK = 16
NQ = 4


def _sinusoidal(n, d):
    pos = np.arange(n, dtype=np.float32)[:, None]
    div = np.exp((np.arange(0, d, 2, dtype=np.float32) * np.float32(-math.log(10000.0) / d)).astype(np.float32))
    ang = (pos * div).astype(np.float32)
    return np.stack([np.sin(ang), np.cos(ang)], axis=-1).reshape(n, d).astype(np.float32)


def _host_consts():
    S = _sinusoidal(CMAX + GS, HD)
    sinv = S[:, 0:32].T
    cosv = S[:, 32:64].T
    rotq = np.zeros((128, 512), np.float32)
    rotkl = np.zeros((128, 2048), np.float32)
    scale = np.float32(1.0 / math.sqrt(HD))
    for r in range(128):
        i = (r % 64) // 2
        tq = np.arange(512) % 256
        tkl = 8 * (np.arange(2048) // 256) + (np.arange(2048) % 256)
        if r % 2 == 0:
            rotq[r] = -cosv[i, tq] * scale
            rotkl[r] = -cosv[i, tkl]
        else:
            rotq[r] = sinv[i, tq] * scale
            rotkl[r] = sinv[i, tkl]
    tkc = np.zeros((64, 1024), np.float32)
    for c in range(CMAX):
        v = np.zeros(64, np.float32)
        v[0::2] = -cosv[:, c]
        v[1::2] = sinv[:, c]
        tkc[c] = np.tile(v, NH)
    tri = np.triu(np.full((128, 128), -1e30, np.float32), 1)
    perm = np.zeros((128, 128), np.float32)
    for k in range(128):
        perm[k, k ^ 1] = 1.0
    idn = np.eye(128, dtype=np.float32)
    blk = np.zeros((64, 128), np.float32)
    blk[np.arange(128) % 64, np.arange(128)] = 1.0
    return {
        "ROTQ": rotq.astype(bf16), "ROTKL": rotkl.astype(bf16),
        "TKC": tkc.astype(bf16), "TRI": tri.astype(bf16),
        "PERM": perm.astype(bf16), "IDB": idn.astype(bf16),
        "C7": (7.0 * blk).astype(bf16), "C15": (15.0 * blk).astype(bf16),
        "C13": (13.0 * blk).astype(bf16),
    }


ENG2ATTR = {"pe": "tensor", "dve": "vector", "act": "scalar", "pool": "gpsimd", "sp": "sync"}


class Bld:
    def __init__(self, nc, stack):
        self.nc = nc
        self.stk = stack
        self.q = {e: [] for e in ENG2ATTR}
        self.sem = {}
        self.cnt = {e: 0 for e in ENG2ATTR}
        self.seen = {e: {} for e in ENG2ATTR}
        self.bufs = {}
        self.ch = {}
        for e in ENG2ATTR:
            self.sem[e] = stack.enter_context(nc.semaphore("sem_" + e))
        for name, eng in [("Ld", "sp"), ("Tx", "sp"), ("St", "sp"),
                          ("Wg", "pool"), ("Sc", "pool")]:
            self.sem[name] = stack.enter_context(nc.semaphore("sem_" + name))
            self.ch[name] = {"eng": eng, "issued": 0, "fenced": 0, "maxwait": 0}

    def _collect(self, reads, writes):
        waits = {}

        def add(tok):
            if tok is None:
                return
            s, v = tok
            if waits.get(s, -1) < v:
                waits[s] = v

        for k in reads:
            b = self.bufs.get(k)
            if b:
                add(b["w"])
        for k in writes:
            b = self.bufs.get(k)
            if b:
                add(b["w"])
                for s, v in b["r"].items():
                    add((s, v))
        return waits

    def _wait_tok_for(self, s):
        if s in self.ch:
            c = self.ch[s]
            v = 16 * c["issued"]
            c["maxwait"] = max(c["maxwait"], v)
            return v
        return None

    def _emit_waits(self, eng, waits):
        out = []
        for s, v in waits.items():
            if s in self.ch:
                v2 = self._wait_tok_for(s)
                v = max(v, 0) if v2 is None else v2
            if self.seen[eng].get(s, -1) >= v:
                continue
            self.seen[eng][s] = v
            out.append((self.sem[s], v))
        return out

    def _track(self, reads, writes, tok):
        s, v = tok
        for k in writes:
            self.bufs[k] = {"w": (s, v), "r": {}}
        for k in reads:
            b = self.bufs.setdefault(k, {"w": None, "r": {}})
            if b["r"].get(s, -1) < v:
                b["r"][s] = v

    def op(self, eng, fn, reads=(), writes=()):
        waits = self._collect(reads, writes)
        wl = self._emit_waits(eng, waits)
        self.cnt[eng] += 1
        n = self.cnt[eng]
        sem = self.sem[eng]

        def run(e, wl=wl, fn=fn, sem=sem, n=n):
            for s, v in wl:
                e.wait_ge(s, v)
            inst = fn(e)
            inst.then_inc(sem, 1)
            return inst

        self.q[eng].append(run)
        self._track(reads, writes, (eng, n))

    def dma(self, chname, fn, reads=(), writes=()):
        c = self.ch[chname]
        eng = c["eng"]
        if c["maxwait"] > 16 * c["fenced"]:
            tgt = 16 * c["issued"]
            c["fenced"] = c["issued"]
            sem = self.sem[chname]

            def fence(e, sem=sem, tgt=tgt):
                e.wait_ge(sem, tgt)
            self.q[eng].append(lambda e, f=fence: f(e) or _NOINST)
            self.seen[eng][chname] = tgt
        waits = self._collect(reads, writes)
        wl = self._emit_waits(eng, waits)
        c["issued"] += 1
        v = 16 * c["issued"]
        sem = self.sem[chname]

        def run(e, wl=wl, fn=fn, sem=sem):
            for s, vv in wl:
                e.wait_ge(s, vv)
            inst = fn(e)
            inst.then_inc(sem, 16)
            return inst

        self.q[eng].append(run)
        self._track(reads, writes, (chname, v))

    def claim(self, newkey, oldkeys):
        b = self.bufs.setdefault(newkey, {"w": None, "r": {}})
        for k in oldkeys:
            if k == newkey:
                continue
            ob = self.bufs.get(k)
            if not ob:
                continue
            for tok in ([ob["w"]] if ob["w"] else []) + list(ob["r"].items()):
                s_, v = tok
                if b["r"].get(s_, -1) < v:
                    b["r"][s_] = v

    def finish(self):
        for name, c in self.ch.items():
            if c["issued"] > 0:
                tgt = 16 * c["issued"]
                sem = self.sem[name]
                self.q["sp"].append(lambda e, s=sem, t=tgt: e.wait_ge(s, t) or _NOINST)
        blk = self.stk.enter_context(self.nc.Block())
        for ename, attr in ENG2ATTR.items():
            ops = self.q[ename]
            getattr(blk, attr)(lambda eng, ops=ops: [f(eng) for f in ops] and None)


class _NoInst:
    def then_inc(self, *a):
        return self


_NOINST = _NoInst()


def _ap_custom(base_ap, dims, offset):
    ap = base_ap.copy()
    ap.ap = bass_rust.VecI64Pair(dims)
    ap.offset = base_ap.offset + offset
    return ap


def build(taps=(), dbg=False, phases=('attn_a', 'ffn0', 'attn_b', 'ffn1'), b_hp=8, b_stage=99):
    nc = bass.Bass("TRN2", target_bir_lowering=False)
    stack = contextlib.ExitStack()
    stack.__enter__()
    bld = Bld(nc, stack)
    sb = lambda name, shape, dt: stack.enter_context(nc.sbuf_tensor(name, shape, dt))
    ps = lambda name, shape, dt: stack.enter_context(nc.psum_tensor(name, shape, dt))

    di = {}
    ins = [
        ("x", [T, C]), ("waT", [C, 3 * C]), ("b_attn", [3 * C]),
        ("wpaT", [C, C]), ("b_proj", [C]), ("wassT", [C, NCL]),
        ("wqkv", [C, 3 * C]), ("wpb", [C, C]),
        ("rwT", [C, NE]), ("rb", [NE]), ("weT", [NE, C, C]), ("eb", [NE, C]),
    ]
    bf_ins = [("ROTQ", [128, 512]), ("ROTKL", [128, 2048]), ("TKC", [64, 1024]),
              ("TRI", [128, 128]), ("PERM", [128, 128]), ("IDB", [128, 128]),
              ("C7", [64, 128]), ("C15", [64, 128]), ("C13", [64, 128])]
    for n, shape in ins:
        di[n] = nc.dram_tensor(n, shape, F32, kind="ExternalInput").ap()
    for n, shape in bf_ins:
        di[n] = nc.dram_tensor(n, shape, BF16, kind="ExternalInput").ap()
    out_d = nc.dram_tensor("out", [T, C], F32, kind="ExternalOutput").ap()
    tap_d = {}
    for tname in taps:
        tap_d[tname] = nc.dram_tensor("tap_" + tname, [T, C], F32, kind="ExternalOutput").ap()
    dbg_d = {}
    if dbg:
        for n, shape in [("logit", [128, 128]), ("sel", [128, 128]), ("lmax", [128, 16]),
                         ("w16d", [128, 16]), ("xnTd", [128, 16384]), ("y0", [128, 1024])]:
            dbg_d[n] = nc.dram_tensor("dbg_" + n, shape, F32, kind="ExternalOutput").ap()
    lt_dram = nc.dram_tensor("lt_dram", [16, 512], F32, kind="Internal").ap()
    wbig_dram = nc.dram_tensor("wbig_dram", [T, NCL], BF16, kind="Internal").ap()

    r_s = sb("r_s", [128, 16 * 1024], F32)
    rotq = sb("rotq", [128, 512], BF16)
    rotkl = sb("rotkl", [128, 2048], BF16)
    tkc = sb("tkc_s", [64, 1024], BF16)
    tri = sb("tri_s", [128, 128], BF16)
    perm = sb("perm_s", [128, 128], BF16)
    idb = sb("idb_s", [128, 128], BF16)
    c7 = sb("c7_s", [64, 128], BF16)
    c15 = sb("c15_s", [64, 128], BF16)
    c13 = sb("c13_s", [64, 128], BF16)
    onesb = sb("onesb", [1, 128], BF16)
    ones1f = sb("ones1f", [1, 128], F32)
    onesc = sb("onesc", [128, 1], BF16)
    eps_t = sb("eps_t", [128, 1], F32)
    qsc_t = sb("qsc_t", [128, 1], F32)
    sum16 = sb("sum16", [128, 16], F32)
    ssq16 = sb("ssq16", [128, 16], F32)
    mean16 = sb("mean16", [128, 16], F32)
    var16 = sb("var16", [128, 16], F32)
    rstd16 = sb("rstd16", [128, 16], F32)
    tmp16 = sb("tmp16", [128, 16], F32)
    xn_tok = sb("xn_tok", [128, 1024], BF16)
    battn = sb("battn", [128, 24], F32)
    bprow = sb("bprow", [1, 1024], BF16)
    ebrow = sb("ebrow", [1, 2048], BF16)
    rbrow = sb("rbrow", [1, 8], BF16)
    wass = sb("wass", [128, 8 * 64], BF16)
    rwt = sb("rwt", [128, 64], BF16)
    logit_s = sb("logit_s", [128, 128], F32)
    er_s = sb("er_s", [128, 128], F32)
    lmax16 = sb("lmax16", [128, 16], F32)
    esum16 = sb("esum16", [128, 16], F32)
    elmax16 = sb("elmax16", [128, 16], F32)
    w16 = sb("w16", [128, 16], F32)
    sel_s = sb("sel_s", [128, 128], F32)
    eblk = sb("eblk", [16, 32], F32)
    wsum_c = sb("wsum_c", [16, 1], F32)
    wrec_c = sb("wrec_c", [16, 1], F32)
    wblk_bf = sb("wblk_bf", [16, 32], BF16)
    wbig_s = sb("wbig_s", [128, 4 * 64], BF16)
    kcr_cm = sb("kcr_cm", [64, 1024], BF16)
    cv_bf = sb("cv_bf", [64, 1024], BF16)
    kcrT = sb("kcrT", [128, 8 * 56], BF16)
    p_sm_b = [sb("p_sm0", [128, 512], BF16), sb("p_sm1", [128, 512], BF16)]
    pt_sm_b = [sb("pt_sm0", [128, 384], BF16), sb("pt_sm1", [128, 384], BF16)]
    ssum_b = [sb("ssum0", [128, 1], F32), sb("ssum1", [128, 1], F32)]
    srec_b = [sb("srec0", [128, 1], F32), sb("srec1", [128, 1], F32)]
    zero64 = sb("zero64", [128, 64], BF16)
    qlmT = sb("qlmT", [128, 64], BF16)
    klmT = sb("klmT", [128, 64], BF16)
    lm_f = sb("lm_f", [128, 64], F32)
    rsum32 = sb("rsum32", [128, 32], F32)
    rrec32 = sb("rrec32", [128, 32], F32)
    s3sum = sb("s3sum", [128, 2], F32)
    s3tot = sb("s3tot", [128, 1], F32)
    s3rec = sb("s3rec", [128, 1], F32)
    k2sum = sb("k2sum", [64, 2], F32)
    k2rec = sb("k2rec", [64, 2], F32)
    denom_s = sb("denom_s", [1, 128], F32)
    dmax_s = sb("dmax_s", [1, 2], F32)
    drec_s = sb("drec_s", [1, 2], F32)
    dbc_s = sb("dbc_s", [128, 2], F32)
    K2c = sb("K2c", [64, 128], BF16)
    K2Tc = sb("K2Tc", [64, 128], BF16)
    Ac = sb("Ac", [64, 128], BF16)
    Vc = sb("Vc", [64, 128], BF16)
    Vtc = sb("Vtc", [64, 128], BF16)
    U1c = sb("U1c", [64, 128], BF16)
    U2c = sb("U2c", [64, 128], BF16)
    U3c = sb("U3c", [64, 128], BF16)
    U4c = sb("U4c", [64, 128], BF16)
    U4Tc = sb("U4Tc", [64, 128], BF16)
    U5c = sb("U5c", [64, 128], BF16)
    k3vc = sb("k3vc", [64, 128], BF16)
    Zc = sb("Zc", [64, 128], BF16)
    c7b2, c15b2, c13b2 = c7, c15, c13
    ZB = sb("ZB", [128, 50176], BF16)
    ZF = sb("ZF", [128, 2560], F32)
    ck_s = ZF[0:64, 0:1024]
    cv_s = ZF[0:64, 1024:2048]
    lt_s = ZF[0:16, 2048:2560]

    p01 = ps("p01", [128, 1024], F32)
    p23 = ps("p23", [128, 1024], F32)
    p4 = ps("p4", [128, 512], F32)
    p5 = ps("p5", [128, 512], F32)
    p6 = ps("p6", [128, 512], F32)
    p7 = ps("p7", [128, 512], F32)
    p6b = p6.bitcast(BF16)
    p7b = p7.bitcast(BF16)
    p01b = p01.bitcast(BF16)
    PK = {id(p01): "p01", id(p23): "p23", id(p4): "p4", id(p5): "p5",
          id(p6): "p6", id(p7): "p7"}

    assert nc.sbuf_bytes_remaining >= 0, nc.sbuf_bytes_remaining

    def zb(off_b, cols):
        c0 = off_b // 2
        return ZB[:, c0:c0 + cols]

    def zf(off_b, cols):
        c0 = off_b // 4
        return ZF[:, c0:c0 + cols]

    def memset(tile, val, key):
        bld.op("dve", lambda e, t=tile, v=val: e.memset(t, v), writes=(key,))

    def ln_pass(chunks, dst_fn, dst_keys, src_key_fn):
        for i in chunks:
            bld.op("dve", lambda e, i=i: e.reduce_sum(
                sum16[:, i:i + 1], r_s[:, 1024 * i:1024 * (i + 1)], axis=AX.X),
                reads=(src_key_fn(i),), writes=("sum16",))
            bld.op("act", lambda e, i=i: e.activation(
                xn_tok[:], r_s[:, 1024 * i:1024 * (i + 1)], AF.Square,
                accum_out=ssq16[:, i:i + 1]),
                reads=(src_key_fn(i),), writes=("xn_tok", "ssq16"))
        i0, i1 = chunks[0], chunks[-1] + 1
        sl = slice(i0, i1)
        bld.op("dve", lambda e: e.tensor_scalar(mean16[:, sl], sum16[:, sl], 1.0 / C, None, op0=ALU.mult),
               reads=("sum16",), writes=("mean16",))
        bld.op("dve", lambda e: e.tensor_tensor(tmp16[:, sl], mean16[:, sl], mean16[:, sl], op=ALU.mult),
               reads=("mean16",), writes=("tmp16",))
        bld.op("dve", lambda e: e.tensor_scalar(var16[:, sl], ssq16[:, sl], 1.0 / C, None, op0=ALU.mult),
               reads=("ssq16",), writes=("var16",))
        bld.op("dve", lambda e: e.tensor_tensor(var16[:, sl], var16[:, sl], tmp16[:, sl], op=ALU.subtract),
               reads=("tmp16", "var16"), writes=("var16",))
        bld.op("act", lambda e: e.activation(rstd16[:, sl], var16[:, sl], AF.Sqrt, bias=eps_t[:]),
               reads=("var16", "eps"), writes=("rstd16",))
        bld.op("dve", lambda e: e.reciprocal(rstd16[:, sl], rstd16[:, sl]),
               reads=("rstd16",), writes=("rstd16",))
        for i in chunks:
            bld.op("dve", lambda e, i=i: e.tensor_scalar(
                xn_tok[:], r_s[:, 1024 * i:1024 * (i + 1)],
                mean16[:, i:i + 1], rstd16[:, i:i + 1], op0=ALU.subtract, op1=ALU.mult),
                reads=(src_key_fn(i), "mean16", "rstd16"), writes=("xn_tok",))

            def trs(e, i=i):
                last = None
                for b in range(8):
                    last = e.transpose(p6b[:, 128 * b:128 * (b + 1)],
                                       xn_tok[:, 128 * b:128 * (b + 1)], idb[:])
                return last
            bld.op("pe", trs, reads=("xn_tok", "idb"), writes=("p6",))
            dst = dst_fn(i)
            bld.op("act", lambda e, dst=dst: e.copy(
                dst, p6b[:, 0:1024].rearrange("p (c t) -> p c t", t=128)),
                reads=("p6",), writes=dst_keys)

    def mmg(fns, reads, writes):
        def run(e, fns=fns):
            last = None
            for f in fns:
                last = f(e)
            return last
        bld.op("pe", run, reads=reads, writes=writes)

    def loads():
        bld.dma("Ld", lambda e: e.dma_start(
            out=r_s[:], in_=_ap_custom(di["x"], [[1024, 128], [131072, 16], [1, 1024]], 0)),
            writes=tuple(f"r{i}" for i in range(16)))
        for n, t, key in [("ROTQ", rotq, "rotq"), ("ROTKL", rotkl, "rotkl"),
                          ("TKC", tkc, "tkc"), ("TRI", tri, "tri")]:
            bld.dma("Ld", lambda e, n=n, t=t: e.dma_start(out=t[:], in_=di[n]), writes=(key,))
        for n, t, key in [("PERM", perm, "perm"), ("IDB", idb, "idb"),
                          ("C7", c7, "c7"), ("C15", c15, "c15"), ("C13", c13, "c13")]:
            bld.dma("Ld", lambda e, n=n, t=t: e.dma_start(out=t[:], in_=di[n]), writes=(key,))
        bld.dma("Wg", lambda e: e.dma_start(
            out=battn[:], in_=_ap_custom(di["b_attn"], [[1, 128], [128, 24]], 0),
            allow_slow_non_contiguous=True), writes=("battn",))
        bld.dma("Wg", lambda e: e.dma_start(out=bprow[:], in_=di["b_proj"].rearrange("(o d) -> o d", o=1)),
                writes=("bprow",))
        bld.dma("Wg", lambda e: e.dma_start(out=rbrow[:], in_=di["rb"].rearrange("(o d) -> o d", o=1)),
                writes=("rbrow",))
        bld.dma("Wg", lambda e: e.dma_start(
            out=wass[:], in_=_ap_custom(di["wassT"], [[64, 128], [8192, 8], [1, 64]], 0)), writes=("wass",))
        bld.dma("Wg", lambda e: e.dma_start(
            out=rwt[:], in_=_ap_custom(di["rwT"], [[8, 128], [1024, 8], [1, 8]], 0)), writes=("rwt",))
        memset(onesb[:], 1.0, "onesb")
        memset(ones1f[:], 1.0, "ones1f")
        memset(onesc[:], 1.0, "onesc")
        memset(eps_t[:], 1e-5, "eps")
        memset(qsc_t[:], 0.125, "qsc")
        memset(zero64[:], 0.0, "zero64")
        memset(ck_s[:], 0.0, "ck_s")
        memset(cv_s[:], 0.0, "cv_s")
        for i in range(16):
            bld.dma("Sc", lambda e, i=i: e.dma_start(
                out=wbig_dram[128 * i:128 * (i + 1), :], in_=zero64[:]),
                reads=("zero64",), writes=("wbig",))

    ARENA_A = ["waT0", "waT1", "xnTq", "qrT", "klrT", "ktok", "vtok",
               "kslot", "vslot_a", "yT", "wpT"]

    def attn_a():
        A_waT = 0
        A_xnTq = 49152
        A_qrT = 57344
        A_klrT = 65536
        A_ktok = 73728
        A_vtok = 81920
        A_kslot = 90112
        A_vslot = 91136
        A_yT = A_xnTq
        A_wpT = A_klrT
        waT_s = zb(A_waT, 8 * 3072)
        for half in range(2):
            bld.dma("Wg", lambda e, h=half: e.dma_start(
                out=zb(A_waT, 8 * 3072).rearrange("p (c f) -> p c f", f=3072)[:, :, 1536 * h:1536 * (h + 1)],
                in_=_ap_custom(di["waT"], [[3072, 128], [393216, 8], [1, 1536]], 1536 * h)),
                writes=("waT%d" % half,))

        for Q in range(NQ):
            xnTq = zb(A_xnTq, 4096)
            qrT = zb(A_qrT, 4096)
            klrT = zb(A_klrT, 4096)
            ktok = zb(A_ktok, 4096)
            vtok = zb(A_vtok, 4096)
            kslot = zb(A_kslot, 512)
            vslot = zb(A_vslot, 512)
            yT = zb(A_yT, 4096)

            bld.claim("xnTq", ["yT"])
            bld.claim("klrT", ["wpT"])
            bld.claim("ktok", ["wpT"])

            def xnq_dst(i, Q=Q):
                ii = i - 4 * Q
                return zb(A_xnTq, 4096).rearrange("p (c t) -> p c t", c=8)[:, :, 128 * ii:128 * (ii + 1)]
            ln_pass(list(range(4 * Q, 4 * Q + 4)), xnq_dst, ("xnTq",),
                    lambda i: "r%d" % i)

            for F in range(24):
                kind, co = F // 8, F % 8
                pb, pk = (p4, "p4") if F % 2 == 0 else (p5, "p5")

                def qf(e, F=F, pb=pb):
                    last = None
                    for c in range(8):
                        last = e.matmul(pb[:, 0:512],
                                        waT_s[:, 3072 * c + 128 * F:3072 * c + 128 * (F + 1)],
                                        xnTq[:, 512 * c:512 * (c + 1)],
                                        start=(c == 0), stop=(c == 7))
                    return last
                mmg([qf], reads=("waT0", "waT1", "xnTq"), writes=(pk,))
                if kind == 0:
                    bld.op("dve", lambda e, co=co, F=F, pb=pb: e.scalar_tensor_tensor(
                        qrT[:, 512 * co:512 * (co + 1)], pb[:, 0:512],
                        battn[:, F:F + 1], rotq[:], op0=ALU.add, op1=ALU.mult),
                        reads=(pk, "battn", "rotq"), writes=("qrT",))
                else:
                    slot, slotk = (kslot, "kslot") if kind == 1 else (vslot, "vslot_a")
                    bld.op("act", lambda e, F=F, pb=pb, slot=slot: e.activation(
                        slot[:], pb[:, 0:512], AF.Identity, bias=battn[:, F:F + 1]),
                        reads=(pk, "battn"), writes=(slotk,))
                    if kind == 1:
                        ppb, ppk = (p5, "p5") if F % 2 == 0 else (p4, "p4")
                        mmg([lambda e, ppb=ppb: e.matmul(
                            ppb[:, 0:512], perm[:], kslot[:], start=True, stop=True)],
                            reads=("kslot", "perm"), writes=(ppk,))
                        bld.op("dve", lambda e, co=co, ppb=ppb, Q=Q: e.tensor_tensor(
                            klrT[:, 512 * co:512 * (co + 1)], ppb[:, 0:512],
                            rotkl[:, 512 * Q:512 * (Q + 1)], op=ALU.mult),
                            reads=(ppk, "rotkl"), writes=("klrT",))
                    tokd, tokk = (ktok, "ktok") if kind == 1 else (vtok, "vtok")

                    def ttr(e, slot=slot):
                        last = None
                        for jj in range(4):
                            last = e.transpose(p6b[:, 128 * jj:128 * (jj + 1)],
                                               slot[:, 128 * jj:128 * (jj + 1)], idb[:])
                        return last
                    bld.op("pe", ttr, reads=(slotk, "idb"), writes=("p6",))
                    dst = tokd.rearrange("p (j d) -> p j d", j=4)[:, :, 128 * co:128 * (co + 1)]
                    bld.op("act", lambda e, dst=dst: e.copy(
                        dst, p6b[:, 0:512].rearrange("p (j d) -> p j d", j=4)),
                        reads=("p6",), writes=(tokk,))

            mmg([lambda e, Q=Q: [e.matmul(p7[0:16, 0:512],
                                          wass[:, 64 * c + 16 * Q:64 * c + 16 * Q + 16],
                                          xnTq[:, 512 * c:512 * (c + 1)],
                                          start=(c == 0), stop=(c == 7)) for c in range(8)][-1]],
                reads=("wass", "xnTq"), writes=("p7",))
            bld.op("dve", lambda e: e.tensor_copy(lt_s[:], p7[0:16, 0:512]),
                   reads=("p7",), writes=("lt_s",))
            bld.dma("Sc", lambda e: e.dma_start(out=lt_dram, in_=lt_s[:]),
                    reads=("lt_s",), writes=("lt_dram",))
            diag = _ap_custom(lt_dram, [[544, 16], [1, 32]], 0)
            bld.dma("Sc", lambda e, diag=diag: e.dma_start(out=eblk[:], in_=diag),
                    reads=("lt_dram",), writes=("eblk",))
            bld.op("act", lambda e: e.activation(eblk[:], eblk[:], AF.Exp, accum_out=wsum_c[:]),
                   reads=("eblk",), writes=("eblk", "wsum_c"))
            bld.op("dve", lambda e: e.reciprocal(wrec_c[:], wsum_c[:]),
                   reads=("wsum_c",), writes=("wrec_c",))
            bld.op("dve", lambda e: e.tensor_scalar(wblk_bf[:], eblk[:], wrec_c[:], None, op0=ALU.mult),
                   reads=("eblk", "wrec_c"), writes=("wblk_bf",))
            scat = _ap_custom(wbig_dram, [[2049, 16], [64, 32]], 32768 * Q + 16 * Q)
            bld.dma("Sc", lambda e, scat=scat: e.dma_start(
                out=scat, in_=wblk_bf[:], allow_slow_non_contiguous=True),
                reads=("wblk_bf",), writes=("wbig",))
            bld.dma("Sc", lambda e, Q=Q: e.dma_start(
                out=wbig_s[:], in_=_ap_custom(wbig_dram, [[64, 128], [8192, 4], [1, 64]], 32768 * Q)),
                reads=("wbig",), writes=("wbig_s",))

            for (tok, tk, pb, pk, dst, dstk) in [(ktok, "ktok", p01, "p01", ck_s, "ck_s"),
                                                 (vtok, "vtok", p23, "p23", cv_s, "cv_s")]:
                def sf(e, tok=tok, pb=pb):
                    last = None
                    for j in range(4):
                        for nb in range(2):
                            last = e.matmul(pb[0:64, 512 * nb:512 * (nb + 1)],
                                            wbig_s[:, 64 * j:64 * (j + 1)],
                                            tok[:, 1024 * j + 512 * nb:1024 * j + 512 * (nb + 1)],
                                            start=(j == 0), stop=(j == 3))
                    return last
                mmg([sf], reads=("wbig_s", tk), writes=(pk,))
                bld.op("dve", lambda e, pb=pb, dst=dst: e.tensor_tensor(
                    dst[:], pb[0:64, 0:1024], dst[:], op=ALU.add),
                    reads=(pk, dstk), writes=(dstk,))

            evn = lambda ap: ap.rearrange("p (a two) -> p two a", two=2)[:, 0:1, :]
            odd = lambda ap: ap.rearrange("p (a two) -> p two a", two=2)[:, 1:2, :]
            bld.op("dve", lambda e: e.tensor_tensor(evn(kcr_cm[:]), odd(ck_s[:]), evn(tkc[:]), op=ALU.mult),
                   reads=("ck_s", "tkc"), writes=("kcr_cm",))
            bld.op("dve", lambda e: e.tensor_tensor(odd(kcr_cm[:]), evn(ck_s[:]), odd(tkc[:]), op=ALU.mult),
                   reads=("ck_s", "tkc"), writes=("kcr_cm",))
            bld.op("dve", lambda e: e.tensor_copy(cv_bf[:], cv_s[:]),
                   reads=("cv_s",), writes=("cv_bf",))

            def ktrs(e):
                last = None
                for h in range(16):
                    hb = 64 * (h % 2)
                    last = e.transpose(
                        p6b[hb:hb + 64, 56 * (h // 2) + 448 * (h % 2):56 * (h // 2) + 448 * (h % 2) + 56],
                        kcr_cm[0:56, 64 * h:64 * (h + 1)],
                        idb[0:56, 0:56], tile_position=(0, hb))
                return last
            bld.op("pe", ktrs, reads=("kcr_cm", "idb"), writes=("p6",))
            bld.op("act", lambda e: e.copy(kcrT[0:64, 0:448], p6b[0:64, 0:448]),
                   reads=("p6",), writes=("kcrT",))
            bld.op("act", lambda e: e.copy(kcrT[64:128, 0:448], p6b[64:128, 448:896]),
                   reads=("p6",), writes=("kcrT",))

            bld.claim("yT", ["xnTq"])
            for gl in range(2):
                gg = 2 * Q + gl
                ncl_g = NCPG * gg
                for hp in range(8):
                    yc0 = 256 * (hp % 2)
                    for hh in range(2):
                        h = 2 * hp + hh
                        hb = 64 * hh
                        for qc in range(2):
                            it = 2 * hh + qc
                            pp = it % 2
                            qend = 128 * (qc + 1)
                            ncols = ncl_g + qend
                            spb, spk = (p4, "p4") if pp == 0 else (p5, "p5")
                            p_sm = p_sm_b[pp]
                            pt_sm = pt_sm_b[pp]
                            ssum1 = ssum_b[pp]
                            srec1 = srec_b[pp]
                            psk, ptk = "p_sm%d" % pp, "pt_sm%d" % pp
                            ssk, srk = "ssum%d" % pp, "srec%d" % pp
                            ptp, ptpk = (p6b, "p6") if pp == 0 else (p01b, "p01")

                            def sc_f(e, hp=hp, hb=hb, gl=gl, qc=qc, ncl_g=ncl_g, qend=qend, spb=spb):
                                last = None
                                if ncl_g > 0:
                                    last = e.matmul(
                                        spb[:, 0:ncl_g],
                                        qrT[hb:hb + 64, 512 * hp + 256 * gl + 128 * qc:512 * hp + 256 * gl + 128 * (qc + 1)],
                                        kcrT[hb:hb + 64, 56 * hp:56 * hp + ncl_g],
                                        start=True, stop=True, tile_position=(hb, 0))
                                last = e.matmul(
                                    spb[:, ncl_g:ncl_g + qend],
                                    qrT[hb:hb + 64, 512 * hp + 256 * gl + 128 * qc:512 * hp + 256 * gl + 128 * (qc + 1)],
                                    klrT[hb:hb + 64, 512 * hp + 256 * gl:512 * hp + 256 * gl + qend],
                                    start=True, stop=True, tile_position=(hb, 0))
                                return last
                            mmg([sc_f], reads=("qrT", "kcrT", "klrT"), writes=(spk,))
                            bld.op("dve", lambda e, spb=spb, ncl_g=ncl_g, qc=qc: e.tensor_tensor(
                                spb[:, ncl_g + 128 * qc:ncl_g + 128 * (qc + 1)],
                                spb[:, ncl_g + 128 * qc:ncl_g + 128 * (qc + 1)],
                                tri[:], op=ALU.add),
                                reads=(spk, "tri"), writes=(spk,))
                            bld.op("act", lambda e, spb=spb, ncols=ncols, p_sm=p_sm, ssum1=ssum1: e.activation(
                                p_sm[:, 0:ncols], spb[:, 0:ncols], AF.Exp, accum_out=ssum1[:]),
                                reads=(spk,), writes=(psk, ssk))
                            bld.op("dve", lambda e, srec1=srec1, ssum1=ssum1: e.reciprocal(srec1[:], ssum1[:]),
                                   reads=(ssk,), writes=(srk,))
                            bld.op("dve", lambda e, ncols=ncols, p_sm=p_sm, srec1=srec1: e.tensor_scalar(
                                p_sm[:, 0:ncols], p_sm[:, 0:ncols], srec1[:], None, op0=ALU.mult),
                                reads=(psk, srk), writes=(psk,))

                            def pt_f(e, ncl_g=ncl_g, qc=qc, p_sm=p_sm, ptp=ptp):
                                last = None
                                if ncl_g > 0:
                                    last = e.transpose(ptp[0:ncl_g, 0:128], p_sm[:, 0:ncl_g],
                                                       idb[:])
                                for kc in range(qc + 1):
                                    last = e.transpose(ptp[:, 128 * (kc + 1):128 * (kc + 2)],
                                                       p_sm[:, ncl_g + 128 * kc:ncl_g + 128 * (kc + 1)],
                                                       idb[:])
                                return last
                            bld.op("pe", pt_f, reads=(psk, "idb"), writes=(ptpk,))
                            if ncl_g > 0:
                                bld.op("act", lambda e, ncl_g=ncl_g, pt_sm=pt_sm, ptp=ptp: e.copy(
                                    pt_sm[0:ncl_g, 0:128], ptp[0:ncl_g, 0:128]),
                                    reads=(ptpk,), writes=(ptk,))
                            bld.op("act", lambda e, qc=qc, pt_sm=pt_sm, ptp=ptp: e.copy(
                                pt_sm[:, 128:128 * (qc + 2)], ptp[:, 128:128 * (qc + 2)]),
                                reads=(ptpk,), writes=(ptk,))

                            def pv_f(e, h=h, hb=hb, gl=gl, qc=qc, ncl_g=ncl_g, yc0=yc0, pt_sm=pt_sm):
                                last = None
                                first = True
                                yc = yc0 + 128 * qc
                                if ncl_g > 0:
                                    last = e.matmul(p7[hb:hb + 64, yc:yc + 128],
                                                    cv_bf[0:ncl_g, 64 * h:64 * (h + 1)],
                                                    pt_sm[0:ncl_g, 0:128],
                                                    start=True, stop=False, tile_position=(0, hb))
                                    first = False
                                for kc in range(qc + 1):
                                    j = 2 * gl + kc
                                    last = e.matmul(p7[hb:hb + 64, yc:yc + 128],
                                                    vtok[:, 1024 * j + 64 * h:1024 * j + 64 * (h + 1)],
                                                    pt_sm[:, 128 * (kc + 1):128 * (kc + 2)],
                                                    start=first, stop=(kc == qc),
                                                    tile_position=(0, hb))
                                    first = False
                                return last
                            mmg([pv_f], reads=(ptk, "cv_bf", "vtok"), writes=("p7",))
                    bld.op("dve", lambda e, hp=hp, gl=gl, yc0=yc0: e.tensor_copy(
                        yT[:, 512 * hp + 256 * gl:512 * hp + 256 * (gl + 1)], p7[:, yc0:yc0 + 256]),
                        reads=("p7",), writes=("yT",))

            bld.claim("wpT", ["klrT", "ktok"])
            wpT_s = zb(A_wpT, 8192)
            bld.dma("Wg", lambda e: e.dma_start(
                out=zb(A_wpT, 8192), in_=_ap_custom(di["wpaT"], [[1024, 128], [131072, 8], [1, 1024]], 0)),
                writes=("wpT",))
            for qb in range(4):
                i = 4 * Q + qb

                def opf(e, qb=qb, wpT_s=wpT_s):
                    last = None
                    for nb in range(2):
                        for c in range(8):
                            last = e.matmul(p01[:, 512 * nb:512 * (nb + 1)],
                                            yT[:, 512 * c + 128 * qb:512 * c + 128 * (qb + 1)],
                                            wpT_s[:, 1024 * c + 512 * nb:1024 * c + 512 * nb + 512],
                                            start=(c == 0), stop=False)
                        last = e.matmul(p01[:, 512 * nb:512 * (nb + 1)], onesb[:],
                                        bprow[0:1, 512 * nb:512 * (nb + 1)],
                                        start=False, stop=True)
                    return last
                mmg([opf], reads=("yT", "wpT", "onesb", "bprow"), writes=("p01",))
                bld.op("dve", lambda e, i=i: e.tensor_tensor(
                    r_s[:, 1024 * i:1024 * (i + 1)], p01[:, 0:1024],
                    r_s[:, 1024 * i:1024 * (i + 1)], op=ALU.add),
                    reads=("p01", "r%d" % i), writes=("r%d" % i,))

    ARENA_F = ["weT0", "weT1", "xnT"]
    ARENA_B = ["svT", "wq0", "wq1", "qTb", "kTb", "P1", "P3", "vtokb",
               "vslot_b", "ptslot", "wpb", "xnT"]

    def ffn(p):
        A_weT = [0, 16384]
        A_xnT = 32768
        prev = ARENA_A if p == 0 else ARENA_B
        for k in ARENA_F:
            bld.claim(k, prev)
        xnT = zb(A_xnT, 16384)

        def xn_dst(i):
            return zb(A_xnT, 16384).rearrange("p (c t) -> p c t", c=8)[:, :, 128 * i:128 * (i + 1)]
        ln_pass(list(range(16)), xn_dst, ("xnT",), lambda i: "r%d" % i)

        def rf(e):
            last = None
            for i in range(16):
                for c in range(8):
                    last = e.matmul(p4[:, 8 * i:8 * (i + 1)],
                                    xnT[:, 2048 * c + 128 * i:2048 * c + 128 * (i + 1)],
                                    rwt[:, 8 * c:8 * (c + 1)], start=(c == 0), stop=False)
                last = e.matmul(p4[:, 8 * i:8 * (i + 1)], onesb[:], rbrow[:],
                                start=False, stop=True)
            return last
        mmg([rf], reads=("xnT", "rwt", "onesb", "rbrow"), writes=("p4",))
        bld.op("dve", lambda e: e.tensor_copy(logit_s[:], p4[:, 0:128]),
               reads=("p4",), writes=("logit_s",))
        bld.op("dve", lambda e: e.tensor_reduce(
            lmax16[:].rearrange("p (a b) -> p a b", b=1),
            logit_s[:].rearrange("p (a b) -> p a b", b=8),
            axis=AX.X, op=ALU.max, opt_input=False, opt_output=False),
            reads=("logit_s",), writes=("lmax16",))
        bld.op("act", lambda e: e.activation(er_s[:], logit_s[:], AF.Exp),
               reads=("logit_s",), writes=("er_s",))
        bld.op("act", lambda e: e.activation(elmax16[:], lmax16[:], AF.Exp),
               reads=("lmax16",), writes=("elmax16",))
        bld.op("dve", lambda e: e.tensor_reduce(
            esum16[:].rearrange("p (a b) -> p a b", b=1),
            er_s[:].rearrange("p (a b) -> p a b", b=8),
            axis=AX.X, op=ALU.add, opt_input=False, opt_output=False),
            reads=("er_s",), writes=("esum16",))
        bld.op("dve", lambda e: e.reciprocal(esum16[:], esum16[:]),
               reads=("esum16",), writes=("esum16",))
        bld.op("dve", lambda e: e.tensor_tensor(w16[:], elmax16[:], esum16[:], op=ALU.mult),
               reads=("elmax16", "esum16"), writes=("w16",))
        for i in range(16):
            bld.op("dve", lambda e, i=i: e.tensor_scalar(
                sel_s[:, 8 * i:8 * (i + 1)], logit_s[:, 8 * i:8 * (i + 1)],
                lmax16[:, i:i + 1], w16[:, i:i + 1], op0=ALU.is_equal, op1=ALU.mult),
                reads=("logit_s", "lmax16", "w16"), writes=("sel_s",))

        if dbg and p == 0:
            bld.dma("St", lambda e: e.dma_start(out=dbg_d["logit"], in_=logit_s[:]), reads=("logit_s",))
            bld.dma("St", lambda e: e.dma_start(out=dbg_d["sel"], in_=sel_s[:]), reads=("sel_s",))
            bld.dma("St", lambda e: e.dma_start(out=dbg_d["lmax"], in_=lmax16[:]), reads=("lmax16",))
            bld.dma("St", lambda e: e.dma_start(out=dbg_d["w16d"], in_=w16[:]), reads=("w16",))
            dxn = nc.dram_tensor("dbg_xnb", [128, 16384], BF16, kind="ExternalOutput").ap()
            dbg_d["xnTb"] = dxn
            bld.dma("St", lambda e: e.dma_start(out=dxn, in_=xnT), reads=("xnT",))
        for e_i in range(NE):
            buf = e_i % 2
            weT = zb(A_weT[buf], 8192)
            bld.dma("Wg", lambda e, e_i=e_i, buf=buf: e.dma_start(
                out=zb(A_weT[buf], 8192),
                in_=_ap_custom(di["weT"][e_i], [[1024, 128], [131072, 8], [1, 1024]], 0)),
                writes=("weT%d" % buf,))
            bld.dma("Wg", lambda e, e_i=e_i, buf=buf: e.dma_start(
                out=ebrow[0:1, 1024 * buf:1024 * (buf + 1)], in_=di["eb"][e_i].rearrange("(o d) -> o d", o=1)),
                writes=("ebrow%d" % buf,))
            for i in range(16):
                pb, pk = (p01, "p01") if i % 2 == 0 else (p23, "p23")

                def ef(e, i=i, pb=pb, weT=weT, buf=buf):
                    last = None
                    for nb in range(2):
                        for c in range(8):
                            last = e.matmul(pb[:, 512 * nb:512 * (nb + 1)],
                                            xnT[:, 2048 * c + 128 * i:2048 * c + 128 * (i + 1)],
                                            weT[:, 1024 * c + 512 * nb:1024 * c + 512 * nb + 512],
                                            start=(c == 0), stop=False)
                        last = e.matmul(pb[:, 512 * nb:512 * (nb + 1)], onesb[:],
                                        ebrow[0:1, 1024 * buf + 512 * nb:1024 * buf + 512 * (nb + 1)],
                                        start=False, stop=True)
                    return last
                mmg([ef], reads=("xnT", "weT%d" % buf, "onesb", "ebrow%d" % buf), writes=(pk,))
                bld.op("dve", lambda e, i=i, e_i=e_i, pb=pb: e.scalar_tensor_tensor(
                    r_s[:, 1024 * i:1024 * (i + 1)], pb[:, 0:1024],
                    sel_s[:, 8 * i + e_i:8 * i + e_i + 1],
                    r_s[:, 1024 * i:1024 * (i + 1)], op0=ALU.mult, op1=ALU.add),
                    reads=(pk, "sel_s", "r%d" % i), writes=("r%d" % i,))

    def attn_b():
        for k in ARENA_B:
            bld.claim(k, ARENA_F)
        A_xnT = 32768
        A_svT = 0
        A_wq = [65536, 71680]
        A_qT = 77824
        A_kT = 81920
        A_P1 = 86016
        A_P3 = 90112
        A_vtokb = 94208
        A_vslot = 98304
        A_ptslot = 99328
        xnT = zb(A_xnT, 16384)
        svT = zb(A_svT, 16384)

        def xn_dst(i):
            return zb(A_xnT, 16384).rearrange("p (c t) -> p c t", c=8)[:, :, 128 * i:128 * (i + 1)]
        ln_pass(list(range(16)), xn_dst, ("xnT",), lambda i: "r%d" % i)

        for hp in range(b_hp):
            buf = hp % 2
            wq = zb(A_wq[buf], 3072)
            qT = zb(A_qT, 2048)
            kT = zb(A_kT, 2048)
            P1 = zb(A_P1, 2048)
            P3 = zb(A_P3, 2048)
            vtokb = zb(A_vtokb, 2048)
            vslot = zb(A_vslot, 512)
            ptslot = zb(A_ptslot, 512)
            for kind in range(3):
                src = _ap_custom(di["wqkv"], [[3072, 128], [393216, 8], [1, 128]],
                                 1024 * kind + 128 * hp)
                bld.dma("Wg", lambda e, buf=buf, src=src, kind=kind: e.dma_start(
                    out=zb(A_wq[buf], 3072).rearrange("p (c f) -> p c f", f=384)[:, :, 128 * kind:128 * (kind + 1)],
                    in_=src), writes=("wq%d" % buf,))

            for kind in range(3):
                for nb in range(4):
                    pb, pk = (p4, "p4") if nb % 2 == 0 else (p5, "p5")

                    def qf(e, kind=kind, nb=nb, pb=pb, wq=wq):
                        last = None
                        for c in range(8):
                            last = e.matmul(pb[:, 0:512],
                                            wq[:, 384 * c + 128 * kind:384 * c + 128 * (kind + 1)],
                                            xnT[:, 2048 * c + 512 * nb:2048 * c + 512 * (nb + 1)],
                                            start=(c == 0), stop=(c == 7))
                        return last
                    mmg([qf], reads=("wq%d" % buf, "xnT"), writes=(pk,))
                    if kind == 0:
                        bld.op("act", lambda e, nb=nb, pb=pb: e.activation(
                            qT[:, 512 * nb:512 * (nb + 1)], pb[:, 0:512], AF.Copy, scale=qsc_t[:]),
                            reads=(pk, "qsc"), writes=("qTb",))
                    elif kind == 1:
                        bld.op("act", lambda e, nb=nb, pb=pb: e.copy(
                            kT[:, 512 * nb:512 * (nb + 1)], pb[:, 0:512]),
                            reads=(pk,), writes=("kTb",))
                    else:
                        bld.op("act", lambda e, nb=nb, pb=pb: e.copy(
                            vslot[:], pb[:, 0:512]), reads=(pk,), writes=("vslot_b",))

                        def vtr(e, nb=nb):
                            last = None
                            for jj in range(4):
                                last = e.transpose(p6b[:, 128 * jj:128 * (jj + 1)],
                                                   vslot[:, 128 * jj:128 * (jj + 1)], idb[:])
                            return last
                        bld.op("pe", vtr, reads=("vslot_b", "idb"), writes=("p6",))
                        bld.op("act", lambda e, nb=nb: e.copy(
                            vtokb[:, 512 * nb:512 * (nb + 1)], p6b[:, 0:512]),
                            reads=("p6",), writes=("vtokb",))

            if b_stage < 1:
                continue
            for (srcT, sk, dst, dk) in [(qT, "qTb", qlmT, "qlmT"), (kT, "kTb", klmT, "klmT")]:
                bld.op("dve", lambda e, srcT=srcT: e.tensor_reduce(
                    lm_f[:].rearrange("p (a b) -> p a b", b=1),
                    srcT.rearrange("p (a b) -> p a b", b=32),
                    axis=AX.X, op=ALU.add, opt_input=False, opt_output=False),
                    reads=(sk,), writes=("lm_f",))
                bld.op("dve", lambda e, dst=dst: e.tensor_scalar(
                    dst[:], lm_f[:], 1.0 / 32, None, op0=ALU.mult),
                    reads=("lm_f",), writes=(dk,))

            if b_stage < 2:
                continue
            for j4 in range(4):
                def k1f(e, j4=j4, hh=0, pb=p7):
                    last = None
                    for jj in range(4):
                        j = 4 * j4 + jj
                        last = e.matmul(pb[:, 64 * jj:64 * (jj + 1)],
                                        qT[64 * hh:64 * (hh + 1), 128 * j:128 * (j + 1)],
                                        klmT[64 * hh:64 * (hh + 1), :],
                                        start=True, stop=True, tile_position=(64 * hh, 0))
                    return last
                mmg([lambda e, j4=j4: k1f(e, j4, 0, p7)], reads=("qTb", "klmT"), writes=("p7",))
                mmg([lambda e, j4=j4: k1f(e, j4, 1, p6)], reads=("qTb", "klmT"), writes=("p6",))
                for hh, pb, pk in [(0, p7, "p7"), (1, p6, "p6")]:
                    dst = P1[:, 512 * j4:512 * (j4 + 1)].rearrange(
                        "p (j h m) -> p j h m", h=2, m=64)[:, :, hh, :]
                    bld.op("act", lambda e, pb=pb, dst=dst: e.activation(
                        dst, pb[:, 0:256].rearrange("p (j m) -> p j m", m=64), AF.Exp),
                        reads=(pk,), writes=("P1",))
                bld.op("dve", lambda e, j4=j4: e.tensor_reduce(
                    rsum32[:, 8 * j4:8 * (j4 + 1)].rearrange("p (a b) -> p a b", b=1),
                    P1[:, 512 * j4:512 * (j4 + 1)].rearrange("p (a b) -> p a b", b=64),
                    axis=AX.X, op=ALU.add, opt_input=False, opt_output=False),
                    reads=("P1",), writes=("rsum32",))
            bld.op("dve", lambda e: e.reciprocal(rrec32[:], rsum32[:]),
                   reads=("rsum32",), writes=("rrec32",))
            for j in range(16 if b_stage != 102 else 0):
                for hh in range(2):
                    bld.op("dve", lambda e, j=j, hh=hh: e.tensor_scalar(
                        P1[:, 128 * j + 64 * hh:128 * j + 64 * (hh + 1)],
                        P1[:, 128 * j + 64 * hh:128 * j + 64 * (hh + 1)],
                        rrec32[:, 2 * j + hh:2 * j + hh + 1], None, op0=ALU.mult),
                        reads=("P1", "rrec32"), writes=("P1",))

            if b_stage < 3:
                continue
            for half in range(2):
                for hh, pb, pk in [(0, p01, "p01"), (1, p23, "p23")]:
                    def k3f(e, half=half, hh=hh, pb=pb):
                        last = None
                        for nb in range(2):
                            last = e.matmul(pb[64 * hh:64 * (hh + 1), 512 * nb:512 * (nb + 1)],
                                            qlmT[64 * hh:64 * (hh + 1), :],
                                            kT[64 * hh:64 * (hh + 1), 1024 * half + 512 * nb:1024 * half + 512 * (nb + 1)],
                                            start=True, stop=True, tile_position=(64 * hh, 64 * hh))
                        return last
                    mmg([k3f], reads=("qlmT", "kTb"), writes=(pk,))
                    bld.op("act", lambda e, half=half, hh=hh, pb=pb: e.activation(
                        P3[64 * hh:64 * (hh + 1), 1024 * half:1024 * (half + 1)],
                        pb[64 * hh:64 * (hh + 1), 0:1024], AF.Exp,
                        accum_out=s3sum[64 * hh:64 * (hh + 1), half:half + 1]),
                        reads=(pk,), writes=("P3", "s3sum"))
            bld.op("dve", lambda e: e.tensor_tensor(s3tot[:], s3sum[:, 0:1], s3sum[:, 1:2], op=ALU.add),
                   reads=("s3sum",), writes=("s3tot",))
            bld.op("dve", lambda e: e.reciprocal(s3rec[:], s3tot[:]),
                   reads=("s3tot",), writes=("s3rec",))
            bld.op("dve", lambda e: e.tensor_scalar(P3[:], P3[:], s3rec[:], None, op0=ALU.mult),
                   reads=("P3", "s3rec"), writes=("P3",))

            if b_stage < 4:
                continue
            for j4 in range(4):
                def p3t(e, j4=j4):
                    last = None
                    for jj in range(4):
                        last = e.transpose(p6b[:, 128 * jj:128 * (jj + 1)],
                                           P3[:, 128 * (4 * j4 + jj):128 * (4 * j4 + jj + 1)], idb[:])
                    return last
                bld.op("pe", p3t, reads=("P3", "idb"), writes=("p6",))
                bld.op("act", lambda e: e.copy(ptslot[:], p6b[:, 0:512]),
                       reads=("p6",), writes=("ptslot",))

                def k3vf(e, j4=j4):
                    last = None
                    for jj in range(4):
                        j = 4 * j4 + jj
                        for hh in range(2):
                            last = e.matmul(p23[0:64, 64 * hh:64 * (hh + 1)],
                                            ptslot[:, 128 * jj + 64 * hh:128 * jj + 64 * (hh + 1)],
                                            vtokb[:, 128 * j + 64 * hh:128 * j + 64 * (hh + 1)],
                                            start=(j == 0 and hh == 0), stop=(j == 15 and hh == 1),
                                            tile_position=(0, 0))
                    return last
                mmg([k3vf], reads=("ptslot", "vtokb"), writes=("p23",))
            bld.op("dve", lambda e: e.tensor_copy(k3vc[:], p23[0:64, 0:128]),
                   reads=("p23",), writes=("i_k3v",))

            if b_stage < 5:
                continue
            mmg([lambda e: e.matmul(p4[0:64, 0:64], qlmT[0:64, :], klmT[0:64, :],
                                    start=True, stop=True, tile_position=(0, 0))],
                reads=("qlmT", "klmT"), writes=("p4",))
            mmg([lambda e: e.matmul(p5[0:64, 0:64], qlmT[64:128, :], klmT[64:128, :],
                                    start=True, stop=True, tile_position=(64, 0))],
                reads=("qlmT", "klmT"), writes=("p5",))
            for hh, pb, pk in [(0, p4, "p4"), (1, p5, "p5")]:
                bld.op("act", lambda e, hh=hh, pb=pb: e.activation(
                    er_s[0:64, 64 * hh:64 * (hh + 1)], pb[0:64, 0:64], AF.Exp,
                    accum_out=k2sum[0:64, hh:hh + 1]),
                    reads=(pk,), writes=("er_s", "k2sum"))
            bld.op("dve", lambda e: e.reciprocal(k2rec[0:64, 0:2], k2sum[0:64, 0:2]),
                   reads=("k2sum",), writes=("k2rec",))
            for hh in range(2):
                bld.op("dve", lambda e, hh=hh: e.tensor_scalar(
                    K2c[:, 64 * hh:64 * (hh + 1)], er_s[0:64, 64 * hh:64 * (hh + 1)],
                    k2rec[0:64, hh:hh + 1], None, op0=ALU.mult),
                    reads=("er_s", "k2rec"), writes=("i_K2",))

            if b_stage < 6:
                continue
            mmg([lambda e: [e.matmul(p4[0:1, 64 + 64 * hh:128 + 64 * hh],
                                     onesc[0:64, :], K2c[:, 64 * hh:64 * (hh + 1)],
                                     start=True, stop=True, tile_position=(0, 0))
                            for hh in range(2)][-1]],
                reads=("i_K2", "onesc"), writes=("p4",))
            bld.op("dve", lambda e: e.tensor_copy(denom_s[:], p4[0:1, 64:192]),
                   reads=("p4",), writes=("denom",))
            bld.op("dve", lambda e: e.tensor_reduce(
                dmax_s[:].rearrange("p (a b) -> p a b", b=1),
                denom_s[:].rearrange("p (a b) -> p a b", b=64),
                axis=AX.X, op=ALU.max, opt_input=False, opt_output=False),
                reads=("denom",), writes=("dmax",))
            bld.op("dve", lambda e: e.reciprocal(drec_s[:], dmax_s[:]),
                   reads=("dmax",), writes=("drec",))
            mmg([lambda e: e.matmul(p4[:, 192:194], ones1f[:], drec_s[:], start=True, stop=True)],
                reads=("drec", "ones1f"), writes=("p4",))
            bld.op("dve", lambda e: e.tensor_copy(dbc_s[:], p4[:, 192:194]),
                   reads=("p4",), writes=("dbc",))

            if b_stage < 7:
                continue
            bld.op("pe", lambda e: e.transpose(p6b[0:64, 0:64], K2c[:, 0:64], idb[0:64, 0:64],
                                               tile_position=(0, 0)),
                   reads=("i_K2", "idb"), writes=("p6",))
            bld.op("act", lambda e: e.copy(K2Tc[:, 0:64], p6b[0:64, 0:64]),
                   reads=("p6",), writes=("i_K2T",))
            bld.op("pe", lambda e: e.transpose(p7b[0:64, 0:64], K2c[:, 64:128], idb[0:64, 0:64],
                                               tile_position=(0, 0)),
                   reads=("i_K2", "idb"), writes=("p7",))
            bld.op("act", lambda e: e.copy(K2Tc[:, 64:128], p7b[0:64, 0:64]),
                   reads=("p7",), writes=("i_K2T",))
            for hh in range(2):
                bld.op("dve", lambda e, hh=hh: e.tensor_scalar(
                    Vc[:, 64 * hh:64 * (hh + 1)], K2Tc[:, 64 * hh:64 * (hh + 1)],
                    dbc_s[0:64, hh:hh + 1], None, op0=ALU.mult),
                    reads=("i_K2T", "dbc"), writes=("i_V",))
                bld.op("dve", lambda e, hh=hh: e.tensor_scalar(
                    Vtc[:, 64 * hh:64 * (hh + 1)], K2c[:, 64 * hh:64 * (hh + 1)],
                    dbc_s[0:64, hh:hh + 1], None, op0=ALU.mult),
                    reads=("i_K2", "dbc"), writes=("i_Vt",))

            if b_stage < 8:
                continue

            def pair0(dst_p, c0, lhs, rhs, lk, rk, pk):
                def f(e, dst_p=dst_p, c0=c0, lhs=lhs, rhs=rhs):
                    last = None
                    for hh in range(2):
                        last = e.matmul(dst_p[0:64, c0 + 64 * hh:c0 + 64 * (hh + 1)],
                                        lhs[:, 64 * hh:64 * (hh + 1)],
                                        rhs[:, 64 * hh:64 * (hh + 1)],
                                        start=True, stop=True, tile_position=(0, 0))
                    return last
                mmg([f], reads=(lk, rk), writes=(pk,))

            def evac(dst, c0, dk, pk, pb, scale=None):
                if scale is None:
                    bld.op("dve", lambda e, dst=dst, c0=c0, pb=pb: e.tensor_copy(
                        dst[:], pb[0:64, c0:c0 + 128]), reads=(pk,), writes=(dk,))
                else:
                    bld.op("dve", lambda e, dst=dst, c0=c0, pb=pb: e.tensor_scalar(
                        dst[:], pb[0:64, c0:c0 + 128], scale, None, op0=ALU.mult),
                        reads=(pk,), writes=(dk,))

            pair0(p4, 0, K2Tc, Vc, "i_K2T", "i_V", "p4")
            evac(Ac, 0, "i_A", "p4", p4)
            for it in range(6):
                bld.op("dve", lambda e: e.scalar_tensor_tensor(
                    U1c[:], Ac[:], -1.0, c7b2[:], op0=ALU.mult, op1=ALU.add),
                    reads=("i_A", "c7"), writes=("i_U1",))
                pair0(p4, 0, Ac, U1c, "i_A", "i_U1", "p4")
                evac(U2c, 0, "i_U2", "p4", p4)
                bld.op("dve", lambda e: e.scalar_tensor_tensor(
                    U3c[:], U2c[:], -1.0, c15b2[:], op0=ALU.mult, op1=ALU.add),
                    reads=("i_U2", "c15"), writes=("i_U3",))
                pair0(p4, 128, Ac, U3c, "i_A", "i_U3", "p4")
                evac(U4c, 128, "i_U4", "p4", p4)
                bld.op("dve", lambda e: e.scalar_tensor_tensor(
                    U5c[:], U4c[:], -1.0, c13b2[:], op0=ALU.mult, op1=ALU.add),
                    reads=("i_U4", "c13"), writes=("i_U5",))
                pair0(p4, 0, Vtc, U5c, "i_Vt", "i_U5", "p4")
                pair0(p4, 128, U5c, Vtc, "i_U5", "i_Vt", "p4")
                pair0(p4, 256, Ac, U5c, "i_A", "i_U5", "p4")
                evac(Vc, 0, "i_V", "p4", p4, scale=0.25)
                evac(Vtc, 128, "i_Vt", "p4", p4, scale=0.25)
                evac(Ac, 256, "i_A", "p4", p4, scale=0.25)

            if b_stage < 9:
                continue
            pair0(p4, 0, Vtc, k3vc, "i_Vt", "i_k3v", "p4")
            evac(Zc, 0, "i_Z", "p4", p4)

            if b_stage < 10:
                continue
            for j4 in range(4):
                def p1t(e, j4=j4):
                    last = None
                    for jj in range(4):
                        for hh in range(2):
                            last = e.transpose(
                                p6b[0:64, 256 * jj + 128 * hh:256 * jj + 128 * (hh + 1)],
                                P1[:, 128 * (4 * j4 + jj) + 64 * hh:128 * (4 * j4 + jj) + 64 * (hh + 1)],
                                idb[:], tile_position=(0, 0))
                    return last
                bld.op("pe", p1t, reads=("P1", "idb"), writes=("p6",))
                ptsv = zb(A_vslot, 1024)
                bld.op("act", lambda e, ptsv=ptsv: e.copy(ptsv[0:64, 0:1024], p6b[0:64, 0:1024]),
                       reads=("p6",), writes=("ptslot", "vslot_b"))

                def svf(e, j4=j4, ptsv=ptsv):
                    last = None
                    for jj in range(4):
                        for hh in range(2):
                            last = e.matmul(p23[64 * hh:64 * (hh + 1), 512 * hh + 128 * jj:512 * hh + 128 * (jj + 1)],
                                            Zc[:, 64 * hh:64 * (hh + 1)],
                                            ptsv[0:64, 256 * jj + 128 * hh:256 * jj + 128 * (hh + 1)],
                                            start=True, stop=True, tile_position=(0, 64 * hh))
                    return last
                mmg([svf], reads=("i_Z", "ptslot"), writes=("p23",))
                for hh in range(2):
                    bld.op("dve", lambda e, j4=j4, hp=hp, hh=hh: e.tensor_copy(
                        svT[64 * hh:64 * (hh + 1), 2048 * hp + 512 * j4:2048 * hp + 512 * (j4 + 1)],
                        p23[64 * hh:64 * (hh + 1), 512 * hh:512 * hh + 512]),
                        reads=("p23",), writes=("svT",))

        if b_stage < 11:
            return
        A_wpb = 77824
        wpb_s = zb(A_wpb, 8192)
        bld.claim("wpb", ["qTb", "kTb", "P1", "P3"])
        bld.dma("Wg", lambda e: e.dma_start(
            out=zb(A_wpb, 8192), in_=_ap_custom(di["wpb"], [[1024, 128], [131072, 8], [1, 1024]], 0)),
            writes=("wpb",))
        for i in range(16):
            pb, pk = (p01, "p01") if i % 2 == 0 else (p23, "p23")

            def bpf(e, i=i, pb=pb):
                last = None
                for nb in range(2):
                    for c in range(8):
                        last = e.matmul(pb[:, 512 * nb:512 * (nb + 1)],
                                        svT[:, 2048 * c + 128 * i:2048 * c + 128 * (i + 1)],
                                        wpb_s[:, 1024 * c + 512 * nb:1024 * c + 512 * nb + 512],
                                        start=(c == 0), stop=(c == 7))
                return last
            mmg([bpf], reads=("svT", "wpb"), writes=(pk,))
            bld.op("dve", lambda e, i=i, pb=pb: e.tensor_tensor(
                r_s[:, 1024 * i:1024 * (i + 1)], pb[:, 0:1024],
                r_s[:, 1024 * i:1024 * (i + 1)], op=ALU.add),
                reads=(pk, "r%d" % i), writes=("r%d" % i,))

    def tap(name):
        if name in tap_d:
            bld.dma("St", lambda e: e.dma_start(
                out=_ap_custom(tap_d[name], [[1024, 128], [131072, 16], [1, 1024]], 0), in_=r_s[:]),
                reads=tuple("r%d" % i for i in range(16)))

    loads()
    if 'attn_a' in phases:
        attn_a()
        tap("attn_a")
    if 'ffn0' in phases:
        ffn(0)
        tap("ffn1")
    if 'attn_b' in phases:
        attn_b()
        tap("attn_b")
    if 'ffn1' in phases:
        ffn(1)
    bld.dma("St", lambda e: e.dma_start(
        out=_ap_custom(out_d, [[1024, 128], [131072, 16], [1, 1024]], 0), in_=r_s[:]),
        reads=tuple("r%d" % i for i in range(16)))
    bld.finish()
    return nc, stack


def _in_maps(inputs, n_cores=8):
    cons = _host_consts()
    w_attn = np.asarray(inputs["L0_attn_a_w_attn"], np.float32)
    wq = w_attn[0:C].reshape(C // 2, 2, C)[:, ::-1, :].reshape(C, C)
    waT = np.ascontiguousarray(np.concatenate([wq, w_attn[C:]], 0).T)
    b_attn = np.asarray(inputs["L0_attn_a_b_attn"], np.float32).copy()
    b_attn[0:C] = b_attn[0:C].reshape(C // 2, 2)[:, ::-1].reshape(C)
    base = {
        "waT": waT,
        "b_attn": b_attn,
        "wpaT": np.ascontiguousarray(np.asarray(inputs["L0_attn_a_w_proj"], np.float32).T),
        "b_proj": np.asarray(inputs["L0_attn_a_b_proj"], np.float32),
        "wassT": np.ascontiguousarray(np.asarray(inputs["L0_attn_a_w_assign"], np.float32).T),
        "wqkv": np.asarray(inputs["L0_attn_b_w_qkv"], np.float32),
        "wpb": np.asarray(inputs["L0_attn_b_w_proj"], np.float32),
        "rwT": np.ascontiguousarray(np.asarray(inputs["L0_ffn_router_w"], np.float32).T),
        "rb": np.asarray(inputs["L0_ffn_router_b"], np.float32),
        "weT": np.ascontiguousarray(np.asarray(inputs["L0_ffn_expert_w"], np.float32).transpose(0, 2, 1)),
        "eb": np.asarray(inputs["L0_ffn_expert_b"], np.float32),
    }
    base.update(cons)
    x = np.asarray(inputs["L0_attn_a_x"], np.float32)
    maps = []
    for cid in range(n_cores):
        m = dict(base)
        m["x"] = np.ascontiguousarray(x[cid])
        maps.append(m)
    return maps


def kernel(**inputs):
    nc, stack = build()
    try:
        maps = _in_maps(inputs)
        res = run_bass_kernel_spmd(nc, maps, core_ids=list(range(8)))
    finally:
        stack.__exit__(None, None, None)
    return np.stack([np.asarray(r["out"], np.float32) for r in res.results], axis=0)


# version 04
# speedup vs baseline: 1.3179x; 1.3179x over previous
import contextlib
import math

import numpy as np
import ml_dtypes
import bass_rust
import concourse.bass as bass
import concourse.mybir as mybir
from concourse.bass_utils import run_bass_kernel_spmd

F32 = mybir.dt.float32
BF16 = mybir.dt.bfloat16
AF = mybir.ActivationFunctionType
ALU = mybir.AluOpType
AX = mybir.AxisListType
bf16 = ml_dtypes.bfloat16

B, T, C = 8, 2048, 1024
NH, HD = 16, 64
NCL, GS = 64, 256
NTPC = T // NCL
G = T // GS
NCPG = GS // NTPC
CMAX = (G - 1) * NCPG
NLM = 64
NE = 8

NCHUNK = 16
NQ = 4


def _sinusoidal(n, d):
    pos = np.arange(n, dtype=np.float32)[:, None]
    div = np.exp((np.arange(0, d, 2, dtype=np.float32) * np.float32(-math.log(10000.0) / d)).astype(np.float32))
    ang = (pos * div).astype(np.float32)
    return np.stack([np.sin(ang), np.cos(ang)], axis=-1).reshape(n, d).astype(np.float32)


def _host_consts():
    S = _sinusoidal(CMAX + GS, HD)
    sinv = S[:, 0:32].T
    cosv = S[:, 32:64].T
    rotq = np.zeros((128, 512), np.float32)
    rotkl = np.zeros((128, 2048), np.float32)
    scale = np.float32(1.0 / math.sqrt(HD))
    for r in range(128):
        i = (r % 64) // 2
        tq = np.arange(512) % 256
        tkl = 8 * (np.arange(2048) // 256) + (np.arange(2048) % 256)
        if r % 2 == 0:
            rotq[r] = -cosv[i, tq] * scale
            rotkl[r] = -cosv[i, tkl]
        else:
            rotq[r] = sinv[i, tq] * scale
            rotkl[r] = sinv[i, tkl]
    tkc = np.zeros((64, 1024), np.float32)
    for c in range(CMAX):
        v = np.zeros(64, np.float32)
        v[0::2] = -cosv[:, c]
        v[1::2] = sinv[:, c]
        tkc[c] = np.tile(v, NH)
    tri = np.triu(np.full((128, 128), -1e30, np.float32), 1)
    perm = np.zeros((128, 128), np.float32)
    for k in range(128):
        perm[k, k ^ 1] = 1.0
    idn = np.eye(128, dtype=np.float32)
    blk = np.zeros((64, 128), np.float32)
    blk[np.arange(128) % 64, np.arange(128)] = 1.0
    return {
        "ROTQ": rotq.astype(bf16), "ROTKL": rotkl.astype(bf16),
        "TKC": tkc.astype(bf16), "TRI": tri.astype(bf16),
        "PERM": perm.astype(bf16), "IDB": idn.astype(bf16),
        "C7": (7.0 * blk).astype(bf16), "C15": (15.0 * blk).astype(bf16),
        "C13": (13.0 * blk).astype(bf16),
    }


ENG2ATTR = {"pe": "tensor", "dve": "vector", "act": "scalar", "pool": "gpsimd", "sp": "sync"}


class Bld:
    def __init__(self, nc, stack):
        self.nc = nc
        self.stk = stack
        self.q = {e: [] for e in ENG2ATTR}
        self.sem = {}
        self.cnt = {e: 0 for e in ENG2ATTR}
        self.seen = {e: {} for e in ENG2ATTR}
        self.bufs = {}
        self.ch = {}
        for e in ENG2ATTR:
            self.sem[e] = stack.enter_context(nc.semaphore("sem_" + e))
        for name, eng in [("Ld", "sp"), ("Tx", "sp"), ("St", "sp"),
                          ("Wg", "pool"), ("Sc", "pool")]:
            self.sem[name] = stack.enter_context(nc.semaphore("sem_" + name))
            self.ch[name] = {"eng": eng, "issued": 0, "fenced": 0, "maxwait": 0}

    def _collect(self, reads, writes):
        waits = {}

        def add(tok):
            if tok is None:
                return
            s, v = tok
            if waits.get(s, -1) < v:
                waits[s] = v

        for k in reads:
            b = self.bufs.get(k)
            if b:
                add(b["w"])
        for k in writes:
            b = self.bufs.get(k)
            if b:
                add(b["w"])
                for s, v in b["r"].items():
                    add((s, v))
        return waits

    def _wait_tok_for(self, s):
        if s in self.ch:
            c = self.ch[s]
            v = 16 * c["issued"]
            c["maxwait"] = max(c["maxwait"], v)
            return v
        return None

    def _emit_waits(self, eng, waits):
        out = []
        for s, v in waits.items():
            if s in self.ch:
                v2 = self._wait_tok_for(s)
                v = max(v, 0) if v2 is None else v2
            if self.seen[eng].get(s, -1) >= v:
                continue
            self.seen[eng][s] = v
            out.append((self.sem[s], v))
        return out

    def _track(self, reads, writes, tok):
        s, v = tok
        for k in writes:
            self.bufs[k] = {"w": (s, v), "r": {}}
        for k in reads:
            b = self.bufs.setdefault(k, {"w": None, "r": {}})
            if b["r"].get(s, -1) < v:
                b["r"][s] = v

    def op(self, eng, fn, reads=(), writes=()):
        waits = self._collect(reads, writes)
        wl = self._emit_waits(eng, waits)
        self.cnt[eng] += 1
        n = self.cnt[eng]
        sem = self.sem[eng]

        def run(e, wl=wl, fn=fn, sem=sem, n=n):
            for s, v in wl:
                e.wait_ge(s, v)
            inst = fn(e)
            inst.then_inc(sem, 1)
            return inst

        self.q[eng].append(run)
        self._track(reads, writes, (eng, n))

    def dma(self, chname, fn, reads=(), writes=()):
        c = self.ch[chname]
        eng = c["eng"]
        if c["maxwait"] > 16 * c["fenced"]:
            tgt = 16 * c["issued"]
            c["fenced"] = c["issued"]
            sem = self.sem[chname]

            def fence(e, sem=sem, tgt=tgt):
                e.wait_ge(sem, tgt)
            self.q[eng].append(lambda e, f=fence: f(e) or _NOINST)
            self.seen[eng][chname] = tgt
        waits = self._collect(reads, writes)
        wl = self._emit_waits(eng, waits)
        c["issued"] += 1
        v = 16 * c["issued"]
        sem = self.sem[chname]

        def run(e, wl=wl, fn=fn, sem=sem):
            for s, vv in wl:
                e.wait_ge(s, vv)
            inst = fn(e)
            inst.then_inc(sem, 16)
            return inst

        self.q[eng].append(run)
        self._track(reads, writes, (chname, v))

    def claim(self, newkey, oldkeys):
        b = self.bufs.setdefault(newkey, {"w": None, "r": {}})
        for k in oldkeys:
            if k == newkey:
                continue
            ob = self.bufs.get(k)
            if not ob:
                continue
            for tok in ([ob["w"]] if ob["w"] else []) + list(ob["r"].items()):
                s_, v = tok
                if b["r"].get(s_, -1) < v:
                    b["r"][s_] = v

    def finish(self):
        for name, c in self.ch.items():
            if c["issued"] > 0:
                tgt = 16 * c["issued"]
                sem = self.sem[name]
                self.q["sp"].append(lambda e, s=sem, t=tgt: e.wait_ge(s, t) or _NOINST)
        blk = self.stk.enter_context(self.nc.Block())
        for ename, attr in ENG2ATTR.items():
            ops = self.q[ename]
            getattr(blk, attr)(lambda eng, ops=ops: [f(eng) for f in ops] and None)


class _NoInst:
    def then_inc(self, *a):
        return self


_NOINST = _NoInst()


def _ap_custom(base_ap, dims, offset):
    ap = base_ap.copy()
    ap.ap = bass_rust.VecI64Pair(dims)
    ap.offset = base_ap.offset + offset
    return ap


def build(taps=(), dbg=False, phases=('attn_a', 'ffn0', 'attn_b', 'ffn1'), b_hp=8, b_stage=99):
    nc = bass.Bass("TRN2", target_bir_lowering=False)
    stack = contextlib.ExitStack()
    stack.__enter__()
    bld = Bld(nc, stack)
    sb = lambda name, shape, dt: stack.enter_context(nc.sbuf_tensor(name, shape, dt))
    ps = lambda name, shape, dt: stack.enter_context(nc.psum_tensor(name, shape, dt))

    di = {}
    ins = [
        ("x", [T, C]), ("waT", [C, 3 * C]), ("b_attn", [3 * C]),
        ("wpaT", [C, C]), ("b_proj", [C]), ("wassT", [C, NCL]),
        ("wqkv", [C, 3 * C]), ("wpb", [C, C]),
        ("rwT", [C, NE]), ("rb", [NE]), ("weT", [NE, C, C]), ("eb", [NE, C]),
    ]
    bf_ins = [("ROTQ", [128, 512]), ("ROTKL", [128, 2048]), ("TKC", [64, 1024]),
              ("TRI", [128, 128]), ("PERM", [128, 128]), ("IDB", [128, 128]),
              ("C7", [64, 128]), ("C15", [64, 128]), ("C13", [64, 128])]
    for n, shape in ins:
        di[n] = nc.dram_tensor(n, shape, F32, kind="ExternalInput").ap()
    for n, shape in bf_ins:
        di[n] = nc.dram_tensor(n, shape, BF16, kind="ExternalInput").ap()
    out_d = nc.dram_tensor("out", [T, C], F32, kind="ExternalOutput").ap()
    tap_d = {}
    for tname in taps:
        tap_d[tname] = nc.dram_tensor("tap_" + tname, [T, C], F32, kind="ExternalOutput").ap()
    dbg_d = {}
    if dbg:
        for n, shape in [("logit", [128, 128]), ("sel", [128, 128]), ("lmax", [128, 16]),
                         ("w16d", [128, 16]), ("xnTd", [128, 16384]), ("y0", [128, 1024])]:
            dbg_d[n] = nc.dram_tensor("dbg_" + n, shape, F32, kind="ExternalOutput").ap()
    lt_dram = nc.dram_tensor("lt_dram", [16, 512], F32, kind="Internal").ap()
    wbig_dram = nc.dram_tensor("wbig_dram", [T, NCL], BF16, kind="Internal").ap()

    r_s = sb("r_s", [128, 16 * 1024], F32)
    rotq = sb("rotq", [128, 512], BF16)
    rotkl = sb("rotkl", [128, 2048], BF16)
    tkc = sb("tkc_s", [64, 1024], BF16)
    tri = sb("tri_s", [128, 128], BF16)
    perm = sb("perm_s", [128, 128], BF16)
    idb = sb("idb_s", [128, 128], BF16)
    c7 = sb("c7_s", [64, 128], BF16)
    c15 = sb("c15_s", [64, 128], BF16)
    c13 = sb("c13_s", [64, 128], BF16)
    onesb = sb("onesb", [1, 128], BF16)
    ones1f = sb("ones1f", [1, 128], F32)
    onesc = sb("onesc", [128, 1], BF16)
    eps_t = sb("eps_t", [128, 1], F32)
    qsc_t = sb("qsc_t", [128, 1], F32)
    sum16 = sb("sum16", [128, 16], F32)
    ssq16 = sb("ssq16", [128, 16], F32)
    mean16 = sb("mean16", [128, 16], F32)
    var16 = sb("var16", [128, 16], F32)
    rstd16 = sb("rstd16", [128, 16], F32)
    tmp16 = sb("tmp16", [128, 16], F32)
    xn_tok = sb("xn_tok", [128, 1024], BF16)
    battn = sb("battn", [128, 24], F32)
    bprow = sb("bprow", [1, 1024], BF16)
    ebrow = sb("ebrow", [1, 2048], BF16)
    rbrow = sb("rbrow", [1, 8], BF16)
    wass = sb("wass", [128, 8 * 64], BF16)
    rwt = sb("rwt", [128, 64], BF16)
    logit_s = sb("logit_s", [128, 128], F32)
    er_s = sb("er_s", [128, 128], F32)
    lmax16 = sb("lmax16", [128, 16], F32)
    esum16 = sb("esum16", [128, 16], F32)
    elmax16 = sb("elmax16", [128, 16], F32)
    w16 = sb("w16", [128, 16], F32)
    sel_s = sb("sel_s", [128, 128], F32)
    eblk = sb("eblk", [16, 32], F32)
    wsum_c = sb("wsum_c", [16, 1], F32)
    wrec_c = sb("wrec_c", [16, 1], F32)
    wblk_bf = sb("wblk_bf", [16, 32], BF16)
    wbig_s = sb("wbig_s", [128, 4 * 64], BF16)
    kcr_cm = sb("kcr_cm", [64, 1024], BF16)
    cv_bf = sb("cv_bf", [64, 1024], BF16)
    kcrT = sb("kcrT", [128, 8 * 56], BF16)
    p_sm_b = [sb("p_sm0", [128, 512], BF16), sb("p_sm1", [128, 512], BF16)]
    pt_sm_b = [sb("pt_sm0", [128, 384], BF16), sb("pt_sm1", [128, 384], BF16)]
    ssum_b = [sb("ssum0", [128, 1], F32), sb("ssum1", [128, 1], F32)]
    srec_b = [sb("srec0", [128, 1], F32), sb("srec1", [128, 1], F32)]
    zero64 = sb("zero64", [128, 64], BF16)
    qlmT = sb("qlmT", [128, 64], BF16)
    klmT = sb("klmT", [128, 64], BF16)
    lm_f = sb("lm_f", [128, 64], F32)
    rsum32 = sb("rsum32", [128, 32], F32)
    rrec32 = sb("rrec32", [128, 32], F32)
    s3sum = sb("s3sum", [128, 2], F32)
    s3tot = sb("s3tot", [128, 1], F32)
    s3rec = sb("s3rec", [128, 1], F32)
    k2sum = sb("k2sum", [64, 2], F32)
    k2rec = sb("k2rec", [64, 2], F32)
    denom_s = sb("denom_s", [1, 128], F32)
    dmax_s = sb("dmax_s", [1, 2], F32)
    drec_s = sb("drec_s", [1, 2], F32)
    dbc_s = sb("dbc_s", [128, 2], F32)
    K2c = sb("K2c", [64, 128], BF16)
    K2Tc = sb("K2Tc", [64, 128], BF16)
    Ac = sb("Ac", [64, 128], BF16)
    Vc = sb("Vc", [64, 128], BF16)
    Vtc = sb("Vtc", [64, 128], BF16)
    U1c = sb("U1c", [64, 128], BF16)
    U2c = sb("U2c", [64, 128], BF16)
    U3c = sb("U3c", [64, 128], BF16)
    U4c = sb("U4c", [64, 128], BF16)
    U4Tc = sb("U4Tc", [64, 128], BF16)
    U5c = sb("U5c", [64, 128], BF16)
    k3vc = sb("k3vc", [64, 128], BF16)
    Zc = sb("Zc", [64, 128], BF16)
    c7b2, c15b2, c13b2 = c7, c15, c13
    ZB = sb("ZB", [128, 50176], BF16)
    ZF = sb("ZF", [128, 2560], F32)
    ck_s = ZF[0:64, 0:1024]
    cv_s = ZF[0:64, 1024:2048]
    lt_s = ZF[0:16, 2048:2560]

    p01 = ps("p01", [128, 1024], F32)
    p23 = ps("p23", [128, 1024], F32)
    p4 = ps("p4", [128, 512], F32)
    p5 = ps("p5", [128, 512], F32)
    p6 = ps("p6", [128, 512], F32)
    p7 = ps("p7", [128, 512], F32)
    p6b = p6.bitcast(BF16)
    p7b = p7.bitcast(BF16)
    p01b = p01.bitcast(BF16)
    PK = {id(p01): "p01", id(p23): "p23", id(p4): "p4", id(p5): "p5",
          id(p6): "p6", id(p7): "p7"}

    assert nc.sbuf_bytes_remaining >= 0, nc.sbuf_bytes_remaining

    def zb(off_b, cols):
        c0 = off_b // 2
        return ZB[:, c0:c0 + cols]

    def zf(off_b, cols):
        c0 = off_b // 4
        return ZF[:, c0:c0 + cols]

    def memset(tile, val, key):
        bld.op("dve", lambda e, t=tile, v=val: e.memset(t, v), writes=(key,))

    def ln_pass(chunks, dst_fn, dst_keys, src_key_fn):
        for i in chunks:
            bld.op("dve", lambda e, i=i: e.reduce_sum(
                sum16[:, i:i + 1], r_s[:, 1024 * i:1024 * (i + 1)], axis=AX.X),
                reads=(src_key_fn(i),), writes=("sum16",))
            bld.op("act", lambda e, i=i: e.activation(
                xn_tok[:], r_s[:, 1024 * i:1024 * (i + 1)], AF.Square,
                accum_out=ssq16[:, i:i + 1]),
                reads=(src_key_fn(i),), writes=("xn_tok", "ssq16"))
        i0, i1 = chunks[0], chunks[-1] + 1
        sl = slice(i0, i1)
        bld.op("dve", lambda e: e.tensor_scalar(mean16[:, sl], sum16[:, sl], 1.0 / C, None, op0=ALU.mult),
               reads=("sum16",), writes=("mean16",))
        bld.op("dve", lambda e: e.tensor_tensor(tmp16[:, sl], mean16[:, sl], mean16[:, sl], op=ALU.mult),
               reads=("mean16",), writes=("tmp16",))
        bld.op("dve", lambda e: e.tensor_scalar(var16[:, sl], ssq16[:, sl], 1.0 / C, None, op0=ALU.mult),
               reads=("ssq16",), writes=("var16",))
        bld.op("dve", lambda e: e.tensor_tensor(var16[:, sl], var16[:, sl], tmp16[:, sl], op=ALU.subtract),
               reads=("tmp16", "var16"), writes=("var16",))
        bld.op("act", lambda e: e.activation(rstd16[:, sl], var16[:, sl], AF.Sqrt, bias=eps_t[:]),
               reads=("var16", "eps"), writes=("rstd16",))
        bld.op("dve", lambda e: e.reciprocal(rstd16[:, sl], rstd16[:, sl]),
               reads=("rstd16",), writes=("rstd16",))
        for i in chunks:
            bld.op("dve", lambda e, i=i: e.tensor_scalar(
                xn_tok[:], r_s[:, 1024 * i:1024 * (i + 1)],
                mean16[:, i:i + 1], rstd16[:, i:i + 1], op0=ALU.subtract, op1=ALU.mult),
                reads=(src_key_fn(i), "mean16", "rstd16"), writes=("xn_tok",))

            def trs(e, i=i):
                last = None
                for b in range(8):
                    last = e.transpose(p6b[:, 128 * b:128 * (b + 1)],
                                       xn_tok[:, 128 * b:128 * (b + 1)], idb[:])
                return last
            bld.op("pe", trs, reads=("xn_tok", "idb"), writes=("p6",))
            dst = dst_fn(i)
            bld.op("act", lambda e, dst=dst: e.copy(
                dst, p6b[:, 0:1024].rearrange("p (c t) -> p c t", t=128)),
                reads=("p6",), writes=dst_keys)

    def mmg(fns, reads, writes):
        def run(e, fns=fns):
            last = None
            for f in fns:
                last = f(e)
            return last
        bld.op("pe", run, reads=reads, writes=writes)

    def loads():
        bld.dma("Ld", lambda e: e.dma_start(
            out=r_s[:], in_=_ap_custom(di["x"], [[1024, 128], [131072, 16], [1, 1024]], 0)),
            writes=tuple(f"r{i}" for i in range(16)))
        for n, t, key in [("ROTQ", rotq, "rotq"), ("ROTKL", rotkl, "rotkl"),
                          ("TKC", tkc, "tkc"), ("TRI", tri, "tri")]:
            bld.dma("Ld", lambda e, n=n, t=t: e.dma_start(out=t[:], in_=di[n]), writes=(key,))
        for n, t, key in [("PERM", perm, "perm"), ("IDB", idb, "idb"),
                          ("C7", c7, "c7"), ("C15", c15, "c15"), ("C13", c13, "c13")]:
            bld.dma("Ld", lambda e, n=n, t=t: e.dma_start(out=t[:], in_=di[n]), writes=(key,))
        bld.dma("Wg", lambda e: e.dma_start(
            out=battn[:], in_=_ap_custom(di["b_attn"], [[1, 128], [128, 24]], 0),
            allow_slow_non_contiguous=True), writes=("battn",))
        bld.dma("Wg", lambda e: e.dma_start(out=bprow[:], in_=di["b_proj"].rearrange("(o d) -> o d", o=1)),
                writes=("bprow",))
        bld.dma("Wg", lambda e: e.dma_start(out=rbrow[:], in_=di["rb"].rearrange("(o d) -> o d", o=1)),
                writes=("rbrow",))
        bld.dma("Wg", lambda e: e.dma_start(
            out=wass[:], in_=_ap_custom(di["wassT"], [[64, 128], [8192, 8], [1, 64]], 0)), writes=("wass",))
        bld.dma("Wg", lambda e: e.dma_start(
            out=rwt[:], in_=_ap_custom(di["rwT"], [[8, 128], [1024, 8], [1, 8]], 0)), writes=("rwt",))
        memset(onesb[:], 1.0, "onesb")
        memset(ones1f[:], 1.0, "ones1f")
        memset(onesc[:], 1.0, "onesc")
        memset(eps_t[:], 1e-5, "eps")
        memset(qsc_t[:], 0.125, "qsc")
        memset(zero64[:], 0.0, "zero64")
        memset(ck_s[:], 0.0, "ck_s")
        memset(cv_s[:], 0.0, "cv_s")
        for i in range(16):
            bld.dma("Sc", lambda e, i=i: e.dma_start(
                out=wbig_dram[128 * i:128 * (i + 1), :], in_=zero64[:]),
                reads=("zero64",), writes=("wbig",))

    ARENA_A = ["waT0", "waT1", "xnTq", "qrT", "klrT", "ktok", "vtok",
               "kslot", "vslot_a", "yT", "wpT"]

    def attn_a():
        A_waT = 0
        A_xnTq = 49152
        A_qrT = 57344
        A_klrT = 65536
        A_ktok = 73728
        A_vtok = 81920
        A_kslot = 90112
        A_vslot = 91136
        A_yT = A_xnTq
        A_wpT = A_klrT
        waT_s = zb(A_waT, 8 * 3072)
        for half in range(2):
            bld.dma("Wg", lambda e, h=half: e.dma_start(
                out=zb(A_waT, 8 * 3072).rearrange("p (c f) -> p c f", f=3072)[:, :, 1536 * h:1536 * (h + 1)],
                in_=_ap_custom(di["waT"], [[3072, 128], [393216, 8], [1, 1536]], 1536 * h)),
                writes=("waT%d" % half,))

        for Q in range(NQ):
            xnTq = zb(A_xnTq, 4096)
            qrT = zb(A_qrT, 4096)
            klrT = zb(A_klrT, 4096)
            ktok = zb(A_ktok, 4096)
            vtok = zb(A_vtok, 4096)
            kslot = zb(A_kslot, 512)
            vslot = zb(A_vslot, 512)
            yT = zb(A_yT, 4096)

            bld.claim("xnTq", ["yT"])
            bld.claim("klrT", ["wpT"])
            bld.claim("ktok", ["wpT"])

            def xnq_dst(i, Q=Q):
                ii = i - 4 * Q
                return zb(A_xnTq, 4096).rearrange("p (c t) -> p c t", c=8)[:, :, 128 * ii:128 * (ii + 1)]
            ln_pass(list(range(4 * Q, 4 * Q + 4)), xnq_dst, ("xnTq",),
                    lambda i: "r%d" % i)

            for F in range(24):
                kind, co = F // 8, F % 8
                pb, pk = (p4, "p4") if F % 2 == 0 else (p5, "p5")

                def qf(e, F=F, pb=pb):
                    last = None
                    for c in range(8):
                        last = e.matmul(pb[:, 0:512],
                                        waT_s[:, 3072 * c + 128 * F:3072 * c + 128 * (F + 1)],
                                        xnTq[:, 512 * c:512 * (c + 1)],
                                        start=(c == 0), stop=(c == 7))
                    return last
                mmg([qf], reads=("waT0", "waT1", "xnTq"), writes=(pk,))
                if kind == 0:
                    bld.op("dve", lambda e, co=co, F=F, pb=pb: e.scalar_tensor_tensor(
                        qrT[:, 512 * co:512 * (co + 1)], pb[:, 0:512],
                        battn[:, F:F + 1], rotq[:], op0=ALU.add, op1=ALU.mult),
                        reads=(pk, "battn", "rotq"), writes=("qrT",))
                else:
                    slot, slotk = (kslot, "kslot") if kind == 1 else (vslot, "vslot_a")
                    bld.op("act", lambda e, F=F, pb=pb, slot=slot: e.activation(
                        slot[:], pb[:, 0:512], AF.Identity, bias=battn[:, F:F + 1]),
                        reads=(pk, "battn"), writes=(slotk,))
                    if kind == 1:
                        ppb, ppk = (p5, "p5") if F % 2 == 0 else (p4, "p4")
                        mmg([lambda e, ppb=ppb: e.matmul(
                            ppb[:, 0:512], perm[:], kslot[:], start=True, stop=True)],
                            reads=("kslot", "perm"), writes=(ppk,))
                        bld.op("dve", lambda e, co=co, ppb=ppb, Q=Q: e.tensor_tensor(
                            klrT[:, 512 * co:512 * (co + 1)], ppb[:, 0:512],
                            rotkl[:, 512 * Q:512 * (Q + 1)], op=ALU.mult),
                            reads=(ppk, "rotkl"), writes=("klrT",))
                    tokd, tokk = (ktok, "ktok") if kind == 1 else (vtok, "vtok")

                    def ttr(e, slot=slot):
                        last = None
                        for jj in range(4):
                            last = e.transpose(p6b[:, 128 * jj:128 * (jj + 1)],
                                               slot[:, 128 * jj:128 * (jj + 1)], idb[:])
                        return last
                    bld.op("pe", ttr, reads=(slotk, "idb"), writes=("p6",))
                    dst = tokd.rearrange("p (j d) -> p j d", j=4)[:, :, 128 * co:128 * (co + 1)]
                    bld.op("act", lambda e, dst=dst: e.copy(
                        dst, p6b[:, 0:512].rearrange("p (j d) -> p j d", j=4)),
                        reads=("p6",), writes=(tokk,))

            mmg([lambda e, Q=Q: [e.matmul(p7[0:16, 0:512],
                                          wass[:, 64 * c + 16 * Q:64 * c + 16 * Q + 16],
                                          xnTq[:, 512 * c:512 * (c + 1)],
                                          start=(c == 0), stop=(c == 7)) for c in range(8)][-1]],
                reads=("wass", "xnTq"), writes=("p7",))
            bld.op("dve", lambda e: e.tensor_copy(lt_s[:], p7[0:16, 0:512]),
                   reads=("p7",), writes=("lt_s",))
            bld.dma("Sc", lambda e: e.dma_start(out=lt_dram, in_=lt_s[:]),
                    reads=("lt_s",), writes=("lt_dram",))
            diag = _ap_custom(lt_dram, [[544, 16], [1, 32]], 0)
            bld.dma("Sc", lambda e, diag=diag: e.dma_start(out=eblk[:], in_=diag),
                    reads=("lt_dram",), writes=("eblk",))
            bld.op("act", lambda e: e.activation(eblk[:], eblk[:], AF.Exp, accum_out=wsum_c[:]),
                   reads=("eblk",), writes=("eblk", "wsum_c"))
            bld.op("dve", lambda e: e.reciprocal(wrec_c[:], wsum_c[:]),
                   reads=("wsum_c",), writes=("wrec_c",))
            bld.op("dve", lambda e: e.tensor_scalar(wblk_bf[:], eblk[:], wrec_c[:], None, op0=ALU.mult),
                   reads=("eblk", "wrec_c"), writes=("wblk_bf",))
            scat = _ap_custom(wbig_dram, [[2049, 16], [64, 32]], 32768 * Q + 16 * Q)
            bld.dma("Sc", lambda e, scat=scat: e.dma_start(
                out=scat, in_=wblk_bf[:], allow_slow_non_contiguous=True),
                reads=("wblk_bf",), writes=("wbig",))
            bld.dma("Sc", lambda e, Q=Q: e.dma_start(
                out=wbig_s[:], in_=_ap_custom(wbig_dram, [[64, 128], [8192, 4], [1, 64]], 32768 * Q)),
                reads=("wbig",), writes=("wbig_s",))

            for (tok, tk, pb, pk, dst, dstk) in [(ktok, "ktok", p01, "p01", ck_s, "ck_s"),
                                                 (vtok, "vtok", p23, "p23", cv_s, "cv_s")]:
                def sf(e, tok=tok, pb=pb):
                    last = None
                    for j in range(4):
                        for nb in range(2):
                            last = e.matmul(pb[0:64, 512 * nb:512 * (nb + 1)],
                                            wbig_s[:, 64 * j:64 * (j + 1)],
                                            tok[:, 1024 * j + 512 * nb:1024 * j + 512 * (nb + 1)],
                                            start=(j == 0), stop=(j == 3))
                    return last
                mmg([sf], reads=("wbig_s", tk), writes=(pk,))
                bld.op("dve", lambda e, pb=pb, dst=dst: e.tensor_tensor(
                    dst[:], pb[0:64, 0:1024], dst[:], op=ALU.add),
                    reads=(pk, dstk), writes=(dstk,))

            evn = lambda ap: ap.rearrange("p (a two) -> p two a", two=2)[:, 0:1, :]
            odd = lambda ap: ap.rearrange("p (a two) -> p two a", two=2)[:, 1:2, :]
            bld.op("dve", lambda e: e.tensor_tensor(evn(kcr_cm[:]), odd(ck_s[:]), evn(tkc[:]), op=ALU.mult),
                   reads=("ck_s", "tkc"), writes=("kcr_cm",))
            bld.op("dve", lambda e: e.tensor_tensor(odd(kcr_cm[:]), evn(ck_s[:]), odd(tkc[:]), op=ALU.mult),
                   reads=("ck_s", "tkc"), writes=("kcr_cm",))
            bld.op("dve", lambda e: e.tensor_copy(cv_bf[:], cv_s[:]),
                   reads=("cv_s",), writes=("cv_bf",))

            def ktrs(e):
                last = None
                for h in range(16):
                    hb = 64 * (h % 2)
                    last = e.transpose(
                        p6b[hb:hb + 64, 56 * (h // 2) + 448 * (h % 2):56 * (h // 2) + 448 * (h % 2) + 56],
                        kcr_cm[0:56, 64 * h:64 * (h + 1)],
                        idb[0:56, 0:56], tile_position=(0, hb))
                return last
            bld.op("pe", ktrs, reads=("kcr_cm", "idb"), writes=("p6",))
            bld.op("act", lambda e: e.copy(kcrT[0:64, 0:448], p6b[0:64, 0:448]),
                   reads=("p6",), writes=("kcrT",))
            bld.op("act", lambda e: e.copy(kcrT[64:128, 0:448], p6b[64:128, 448:896]),
                   reads=("p6",), writes=("kcrT",))

            bld.claim("yT", ["xnTq"])
            iters = [(gl, hp, hh, qc) for gl in range(2) for hp in range(8)
                     for hh in range(2) for qc in range(2)]

            def front(i):
                gl, hp, hh, qc = iters[i]
                ncl_g = NCPG * (2 * Q + gl)
                hb = 64 * hh
                pp = i % 2
                qend = 128 * (qc + 1)
                ncols = ncl_g + qend
                spb, spk = (p4, "p4") if pp == 0 else (p5, "p5")
                p_sm, ssum1, srec1 = p_sm_b[pp], ssum_b[pp], srec_b[pp]
                psk, ssk, srk = "p_sm%d" % pp, "ssum%d" % pp, "srec%d" % pp

                def sc_f(e, hp=hp, hb=hb, gl=gl, qc=qc, ncl_g=ncl_g, qend=qend, spb=spb):
                    last = None
                    if ncl_g > 0:
                        last = e.matmul(
                            spb[:, 0:ncl_g],
                            qrT[hb:hb + 64, 512 * hp + 256 * gl + 128 * qc:512 * hp + 256 * gl + 128 * (qc + 1)],
                            kcrT[hb:hb + 64, 56 * hp:56 * hp + ncl_g],
                            start=True, stop=True, tile_position=(hb, 0))
                    last = e.matmul(
                        spb[:, ncl_g:ncl_g + qend],
                        qrT[hb:hb + 64, 512 * hp + 256 * gl + 128 * qc:512 * hp + 256 * gl + 128 * (qc + 1)],
                        klrT[hb:hb + 64, 512 * hp + 256 * gl:512 * hp + 256 * gl + qend],
                        start=True, stop=True, tile_position=(hb, 0))
                    return last
                mmg([sc_f], reads=("qrT", "kcrT", "klrT"), writes=(spk,))
                bld.op("dve", lambda e, spb=spb, ncl_g=ncl_g, qc=qc: e.tensor_tensor(
                    spb[:, ncl_g + 128 * qc:ncl_g + 128 * (qc + 1)],
                    spb[:, ncl_g + 128 * qc:ncl_g + 128 * (qc + 1)],
                    tri[:], op=ALU.add),
                    reads=(spk, "tri"), writes=(spk,))
                bld.op("act", lambda e, spb=spb, ncols=ncols, p_sm=p_sm, ssum1=ssum1: e.activation(
                    p_sm[:, 0:ncols], spb[:, 0:ncols], AF.Exp, accum_out=ssum1[:]),
                    reads=(spk,), writes=(psk, ssk))
                bld.op("dve", lambda e, srec1=srec1, ssum1=ssum1: e.reciprocal(srec1[:], ssum1[:]),
                       reads=(ssk,), writes=(srk,))
                bld.op("dve", lambda e, ncols=ncols, p_sm=p_sm, srec1=srec1: e.tensor_scalar(
                    p_sm[:, 0:ncols], p_sm[:, 0:ncols], srec1[:], None, op0=ALU.mult),
                    reads=(psk, srk), writes=(psk,))

            def back(i):
                gl, hp, hh, qc = iters[i]
                ncl_g = NCPG * (2 * Q + gl)
                h = 2 * hp + hh
                hb = 64 * hh
                pp = i % 2
                qend = 128 * (qc + 1)
                yc0 = 256 * (hp % 2)
                p_sm, pt_sm = p_sm_b[pp], pt_sm_b[pp]
                psk, ptk = "p_sm%d" % pp, "pt_sm%d" % pp
                ptp, ptpk = (p6b, "p6") if pp == 0 else (p01b, "p01")

                def pt_f(e, ncl_g=ncl_g, qc=qc, p_sm=p_sm, ptp=ptp):
                    last = None
                    if ncl_g > 0:
                        last = e.transpose(ptp[0:ncl_g, 0:128], p_sm[:, 0:ncl_g], idb[:])
                    for kc in range(qc + 1):
                        last = e.transpose(ptp[:, 128 * (kc + 1):128 * (kc + 2)],
                                           p_sm[:, ncl_g + 128 * kc:ncl_g + 128 * (kc + 1)],
                                           idb[:])
                    return last
                bld.op("pe", pt_f, reads=(psk, "idb"), writes=(ptpk,))
                if ncl_g > 0:
                    bld.op("act", lambda e, ncl_g=ncl_g, pt_sm=pt_sm, ptp=ptp: e.copy(
                        pt_sm[0:ncl_g, 0:128], ptp[0:ncl_g, 0:128]),
                        reads=(ptpk,), writes=(ptk,))
                bld.op("act", lambda e, qc=qc, pt_sm=pt_sm, ptp=ptp: e.copy(
                    pt_sm[:, 128:128 * (qc + 2)], ptp[:, 128:128 * (qc + 2)]),
                    reads=(ptpk,), writes=(ptk,))

                def pv_f(e, h=h, hb=hb, gl=gl, qc=qc, ncl_g=ncl_g, yc0=yc0, pt_sm=pt_sm):
                    last = None
                    first = True
                    yc = yc0 + 128 * qc
                    if ncl_g > 0:
                        last = e.matmul(p7[hb:hb + 64, yc:yc + 128],
                                        cv_bf[0:ncl_g, 64 * h:64 * (h + 1)],
                                        pt_sm[0:ncl_g, 0:128],
                                        start=True, stop=False, tile_position=(0, hb))
                        first = False
                    for kc in range(qc + 1):
                        j = 2 * gl + kc
                        last = e.matmul(p7[hb:hb + 64, yc:yc + 128],
                                        vtok[:, 1024 * j + 64 * h:1024 * j + 64 * (h + 1)],
                                        pt_sm[:, 128 * (kc + 1):128 * (kc + 2)],
                                        start=first, stop=(kc == qc),
                                        tile_position=(0, hb))
                        first = False
                    return last
                mmg([pv_f], reads=(ptk, "cv_bf", "vtok"), writes=("p7",))
                if hh == 1 and qc == 1:
                    bld.op("dve", lambda e, hp=hp, gl=gl, yc0=yc0: e.tensor_copy(
                        yT[:, 512 * hp + 256 * gl:512 * hp + 256 * (gl + 1)], p7[:, yc0:yc0 + 256]),
                        reads=("p7",), writes=("yT",))

            front(0)
            for i in range(1, len(iters)):
                front(i)
                back(i - 1)
            back(len(iters) - 1)

            bld.claim("wpT", ["klrT", "ktok"])
            wpT_s = zb(A_wpT, 8192)
            bld.dma("Wg", lambda e: e.dma_start(
                out=zb(A_wpT, 8192), in_=_ap_custom(di["wpaT"], [[1024, 128], [131072, 8], [1, 1024]], 0)),
                writes=("wpT",))
            for qb in range(4):
                i = 4 * Q + qb

                def opf(e, qb=qb, wpT_s=wpT_s):
                    last = None
                    for nb in range(2):
                        for c in range(8):
                            last = e.matmul(p01[:, 512 * nb:512 * (nb + 1)],
                                            yT[:, 512 * c + 128 * qb:512 * c + 128 * (qb + 1)],
                                            wpT_s[:, 1024 * c + 512 * nb:1024 * c + 512 * nb + 512],
                                            start=(c == 0), stop=False)
                        last = e.matmul(p01[:, 512 * nb:512 * (nb + 1)], onesb[:],
                                        bprow[0:1, 512 * nb:512 * (nb + 1)],
                                        start=False, stop=True)
                    return last
                mmg([opf], reads=("yT", "wpT", "onesb", "bprow"), writes=("p01",))
                bld.op("dve", lambda e, i=i: e.tensor_tensor(
                    r_s[:, 1024 * i:1024 * (i + 1)], p01[:, 0:1024],
                    r_s[:, 1024 * i:1024 * (i + 1)], op=ALU.add),
                    reads=("p01", "r%d" % i), writes=("r%d" % i,))

    ARENA_F = ["weT0", "weT1", "xnT"]
    ARENA_B = ["svT", "wq0", "wq1", "qTb", "kTb", "P1", "P3", "vtokb",
               "vslot_b", "ptslot", "wpb", "xnT"]

    def ffn(p):
        A_weT = [0, 16384]
        A_xnT = 32768
        prev = ARENA_A if p == 0 else ARENA_B
        for k in ARENA_F:
            bld.claim(k, prev)
        xnT = zb(A_xnT, 16384)

        def xn_dst(i):
            return zb(A_xnT, 16384).rearrange("p (c t) -> p c t", c=8)[:, :, 128 * i:128 * (i + 1)]
        ln_pass(list(range(16)), xn_dst, ("xnT",), lambda i: "r%d" % i)

        def rf(e):
            last = None
            for i in range(16):
                for c in range(8):
                    last = e.matmul(p4[:, 8 * i:8 * (i + 1)],
                                    xnT[:, 2048 * c + 128 * i:2048 * c + 128 * (i + 1)],
                                    rwt[:, 8 * c:8 * (c + 1)], start=(c == 0), stop=False)
                last = e.matmul(p4[:, 8 * i:8 * (i + 1)], onesb[:], rbrow[:],
                                start=False, stop=True)
            return last
        mmg([rf], reads=("xnT", "rwt", "onesb", "rbrow"), writes=("p4",))
        bld.op("dve", lambda e: e.tensor_copy(logit_s[:], p4[:, 0:128]),
               reads=("p4",), writes=("logit_s",))
        bld.op("dve", lambda e: e.tensor_reduce(
            lmax16[:].rearrange("p (a b) -> p a b", b=1),
            logit_s[:].rearrange("p (a b) -> p a b", b=8),
            axis=AX.X, op=ALU.max, opt_input=False, opt_output=False),
            reads=("logit_s",), writes=("lmax16",))
        bld.op("act", lambda e: e.activation(er_s[:], logit_s[:], AF.Exp),
               reads=("logit_s",), writes=("er_s",))
        bld.op("act", lambda e: e.activation(elmax16[:], lmax16[:], AF.Exp),
               reads=("lmax16",), writes=("elmax16",))
        bld.op("dve", lambda e: e.tensor_reduce(
            esum16[:].rearrange("p (a b) -> p a b", b=1),
            er_s[:].rearrange("p (a b) -> p a b", b=8),
            axis=AX.X, op=ALU.add, opt_input=False, opt_output=False),
            reads=("er_s",), writes=("esum16",))
        bld.op("dve", lambda e: e.reciprocal(esum16[:], esum16[:]),
               reads=("esum16",), writes=("esum16",))
        bld.op("dve", lambda e: e.tensor_tensor(w16[:], elmax16[:], esum16[:], op=ALU.mult),
               reads=("elmax16", "esum16"), writes=("w16",))
        for i in range(16):
            bld.op("dve", lambda e, i=i: e.tensor_scalar(
                sel_s[:, 8 * i:8 * (i + 1)], logit_s[:, 8 * i:8 * (i + 1)],
                lmax16[:, i:i + 1], w16[:, i:i + 1], op0=ALU.is_equal, op1=ALU.mult),
                reads=("logit_s", "lmax16", "w16"), writes=("sel_s",))

        if dbg and p == 0:
            bld.dma("St", lambda e: e.dma_start(out=dbg_d["logit"], in_=logit_s[:]), reads=("logit_s",))
            bld.dma("St", lambda e: e.dma_start(out=dbg_d["sel"], in_=sel_s[:]), reads=("sel_s",))
            bld.dma("St", lambda e: e.dma_start(out=dbg_d["lmax"], in_=lmax16[:]), reads=("lmax16",))
            bld.dma("St", lambda e: e.dma_start(out=dbg_d["w16d"], in_=w16[:]), reads=("w16",))
            dxn = nc.dram_tensor("dbg_xnb", [128, 16384], BF16, kind="ExternalOutput").ap()
            dbg_d["xnTb"] = dxn
            bld.dma("St", lambda e: e.dma_start(out=dxn, in_=xnT), reads=("xnT",))
        for e_i in range(NE):
            buf = e_i % 2
            weT = zb(A_weT[buf], 8192)
            bld.dma("Wg", lambda e, e_i=e_i, buf=buf: e.dma_start(
                out=zb(A_weT[buf], 8192),
                in_=_ap_custom(di["weT"][e_i], [[1024, 128], [131072, 8], [1, 1024]], 0)),
                writes=("weT%d" % buf,))
            bld.dma("Wg", lambda e, e_i=e_i, buf=buf: e.dma_start(
                out=ebrow[0:1, 1024 * buf:1024 * (buf + 1)], in_=di["eb"][e_i].rearrange("(o d) -> o d", o=1)),
                writes=("ebrow%d" % buf,))
            for i in range(16):
                pb, pk = (p01, "p01") if i % 2 == 0 else (p23, "p23")

                def ef(e, i=i, pb=pb, weT=weT, buf=buf):
                    last = None
                    for nb in range(2):
                        for c in range(8):
                            last = e.matmul(pb[:, 512 * nb:512 * (nb + 1)],
                                            xnT[:, 2048 * c + 128 * i:2048 * c + 128 * (i + 1)],
                                            weT[:, 1024 * c + 512 * nb:1024 * c + 512 * nb + 512],
                                            start=(c == 0), stop=False)
                        last = e.matmul(pb[:, 512 * nb:512 * (nb + 1)], onesb[:],
                                        ebrow[0:1, 1024 * buf + 512 * nb:1024 * buf + 512 * (nb + 1)],
                                        start=False, stop=True)
                    return last
                mmg([ef], reads=("xnT", "weT%d" % buf, "onesb", "ebrow%d" % buf), writes=(pk,))
                bld.op("dve", lambda e, i=i, e_i=e_i, pb=pb: e.scalar_tensor_tensor(
                    r_s[:, 1024 * i:1024 * (i + 1)], pb[:, 0:1024],
                    sel_s[:, 8 * i + e_i:8 * i + e_i + 1],
                    r_s[:, 1024 * i:1024 * (i + 1)], op0=ALU.mult, op1=ALU.add),
                    reads=(pk, "sel_s", "r%d" % i), writes=("r%d" % i,))

    def attn_b():
        for k in ARENA_B:
            bld.claim(k, ARENA_F)
        A_xnT = 32768
        A_svT = 0
        A_wq = [65536, 71680]
        A_qT = 77824
        A_kT = 81920
        A_P1 = 86016
        A_P3 = 90112
        A_vtokb = 94208
        A_vslot = 98304
        A_ptslot = 99328
        xnT = zb(A_xnT, 16384)
        svT = zb(A_svT, 16384)

        def xn_dst(i):
            return zb(A_xnT, 16384).rearrange("p (c t) -> p c t", c=8)[:, :, 128 * i:128 * (i + 1)]
        ln_pass(list(range(16)), xn_dst, ("xnT",), lambda i: "r%d" % i)

        for hp in range(b_hp):
            buf = hp % 2
            wq = zb(A_wq[buf], 3072)
            qT = zb(A_qT, 2048)
            kT = zb(A_kT, 2048)
            P1 = zb(A_P1, 2048)
            P3 = zb(A_P3, 2048)
            vtokb = zb(A_vtokb, 2048)
            vslot = zb(A_vslot, 512)
            ptslot = zb(A_ptslot, 512)
            for kind in range(3):
                src = _ap_custom(di["wqkv"], [[3072, 128], [393216, 8], [1, 128]],
                                 1024 * kind + 128 * hp)
                bld.dma("Wg", lambda e, buf=buf, src=src, kind=kind: e.dma_start(
                    out=zb(A_wq[buf], 3072).rearrange("p (c f) -> p c f", f=384)[:, :, 128 * kind:128 * (kind + 1)],
                    in_=src), writes=("wq%d" % buf,))

            for kind in range(3):
                for nb in range(4):
                    pb, pk = (p4, "p4") if nb % 2 == 0 else (p5, "p5")

                    def qf(e, kind=kind, nb=nb, pb=pb, wq=wq):
                        last = None
                        for c in range(8):
                            last = e.matmul(pb[:, 0:512],
                                            wq[:, 384 * c + 128 * kind:384 * c + 128 * (kind + 1)],
                                            xnT[:, 2048 * c + 512 * nb:2048 * c + 512 * (nb + 1)],
                                            start=(c == 0), stop=(c == 7))
                        return last
                    mmg([qf], reads=("wq%d" % buf, "xnT"), writes=(pk,))
                    if kind == 0:
                        bld.op("act", lambda e, nb=nb, pb=pb: e.activation(
                            qT[:, 512 * nb:512 * (nb + 1)], pb[:, 0:512], AF.Copy, scale=qsc_t[:]),
                            reads=(pk, "qsc"), writes=("qTb",))
                    elif kind == 1:
                        bld.op("act", lambda e, nb=nb, pb=pb: e.copy(
                            kT[:, 512 * nb:512 * (nb + 1)], pb[:, 0:512]),
                            reads=(pk,), writes=("kTb",))
                    else:
                        bld.op("act", lambda e, nb=nb, pb=pb: e.copy(
                            vslot[:], pb[:, 0:512]), reads=(pk,), writes=("vslot_b",))

                        def vtr(e, nb=nb):
                            last = None
                            for jj in range(4):
                                last = e.transpose(p6b[:, 128 * jj:128 * (jj + 1)],
                                                   vslot[:, 128 * jj:128 * (jj + 1)], idb[:])
                            return last
                        bld.op("pe", vtr, reads=("vslot_b", "idb"), writes=("p6",))
                        bld.op("act", lambda e, nb=nb: e.copy(
                            vtokb[:, 512 * nb:512 * (nb + 1)], p6b[:, 0:512]),
                            reads=("p6",), writes=("vtokb",))

            if b_stage < 1:
                continue
            for (srcT, sk, dst, dk) in [(qT, "qTb", qlmT, "qlmT"), (kT, "kTb", klmT, "klmT")]:
                bld.op("dve", lambda e, srcT=srcT: e.tensor_reduce(
                    lm_f[:].rearrange("p (a b) -> p a b", b=1),
                    srcT.rearrange("p (a b) -> p a b", b=32),
                    axis=AX.X, op=ALU.add, opt_input=False, opt_output=False),
                    reads=(sk,), writes=("lm_f",))
                bld.op("dve", lambda e, dst=dst: e.tensor_scalar(
                    dst[:], lm_f[:], 1.0 / 32, None, op0=ALU.mult),
                    reads=("lm_f",), writes=(dk,))

            if b_stage < 2:
                continue
            for j4 in range(4):
                def k1f(e, j4=j4, hh=0, pb=p7):
                    last = None
                    for jj in range(4):
                        j = 4 * j4 + jj
                        last = e.matmul(pb[:, 64 * jj:64 * (jj + 1)],
                                        qT[64 * hh:64 * (hh + 1), 128 * j:128 * (j + 1)],
                                        klmT[64 * hh:64 * (hh + 1), :],
                                        start=True, stop=True, tile_position=(64 * hh, 0))
                    return last
                mmg([lambda e, j4=j4: k1f(e, j4, 0, p7)], reads=("qTb", "klmT"), writes=("p7",))
                mmg([lambda e, j4=j4: k1f(e, j4, 1, p6)], reads=("qTb", "klmT"), writes=("p6",))
                for hh, pb, pk in [(0, p7, "p7"), (1, p6, "p6")]:
                    dst = P1[:, 512 * j4:512 * (j4 + 1)].rearrange(
                        "p (j h m) -> p j h m", h=2, m=64)[:, :, hh, :]
                    bld.op("act", lambda e, pb=pb, dst=dst: e.activation(
                        dst, pb[:, 0:256].rearrange("p (j m) -> p j m", m=64), AF.Exp),
                        reads=(pk,), writes=("P1",))
                bld.op("dve", lambda e, j4=j4: e.tensor_reduce(
                    rsum32[:, 8 * j4:8 * (j4 + 1)].rearrange("p (a b) -> p a b", b=1),
                    P1[:, 512 * j4:512 * (j4 + 1)].rearrange("p (a b) -> p a b", b=64),
                    axis=AX.X, op=ALU.add, opt_input=False, opt_output=False),
                    reads=("P1",), writes=("rsum32",))
            bld.op("dve", lambda e: e.reciprocal(rrec32[:], rsum32[:]),
                   reads=("rsum32",), writes=("rrec32",))
            for j in range(16 if b_stage != 102 else 0):
                for hh in range(2):
                    bld.op("dve", lambda e, j=j, hh=hh: e.tensor_scalar(
                        P1[:, 128 * j + 64 * hh:128 * j + 64 * (hh + 1)],
                        P1[:, 128 * j + 64 * hh:128 * j + 64 * (hh + 1)],
                        rrec32[:, 2 * j + hh:2 * j + hh + 1], None, op0=ALU.mult),
                        reads=("P1", "rrec32"), writes=("P1",))

            if b_stage < 3:
                continue
            for half in range(2):
                for hh, pb, pk in [(0, p01, "p01"), (1, p23, "p23")]:
                    def k3f(e, half=half, hh=hh, pb=pb):
                        last = None
                        for nb in range(2):
                            last = e.matmul(pb[64 * hh:64 * (hh + 1), 512 * nb:512 * (nb + 1)],
                                            qlmT[64 * hh:64 * (hh + 1), :],
                                            kT[64 * hh:64 * (hh + 1), 1024 * half + 512 * nb:1024 * half + 512 * (nb + 1)],
                                            start=True, stop=True, tile_position=(64 * hh, 64 * hh))
                        return last
                    mmg([k3f], reads=("qlmT", "kTb"), writes=(pk,))
                    bld.op("act", lambda e, half=half, hh=hh, pb=pb: e.activation(
                        P3[64 * hh:64 * (hh + 1), 1024 * half:1024 * (half + 1)],
                        pb[64 * hh:64 * (hh + 1), 0:1024], AF.Exp,
                        accum_out=s3sum[64 * hh:64 * (hh + 1), half:half + 1]),
                        reads=(pk,), writes=("P3", "s3sum"))
            bld.op("dve", lambda e: e.tensor_tensor(s3tot[:], s3sum[:, 0:1], s3sum[:, 1:2], op=ALU.add),
                   reads=("s3sum",), writes=("s3tot",))
            bld.op("dve", lambda e: e.reciprocal(s3rec[:], s3tot[:]),
                   reads=("s3tot",), writes=("s3rec",))
            bld.op("dve", lambda e: e.tensor_scalar(P3[:], P3[:], s3rec[:], None, op0=ALU.mult),
                   reads=("P3", "s3rec"), writes=("P3",))

            if b_stage < 4:
                continue
            for j4 in range(4):
                def p3t(e, j4=j4):
                    last = None
                    for jj in range(4):
                        last = e.transpose(p6b[:, 128 * jj:128 * (jj + 1)],
                                           P3[:, 128 * (4 * j4 + jj):128 * (4 * j4 + jj + 1)], idb[:])
                    return last
                bld.op("pe", p3t, reads=("P3", "idb"), writes=("p6",))
                bld.op("act", lambda e: e.copy(ptslot[:], p6b[:, 0:512]),
                       reads=("p6",), writes=("ptslot",))

                def k3vf(e, j4=j4):
                    last = None
                    for jj in range(4):
                        j = 4 * j4 + jj
                        for hh in range(2):
                            last = e.matmul(p23[0:64, 64 * hh:64 * (hh + 1)],
                                            ptslot[:, 128 * jj + 64 * hh:128 * jj + 64 * (hh + 1)],
                                            vtokb[:, 128 * j + 64 * hh:128 * j + 64 * (hh + 1)],
                                            start=(j == 0 and hh == 0), stop=(j == 15 and hh == 1),
                                            tile_position=(0, 0))
                    return last
                mmg([k3vf], reads=("ptslot", "vtokb"), writes=("p23",))
            bld.op("dve", lambda e: e.tensor_copy(k3vc[:], p23[0:64, 0:128]),
                   reads=("p23",), writes=("i_k3v",))

            if b_stage < 5:
                continue
            mmg([lambda e: e.matmul(p4[0:64, 0:64], qlmT[0:64, :], klmT[0:64, :],
                                    start=True, stop=True, tile_position=(0, 0))],
                reads=("qlmT", "klmT"), writes=("p4",))
            mmg([lambda e: e.matmul(p5[0:64, 0:64], qlmT[64:128, :], klmT[64:128, :],
                                    start=True, stop=True, tile_position=(64, 0))],
                reads=("qlmT", "klmT"), writes=("p5",))
            for hh, pb, pk in [(0, p4, "p4"), (1, p5, "p5")]:
                bld.op("act", lambda e, hh=hh, pb=pb: e.activation(
                    er_s[0:64, 64 * hh:64 * (hh + 1)], pb[0:64, 0:64], AF.Exp,
                    accum_out=k2sum[0:64, hh:hh + 1]),
                    reads=(pk,), writes=("er_s", "k2sum"))
            bld.op("dve", lambda e: e.reciprocal(k2rec[0:64, 0:2], k2sum[0:64, 0:2]),
                   reads=("k2sum",), writes=("k2rec",))
            for hh in range(2):
                bld.op("dve", lambda e, hh=hh: e.tensor_scalar(
                    K2c[:, 64 * hh:64 * (hh + 1)], er_s[0:64, 64 * hh:64 * (hh + 1)],
                    k2rec[0:64, hh:hh + 1], None, op0=ALU.mult),
                    reads=("er_s", "k2rec"), writes=("i_K2",))

            if b_stage < 6:
                continue
            mmg([lambda e: [e.matmul(p4[0:1, 64 + 64 * hh:128 + 64 * hh],
                                     onesc[0:64, :], K2c[:, 64 * hh:64 * (hh + 1)],
                                     start=True, stop=True, tile_position=(0, 0))
                            for hh in range(2)][-1]],
                reads=("i_K2", "onesc"), writes=("p4",))
            bld.op("dve", lambda e: e.tensor_copy(denom_s[:], p4[0:1, 64:192]),
                   reads=("p4",), writes=("denom",))
            bld.op("dve", lambda e: e.tensor_reduce(
                dmax_s[:].rearrange("p (a b) -> p a b", b=1),
                denom_s[:].rearrange("p (a b) -> p a b", b=64),
                axis=AX.X, op=ALU.max, opt_input=False, opt_output=False),
                reads=("denom",), writes=("dmax",))
            bld.op("dve", lambda e: e.reciprocal(drec_s[:], dmax_s[:]),
                   reads=("dmax",), writes=("drec",))
            mmg([lambda e: e.matmul(p4[:, 192:194], ones1f[:], drec_s[:], start=True, stop=True)],
                reads=("drec", "ones1f"), writes=("p4",))
            bld.op("dve", lambda e: e.tensor_copy(dbc_s[:], p4[:, 192:194]),
                   reads=("p4",), writes=("dbc",))

            if b_stage < 7:
                continue
            bld.op("pe", lambda e: e.transpose(p6b[0:64, 0:64], K2c[:, 0:64], idb[0:64, 0:64],
                                               tile_position=(0, 0)),
                   reads=("i_K2", "idb"), writes=("p6",))
            bld.op("act", lambda e: e.copy(K2Tc[:, 0:64], p6b[0:64, 0:64]),
                   reads=("p6",), writes=("i_K2T",))
            bld.op("pe", lambda e: e.transpose(p7b[0:64, 0:64], K2c[:, 64:128], idb[0:64, 0:64],
                                               tile_position=(0, 0)),
                   reads=("i_K2", "idb"), writes=("p7",))
            bld.op("act", lambda e: e.copy(K2Tc[:, 64:128], p7b[0:64, 0:64]),
                   reads=("p7",), writes=("i_K2T",))
            for hh in range(2):
                bld.op("dve", lambda e, hh=hh: e.tensor_scalar(
                    Vc[:, 64 * hh:64 * (hh + 1)], K2Tc[:, 64 * hh:64 * (hh + 1)],
                    dbc_s[0:64, hh:hh + 1], None, op0=ALU.mult),
                    reads=("i_K2T", "dbc"), writes=("i_V",))
                bld.op("dve", lambda e, hh=hh: e.tensor_scalar(
                    Vtc[:, 64 * hh:64 * (hh + 1)], K2c[:, 64 * hh:64 * (hh + 1)],
                    dbc_s[0:64, hh:hh + 1], None, op0=ALU.mult),
                    reads=("i_K2", "dbc"), writes=("i_Vt",))

            if b_stage < 8:
                continue

            def pair0(dst_p, c0, lhs, rhs, lk, rk, pk):
                def f(e, dst_p=dst_p, c0=c0, lhs=lhs, rhs=rhs):
                    last = None
                    for hh in range(2):
                        last = e.matmul(dst_p[0:64, c0 + 64 * hh:c0 + 64 * (hh + 1)],
                                        lhs[:, 64 * hh:64 * (hh + 1)],
                                        rhs[:, 64 * hh:64 * (hh + 1)],
                                        start=True, stop=True, tile_position=(0, 0))
                    return last
                mmg([f], reads=(lk, rk), writes=(pk,))

            def evac(dst, c0, dk, pk, pb, scale=None):
                if scale is None:
                    bld.op("dve", lambda e, dst=dst, c0=c0, pb=pb: e.tensor_copy(
                        dst[:], pb[0:64, c0:c0 + 128]), reads=(pk,), writes=(dk,))
                else:
                    bld.op("dve", lambda e, dst=dst, c0=c0, pb=pb: e.tensor_scalar(
                        dst[:], pb[0:64, c0:c0 + 128], scale, None, op0=ALU.mult),
                        reads=(pk,), writes=(dk,))

            pair0(p4, 0, K2Tc, Vc, "i_K2T", "i_V", "p4")
            evac(Ac, 0, "i_A", "p4", p4)
            for it in range(6):
                bld.op("dve", lambda e: e.scalar_tensor_tensor(
                    U1c[:], Ac[:], -1.0, c7b2[:], op0=ALU.mult, op1=ALU.add),
                    reads=("i_A", "c7"), writes=("i_U1",))
                pair0(p4, 0, Ac, U1c, "i_A", "i_U1", "p4")
                evac(U2c, 0, "i_U2", "p4", p4)
                bld.op("dve", lambda e: e.scalar_tensor_tensor(
                    U3c[:], U2c[:], -1.0, c15b2[:], op0=ALU.mult, op1=ALU.add),
                    reads=("i_U2", "c15"), writes=("i_U3",))
                pair0(p4, 128, Ac, U3c, "i_A", "i_U3", "p4")
                evac(U4c, 128, "i_U4", "p4", p4)
                bld.op("dve", lambda e: e.scalar_tensor_tensor(
                    U5c[:], U4c[:], -1.0, c13b2[:], op0=ALU.mult, op1=ALU.add),
                    reads=("i_U4", "c13"), writes=("i_U5",))
                pair0(p4, 0, Vtc, U5c, "i_Vt", "i_U5", "p4")
                pair0(p4, 128, U5c, Vtc, "i_U5", "i_Vt", "p4")
                pair0(p4, 256, Ac, U5c, "i_A", "i_U5", "p4")
                evac(Vc, 0, "i_V", "p4", p4, scale=0.25)
                evac(Vtc, 128, "i_Vt", "p4", p4, scale=0.25)
                evac(Ac, 256, "i_A", "p4", p4, scale=0.25)

            if b_stage < 9:
                continue
            pair0(p4, 0, Vtc, k3vc, "i_Vt", "i_k3v", "p4")
            evac(Zc, 0, "i_Z", "p4", p4)

            if b_stage < 10:
                continue
            for j4 in range(4):
                def p1t(e, j4=j4):
                    last = None
                    for jj in range(4):
                        for hh in range(2):
                            last = e.transpose(
                                p6b[0:64, 256 * jj + 128 * hh:256 * jj + 128 * (hh + 1)],
                                P1[:, 128 * (4 * j4 + jj) + 64 * hh:128 * (4 * j4 + jj) + 64 * (hh + 1)],
                                idb[:], tile_position=(0, 0))
                    return last
                bld.op("pe", p1t, reads=("P1", "idb"), writes=("p6",))
                ptsv = zb(A_vslot, 1024)
                bld.op("act", lambda e, ptsv=ptsv: e.copy(ptsv[0:64, 0:1024], p6b[0:64, 0:1024]),
                       reads=("p6",), writes=("ptslot", "vslot_b"))

                def svf(e, j4=j4, ptsv=ptsv):
                    last = None
                    for jj in range(4):
                        for hh in range(2):
                            last = e.matmul(p23[64 * hh:64 * (hh + 1), 512 * hh + 128 * jj:512 * hh + 128 * (jj + 1)],
                                            Zc[:, 64 * hh:64 * (hh + 1)],
                                            ptsv[0:64, 256 * jj + 128 * hh:256 * jj + 128 * (hh + 1)],
                                            start=True, stop=True, tile_position=(0, 64 * hh))
                    return last
                mmg([svf], reads=("i_Z", "ptslot"), writes=("p23",))
                for hh in range(2):
                    bld.op("dve", lambda e, j4=j4, hp=hp, hh=hh: e.tensor_copy(
                        svT[64 * hh:64 * (hh + 1), 2048 * hp + 512 * j4:2048 * hp + 512 * (j4 + 1)],
                        p23[64 * hh:64 * (hh + 1), 512 * hh:512 * hh + 512]),
                        reads=("p23",), writes=("svT",))

        if b_stage < 11:
            return
        A_wpb = 77824
        wpb_s = zb(A_wpb, 8192)
        bld.claim("wpb", ["qTb", "kTb", "P1", "P3"])
        bld.dma("Wg", lambda e: e.dma_start(
            out=zb(A_wpb, 8192), in_=_ap_custom(di["wpb"], [[1024, 128], [131072, 8], [1, 1024]], 0)),
            writes=("wpb",))
        for i in range(16):
            pb, pk = (p01, "p01") if i % 2 == 0 else (p23, "p23")

            def bpf(e, i=i, pb=pb):
                last = None
                for nb in range(2):
                    for c in range(8):
                        last = e.matmul(pb[:, 512 * nb:512 * (nb + 1)],
                                        svT[:, 2048 * c + 128 * i:2048 * c + 128 * (i + 1)],
                                        wpb_s[:, 1024 * c + 512 * nb:1024 * c + 512 * nb + 512],
                                        start=(c == 0), stop=(c == 7))
                return last
            mmg([bpf], reads=("svT", "wpb"), writes=(pk,))
            bld.op("dve", lambda e, i=i, pb=pb: e.tensor_tensor(
                r_s[:, 1024 * i:1024 * (i + 1)], pb[:, 0:1024],
                r_s[:, 1024 * i:1024 * (i + 1)], op=ALU.add),
                reads=(pk, "r%d" % i), writes=("r%d" % i,))

    def tap(name):
        if name in tap_d:
            bld.dma("St", lambda e: e.dma_start(
                out=_ap_custom(tap_d[name], [[1024, 128], [131072, 16], [1, 1024]], 0), in_=r_s[:]),
                reads=tuple("r%d" % i for i in range(16)))

    loads()
    if 'attn_a' in phases:
        attn_a()
        tap("attn_a")
    if 'ffn0' in phases:
        ffn(0)
        tap("ffn1")
    if 'attn_b' in phases:
        attn_b()
        tap("attn_b")
    if 'ffn1' in phases:
        ffn(1)
    bld.dma("St", lambda e: e.dma_start(
        out=_ap_custom(out_d, [[1024, 128], [131072, 16], [1, 1024]], 0), in_=r_s[:]),
        reads=tuple("r%d" % i for i in range(16)))
    bld.finish()
    return nc, stack


def _in_maps(inputs, n_cores=8):
    cons = _host_consts()
    w_attn = np.asarray(inputs["L0_attn_a_w_attn"], np.float32)
    wq = w_attn[0:C].reshape(C // 2, 2, C)[:, ::-1, :].reshape(C, C)
    waT = np.ascontiguousarray(np.concatenate([wq, w_attn[C:]], 0).T)
    b_attn = np.asarray(inputs["L0_attn_a_b_attn"], np.float32).copy()
    b_attn[0:C] = b_attn[0:C].reshape(C // 2, 2)[:, ::-1].reshape(C)
    base = {
        "waT": waT,
        "b_attn": b_attn,
        "wpaT": np.ascontiguousarray(np.asarray(inputs["L0_attn_a_w_proj"], np.float32).T),
        "b_proj": np.asarray(inputs["L0_attn_a_b_proj"], np.float32),
        "wassT": np.ascontiguousarray(np.asarray(inputs["L0_attn_a_w_assign"], np.float32).T),
        "wqkv": np.asarray(inputs["L0_attn_b_w_qkv"], np.float32),
        "wpb": np.asarray(inputs["L0_attn_b_w_proj"], np.float32),
        "rwT": np.ascontiguousarray(np.asarray(inputs["L0_ffn_router_w"], np.float32).T),
        "rb": np.asarray(inputs["L0_ffn_router_b"], np.float32),
        "weT": np.ascontiguousarray(np.asarray(inputs["L0_ffn_expert_w"], np.float32).transpose(0, 2, 1)),
        "eb": np.asarray(inputs["L0_ffn_expert_b"], np.float32),
    }
    base.update(cons)
    x = np.asarray(inputs["L0_attn_a_x"], np.float32)
    maps = []
    for cid in range(n_cores):
        m = dict(base)
        m["x"] = np.ascontiguousarray(x[cid])
        maps.append(m)
    return maps


def kernel(**inputs):
    nc, stack = build()
    try:
        maps = _in_maps(inputs)
        res = run_bass_kernel_spmd(nc, maps, core_ids=list(range(8)))
    finally:
        stack.__exit__(None, None, None)
    return np.stack([np.asarray(r["out"], np.float32) for r in res.results], axis=0)


# version 05
# speedup vs baseline: 1.3702x; 1.0397x over previous
import contextlib
import math

import numpy as np
import ml_dtypes
import bass_rust
import concourse.bass as bass
import concourse.mybir as mybir
from concourse.bass_utils import run_bass_kernel_spmd

F32 = mybir.dt.float32
BF16 = mybir.dt.bfloat16
AF = mybir.ActivationFunctionType
ALU = mybir.AluOpType
AX = mybir.AxisListType
bf16 = ml_dtypes.bfloat16

B, T, C = 8, 2048, 1024
NH, HD = 16, 64
NCL, GS = 64, 256
NTPC = T // NCL
G = T // GS
NCPG = GS // NTPC
CMAX = (G - 1) * NCPG
NLM = 64
NE = 8

NCHUNK = 16
NQ = 4


def _sinusoidal(n, d):
    pos = np.arange(n, dtype=np.float32)[:, None]
    div = np.exp((np.arange(0, d, 2, dtype=np.float32) * np.float32(-math.log(10000.0) / d)).astype(np.float32))
    ang = (pos * div).astype(np.float32)
    return np.stack([np.sin(ang), np.cos(ang)], axis=-1).reshape(n, d).astype(np.float32)


def _host_consts():
    S = _sinusoidal(CMAX + GS, HD)
    sinv = S[:, 0:32].T
    cosv = S[:, 32:64].T
    rotq = np.zeros((128, 512), np.float32)
    rotkl = np.zeros((128, 2048), np.float32)
    scale = np.float32(1.0 / math.sqrt(HD))
    for r in range(128):
        i = (r % 64) // 2
        tq = np.arange(512) % 256
        tkl = 8 * (np.arange(2048) // 256) + (np.arange(2048) % 256)
        if r % 2 == 0:
            rotq[r] = -cosv[i, tq] * scale
            rotkl[r] = -cosv[i, tkl]
        else:
            rotq[r] = sinv[i, tq] * scale
            rotkl[r] = sinv[i, tkl]
    tkc = np.zeros((64, 1024), np.float32)
    for c in range(CMAX):
        v = np.zeros(64, np.float32)
        v[0::2] = -cosv[:, c]
        v[1::2] = sinv[:, c]
        tkc[c] = np.tile(v, NH)
    tri = np.triu(np.full((128, 128), -1e30, np.float32), 1)
    perm = np.zeros((128, 128), np.float32)
    for k in range(128):
        perm[k, k ^ 1] = 1.0
    idn = np.eye(128, dtype=np.float32)
    blk = np.zeros((64, 128), np.float32)
    blk[np.arange(128) % 64, np.arange(128)] = 1.0
    return {
        "ROTQ": rotq.astype(bf16), "ROTKL": rotkl.astype(bf16),
        "TKC": tkc.astype(bf16), "TRI": tri.astype(bf16),
        "PERM": perm.astype(bf16), "IDB": idn.astype(bf16),
        "C7": (7.0 * blk).astype(bf16), "C15": (15.0 * blk).astype(bf16),
        "C13": (13.0 * blk).astype(bf16),
    }


ENG2ATTR = {"pe": "tensor", "dve": "vector", "act": "scalar", "pool": "gpsimd", "sp": "sync"}


class Bld:
    def __init__(self, nc, stack):
        self.nc = nc
        self.stk = stack
        self.q = {e: [] for e in ENG2ATTR}
        self.sem = {}
        self.cnt = {e: 0 for e in ENG2ATTR}
        self.seen = {e: {} for e in ENG2ATTR}
        self.bufs = {}
        self.ch = {}
        for e in ENG2ATTR:
            self.sem[e] = stack.enter_context(nc.semaphore("sem_" + e))
        for name, eng in [("Ld", "sp"), ("Tx", "sp"), ("St", "sp"),
                          ("Wg", "pool"), ("Sc", "pool")]:
            self.sem[name] = stack.enter_context(nc.semaphore("sem_" + name))
            self.ch[name] = {"eng": eng, "issued": 0, "fenced": 0, "maxwait": 0}

    def _collect(self, reads, writes):
        waits = {}

        def add(tok):
            if tok is None:
                return
            s, v = tok
            if waits.get(s, -1) < v:
                waits[s] = v

        for k in reads:
            b = self.bufs.get(k)
            if b:
                add(b["w"])
        for k in writes:
            b = self.bufs.get(k)
            if b:
                add(b["w"])
                for s, v in b["r"].items():
                    add((s, v))
        return waits

    def _wait_tok_for(self, s):
        if s in self.ch:
            c = self.ch[s]
            v = 16 * c["issued"]
            c["maxwait"] = max(c["maxwait"], v)
            return v
        return None

    def _emit_waits(self, eng, waits):
        out = []
        for s, v in waits.items():
            if s in self.ch:
                v2 = self._wait_tok_for(s)
                v = max(v, 0) if v2 is None else v2
            if self.seen[eng].get(s, -1) >= v:
                continue
            self.seen[eng][s] = v
            out.append((self.sem[s], v))
        return out

    def _track(self, reads, writes, tok):
        s, v = tok
        for k in writes:
            self.bufs[k] = {"w": (s, v), "r": {}}
        for k in reads:
            b = self.bufs.setdefault(k, {"w": None, "r": {}})
            if b["r"].get(s, -1) < v:
                b["r"][s] = v

    def op(self, eng, fn, reads=(), writes=()):
        waits = self._collect(reads, writes)
        wl = self._emit_waits(eng, waits)
        self.cnt[eng] += 1
        n = self.cnt[eng]
        sem = self.sem[eng]

        def run(e, wl=wl, fn=fn, sem=sem, n=n):
            for s, v in wl:
                e.wait_ge(s, v)
            inst = fn(e)
            inst.then_inc(sem, 1)
            return inst

        self.q[eng].append(run)
        self._track(reads, writes, (eng, n))

    def dma(self, chname, fn, reads=(), writes=()):
        c = self.ch[chname]
        eng = c["eng"]
        if c["maxwait"] > 16 * c["fenced"]:
            tgt = 16 * c["issued"]
            c["fenced"] = c["issued"]
            sem = self.sem[chname]

            def fence(e, sem=sem, tgt=tgt):
                e.wait_ge(sem, tgt)
            self.q[eng].append(lambda e, f=fence: f(e) or _NOINST)
            self.seen[eng][chname] = tgt
        waits = self._collect(reads, writes)
        wl = self._emit_waits(eng, waits)
        c["issued"] += 1
        v = 16 * c["issued"]
        sem = self.sem[chname]

        def run(e, wl=wl, fn=fn, sem=sem):
            for s, vv in wl:
                e.wait_ge(s, vv)
            inst = fn(e)
            inst.then_inc(sem, 16)
            return inst

        self.q[eng].append(run)
        self._track(reads, writes, (chname, v))

    def claim(self, newkey, oldkeys):
        b = self.bufs.setdefault(newkey, {"w": None, "r": {}})
        for k in oldkeys:
            if k == newkey:
                continue
            ob = self.bufs.get(k)
            if not ob:
                continue
            for tok in ([ob["w"]] if ob["w"] else []) + list(ob["r"].items()):
                s_, v = tok
                if b["r"].get(s_, -1) < v:
                    b["r"][s_] = v

    def finish(self):
        for name, c in self.ch.items():
            if c["issued"] > 0:
                tgt = 16 * c["issued"]
                sem = self.sem[name]
                self.q["sp"].append(lambda e, s=sem, t=tgt: e.wait_ge(s, t) or _NOINST)
        blk = self.stk.enter_context(self.nc.Block())
        for ename, attr in ENG2ATTR.items():
            ops = self.q[ename]
            getattr(blk, attr)(lambda eng, ops=ops: [f(eng) for f in ops] and None)


class _NoInst:
    def then_inc(self, *a):
        return self


_NOINST = _NoInst()


def _ap_custom(base_ap, dims, offset):
    ap = base_ap.copy()
    ap.ap = bass_rust.VecI64Pair(dims)
    ap.offset = base_ap.offset + offset
    return ap


def build(taps=(), dbg=False, phases=('attn_a', 'ffn0', 'attn_b', 'ffn1'), b_hp=8, b_stage=99):
    nc = bass.Bass("TRN2", target_bir_lowering=False)
    stack = contextlib.ExitStack()
    stack.__enter__()
    bld = Bld(nc, stack)
    sb = lambda name, shape, dt: stack.enter_context(nc.sbuf_tensor(name, shape, dt))
    ps = lambda name, shape, dt: stack.enter_context(nc.psum_tensor(name, shape, dt))

    di = {}
    ins = [
        ("x", [T, C]), ("waT", [C, 3 * C]), ("b_attn", [3 * C]),
        ("wpaT", [C, C]), ("b_proj", [C]), ("wassT", [C, NCL]),
        ("wqkv", [C, 3 * C]), ("wpb", [C, C]),
        ("rwT", [C, NE]), ("rb", [NE]), ("weT", [NE, C, C]), ("eb", [NE, C]),
    ]
    bf_ins = [("ROTQ", [128, 512]), ("ROTKL", [128, 2048]), ("TKC", [64, 1024]),
              ("TRI", [128, 128]), ("PERM", [128, 128]), ("IDB", [128, 128]),
              ("C7", [64, 128]), ("C15", [64, 128]), ("C13", [64, 128])]
    for n, shape in ins:
        di[n] = nc.dram_tensor(n, shape, F32, kind="ExternalInput").ap()
    for n, shape in bf_ins:
        di[n] = nc.dram_tensor(n, shape, BF16, kind="ExternalInput").ap()
    out_d = nc.dram_tensor("out", [T, C], F32, kind="ExternalOutput").ap()
    tap_d = {}
    for tname in taps:
        tap_d[tname] = nc.dram_tensor("tap_" + tname, [T, C], F32, kind="ExternalOutput").ap()
    dbg_d = {}
    if dbg:
        for n, shape in [("logit", [128, 128]), ("sel", [128, 128]), ("lmax", [128, 16]),
                         ("w16d", [128, 16]), ("xnTd", [128, 16384]), ("y0", [128, 1024])]:
            dbg_d[n] = nc.dram_tensor("dbg_" + n, shape, F32, kind="ExternalOutput").ap()
    lt_dram = nc.dram_tensor("lt_dram", [16, 512], F32, kind="Internal").ap()
    wbig_dram = nc.dram_tensor("wbig_dram", [T, NCL], BF16, kind="Internal").ap()

    r_s = sb("r_s", [128, 16 * 1024], F32)
    rotq = sb("rotq", [128, 512], BF16)
    rotkl = sb("rotkl", [128, 2048], BF16)
    tkc = sb("tkc_s", [64, 1024], BF16)
    tri = sb("tri_s", [128, 128], BF16)
    perm = sb("perm_s", [128, 128], BF16)
    idb = sb("idb_s", [128, 128], BF16)
    c7 = sb("c7_s", [64, 128], BF16)
    c15 = sb("c15_s", [64, 128], BF16)
    c13 = sb("c13_s", [64, 128], BF16)
    onesb = sb("onesb", [1, 128], BF16)
    ones1f = sb("ones1f", [1, 128], F32)
    onesc = sb("onesc", [128, 1], BF16)
    eps_t = sb("eps_t", [128, 1], F32)
    qsc_t = sb("qsc_t", [128, 1], F32)
    sum16 = sb("sum16", [128, 16], F32)
    ssq16 = sb("ssq16", [128, 16], F32)
    mean16 = sb("mean16", [128, 16], F32)
    var16 = sb("var16", [128, 16], F32)
    rstd16 = sb("rstd16", [128, 16], F32)
    tmp16 = sb("tmp16", [128, 16], F32)
    xn_tok = sb("xn_tok", [128, 1024], BF16)
    battn = sb("battn", [128, 24], F32)
    bprow = sb("bprow", [1, 1024], BF16)
    ebrow = sb("ebrow", [1, 2048], BF16)
    rbrow = sb("rbrow", [1, 8], BF16)
    wass = sb("wass", [128, 8 * 64], BF16)
    rwt = sb("rwt", [128, 64], BF16)
    logit_s = sb("logit_s", [128, 128], F32)
    er_s = sb("er_s", [128, 128], F32)
    lmax16 = sb("lmax16", [128, 16], F32)
    esum16 = sb("esum16", [128, 16], F32)
    elmax16 = sb("elmax16", [128, 16], F32)
    w16 = sb("w16", [128, 16], F32)
    sel_s = sb("sel_s", [128, 128], F32)
    eblk = sb("eblk", [16, 32], F32)
    wsum_c = sb("wsum_c", [16, 1], F32)
    wrec_c = sb("wrec_c", [16, 1], F32)
    wblk_bf = sb("wblk_bf", [16, 32], BF16)
    wbig_s = sb("wbig_s", [128, 4 * 64], BF16)
    kcr_cm = sb("kcr_cm", [64, 1024], BF16)
    cv_bf = sb("cv_bf", [64, 1024], BF16)
    kcrT = sb("kcrT", [128, 8 * 56], BF16)
    p_sm_b = [sb("p_sm0", [128, 512], BF16), sb("p_sm1", [128, 512], BF16)]
    pt_sm_b = [sb("pt_sm0", [128, 384], BF16), sb("pt_sm1", [128, 384], BF16)]
    ssum_b = [sb("ssum0", [128, 1], F32), sb("ssum1", [128, 1], F32)]
    srec_b = [sb("srec0", [128, 1], F32), sb("srec1", [128, 1], F32)]
    zero64 = sb("zero64", [128, 64], BF16)
    qlmT = sb("qlmT", [128, 64], BF16)
    klmT = sb("klmT", [128, 64], BF16)
    lm_f = sb("lm_f", [128, 64], F32)
    rsum32 = sb("rsum32", [128, 32], F32)
    rrec32 = sb("rrec32", [128, 32], F32)
    s3sum = sb("s3sum", [128, 2], F32)
    s3tot = sb("s3tot", [128, 1], F32)
    s3rec = sb("s3rec", [128, 1], F32)
    k2sum = sb("k2sum", [64, 2], F32)
    k2rec = sb("k2rec", [64, 2], F32)
    denom_s = sb("denom_s", [1, 128], F32)
    dmax_s = sb("dmax_s", [1, 2], F32)
    drec_s = sb("drec_s", [1, 2], F32)
    dbc_s = sb("dbc_s", [128, 2], F32)
    sb_ptsv = sb("ptsv", [128, 1024], BF16)
    K2c = sb("K2c", [64, 128], BF16)
    K2Tc = sb("K2Tc", [64, 128], BF16)
    Ac = sb("Ac", [64, 128], BF16)
    Vc = sb("Vc", [64, 128], BF16)
    Vtc = sb("Vtc", [64, 128], BF16)
    U1c = sb("U1c", [64, 128], BF16)
    U2c = sb("U2c", [64, 128], BF16)
    U3c = sb("U3c", [64, 128], BF16)
    U4c = sb("U4c", [64, 128], BF16)
    U4Tc = sb("U4Tc", [64, 128], BF16)
    U5c = sb("U5c", [64, 128], BF16)
    k3vc = sb("k3vc", [64, 128], BF16)
    Zc = sb("Zc", [64, 128], BF16)
    c7b2, c15b2, c13b2 = c7, c15, c13
    ZB = sb("ZB", [128, 50176], BF16)
    ZF = sb("ZF", [128, 2048], F32)
    ck_s = ZF[0:64, 0:1024]
    cv_s = ZF[0:64, 1024:2048]
    lt_s = ZB[:, 45056:46080].bitcast(F32)[0:16, :]

    p01 = ps("p01", [128, 1024], F32)
    p23 = ps("p23", [128, 1024], F32)
    p4 = ps("p4", [128, 512], F32)
    p5 = ps("p5", [128, 512], F32)
    p6 = ps("p6", [128, 512], F32)
    p7 = ps("p7", [128, 512], F32)
    p6b = p6.bitcast(BF16)
    p7b = p7.bitcast(BF16)
    p01b = p01.bitcast(BF16)
    PK = {id(p01): "p01", id(p23): "p23", id(p4): "p4", id(p5): "p5",
          id(p6): "p6", id(p7): "p7"}

    assert nc.sbuf_bytes_remaining >= 0, nc.sbuf_bytes_remaining

    def zb(off_b, cols):
        c0 = off_b // 2
        return ZB[:, c0:c0 + cols]

    def zf(off_b, cols):
        c0 = off_b // 4
        return ZF[:, c0:c0 + cols]

    def memset(tile, val, key):
        bld.op("dve", lambda e, t=tile, v=val: e.memset(t, v), writes=(key,))

    def ln_pass(chunks, dst_fn, dst_keys, src_key_fn):
        for i in chunks:
            bld.op("dve", lambda e, i=i: e.reduce_sum(
                sum16[:, i:i + 1], r_s[:, 1024 * i:1024 * (i + 1)], axis=AX.X),
                reads=(src_key_fn(i),), writes=("sum16",))
            bld.op("act", lambda e, i=i: e.activation(
                xn_tok[:], r_s[:, 1024 * i:1024 * (i + 1)], AF.Square,
                accum_out=ssq16[:, i:i + 1]),
                reads=(src_key_fn(i),), writes=("xn_tok", "ssq16"))
        i0, i1 = chunks[0], chunks[-1] + 1
        sl = slice(i0, i1)
        bld.op("dve", lambda e: e.tensor_scalar(mean16[:, sl], sum16[:, sl], 1.0 / C, None, op0=ALU.mult),
               reads=("sum16",), writes=("mean16",))
        bld.op("dve", lambda e: e.tensor_tensor(tmp16[:, sl], mean16[:, sl], mean16[:, sl], op=ALU.mult),
               reads=("mean16",), writes=("tmp16",))
        bld.op("dve", lambda e: e.tensor_scalar(var16[:, sl], ssq16[:, sl], 1.0 / C, None, op0=ALU.mult),
               reads=("ssq16",), writes=("var16",))
        bld.op("dve", lambda e: e.tensor_tensor(var16[:, sl], var16[:, sl], tmp16[:, sl], op=ALU.subtract),
               reads=("tmp16", "var16"), writes=("var16",))
        bld.op("act", lambda e: e.activation(rstd16[:, sl], var16[:, sl], AF.Sqrt, bias=eps_t[:]),
               reads=("var16", "eps"), writes=("rstd16",))
        bld.op("dve", lambda e: e.reciprocal(rstd16[:, sl], rstd16[:, sl]),
               reads=("rstd16",), writes=("rstd16",))
        for i in chunks:
            bld.op("dve", lambda e, i=i: e.tensor_scalar(
                xn_tok[:], r_s[:, 1024 * i:1024 * (i + 1)],
                mean16[:, i:i + 1], rstd16[:, i:i + 1], op0=ALU.subtract, op1=ALU.mult),
                reads=(src_key_fn(i), "mean16", "rstd16"), writes=("xn_tok",))

            def trs(e, i=i):
                last = None
                for b in range(8):
                    last = e.transpose(p6b[:, 128 * b:128 * (b + 1)],
                                       xn_tok[:, 128 * b:128 * (b + 1)], idb[:])
                return last
            bld.op("pe", trs, reads=("xn_tok", "idb"), writes=("p6",))
            dst = dst_fn(i)
            bld.op("act", lambda e, dst=dst: e.copy(
                dst, p6b[:, 0:1024].rearrange("p (c t) -> p c t", t=128)),
                reads=("p6",), writes=dst_keys)

    def mmg(fns, reads, writes):
        def run(e, fns=fns):
            last = None
            for f in fns:
                last = f(e)
            return last
        bld.op("pe", run, reads=reads, writes=writes)

    def loads():
        bld.dma("Ld", lambda e: e.dma_start(
            out=r_s[:], in_=_ap_custom(di["x"], [[1024, 128], [131072, 16], [1, 1024]], 0)),
            writes=tuple(f"r{i}" for i in range(16)))
        for n, t, key in [("ROTQ", rotq, "rotq"), ("ROTKL", rotkl, "rotkl"),
                          ("TKC", tkc, "tkc"), ("TRI", tri, "tri")]:
            bld.dma("Ld", lambda e, n=n, t=t: e.dma_start(out=t[:], in_=di[n]), writes=(key,))
        for n, t, key in [("PERM", perm, "perm"), ("IDB", idb, "idb"),
                          ("C7", c7, "c7"), ("C15", c15, "c15"), ("C13", c13, "c13")]:
            bld.dma("Ld", lambda e, n=n, t=t: e.dma_start(out=t[:], in_=di[n]), writes=(key,))
        bld.dma("Wg", lambda e: e.dma_start(
            out=battn[:], in_=_ap_custom(di["b_attn"], [[1, 128], [128, 24]], 0),
            allow_slow_non_contiguous=True), writes=("battn",))
        bld.dma("Wg", lambda e: e.dma_start(out=bprow[:], in_=di["b_proj"].rearrange("(o d) -> o d", o=1)),
                writes=("bprow",))
        bld.dma("Wg", lambda e: e.dma_start(out=rbrow[:], in_=di["rb"].rearrange("(o d) -> o d", o=1)),
                writes=("rbrow",))
        bld.dma("Wg", lambda e: e.dma_start(
            out=wass[:], in_=_ap_custom(di["wassT"], [[64, 128], [8192, 8], [1, 64]], 0)), writes=("wass",))
        bld.dma("Wg", lambda e: e.dma_start(
            out=rwt[:], in_=_ap_custom(di["rwT"], [[8, 128], [1024, 8], [1, 8]], 0)), writes=("rwt",))
        memset(onesb[:], 1.0, "onesb")
        memset(ones1f[:], 1.0, "ones1f")
        memset(onesc[:], 1.0, "onesc")
        memset(eps_t[:], 1e-5, "eps")
        memset(qsc_t[:], 0.125, "qsc")
        memset(zero64[:], 0.0, "zero64")
        memset(ck_s[:], 0.0, "ck_s")
        memset(cv_s[:], 0.0, "cv_s")
        for i in range(16):
            bld.dma("Sc", lambda e, i=i: e.dma_start(
                out=wbig_dram[128 * i:128 * (i + 1), :], in_=zero64[:]),
                reads=("zero64",), writes=("wbig",))

    ARENA_A = ["waT0", "waT1", "xnTq", "qrT", "klrT", "ktok", "vtok",
               "kslot", "vslot_a", "yT", "wpT"]

    def attn_a():
        A_waT = 0
        A_xnTq = 49152
        A_qrT = 57344
        A_klrT = 65536
        A_ktok = 73728
        A_vtok = 81920
        A_kslot = 90112
        A_vslot = 91136
        A_yT = A_xnTq
        A_wpT = A_klrT
        waT_s = zb(A_waT, 8 * 3072)
        for half in range(2):
            bld.dma("Wg", lambda e, h=half: e.dma_start(
                out=zb(A_waT, 8 * 3072).rearrange("p (c f) -> p c f", f=3072)[:, :, 1536 * h:1536 * (h + 1)],
                in_=_ap_custom(di["waT"], [[3072, 128], [393216, 8], [1, 1536]], 1536 * h)),
                writes=("waT%d" % half,))

        for Q in range(NQ):
            xnTq = zb(A_xnTq, 4096)
            qrT = zb(A_qrT, 4096)
            klrT = zb(A_klrT, 4096)
            ktok = zb(A_ktok, 4096)
            vtok = zb(A_vtok, 4096)
            kslot = zb(A_kslot, 512)
            vslot = zb(A_vslot, 512)
            yT = zb(A_yT, 4096)

            bld.claim("xnTq", ["yT"])
            bld.claim("klrT", ["wpT"])
            bld.claim("ktok", ["wpT"])
            bld.claim("kslot", ["lt_s"])
            bld.claim("vslot_a", ["lt_s"])

            def xnq_dst(i, Q=Q):
                ii = i - 4 * Q
                return zb(A_xnTq, 4096).rearrange("p (c t) -> p c t", c=8)[:, :, 128 * ii:128 * (ii + 1)]
            ln_pass(list(range(4 * Q, 4 * Q + 4)), xnq_dst, ("xnTq",),
                    lambda i: "r%d" % i)

            for F in range(24):
                kind, co = F // 8, F % 8
                pb, pk = (p4, "p4") if F % 2 == 0 else (p5, "p5")

                def qf(e, F=F, pb=pb):
                    last = None
                    for c in range(8):
                        last = e.matmul(pb[:, 0:512],
                                        waT_s[:, 3072 * c + 128 * F:3072 * c + 128 * (F + 1)],
                                        xnTq[:, 512 * c:512 * (c + 1)],
                                        start=(c == 0), stop=(c == 7))
                    return last
                mmg([qf], reads=("waT0", "waT1", "xnTq"), writes=(pk,))
                if kind == 0:
                    bld.op("dve", lambda e, co=co, F=F, pb=pb: e.scalar_tensor_tensor(
                        qrT[:, 512 * co:512 * (co + 1)], pb[:, 0:512],
                        battn[:, F:F + 1], rotq[:], op0=ALU.add, op1=ALU.mult),
                        reads=(pk, "battn", "rotq"), writes=("qrT",))
                else:
                    slot, slotk = (kslot, "kslot") if kind == 1 else (vslot, "vslot_a")
                    bld.op("act", lambda e, F=F, pb=pb, slot=slot: e.activation(
                        slot[:], pb[:, 0:512], AF.Identity, bias=battn[:, F:F + 1]),
                        reads=(pk, "battn"), writes=(slotk,))
                    if kind == 1:
                        ppb, ppk = (p5, "p5") if F % 2 == 0 else (p4, "p4")
                        mmg([lambda e, ppb=ppb: e.matmul(
                            ppb[:, 0:512], perm[:], kslot[:], start=True, stop=True)],
                            reads=("kslot", "perm"), writes=(ppk,))
                        bld.op("dve", lambda e, co=co, ppb=ppb, Q=Q: e.tensor_tensor(
                            klrT[:, 512 * co:512 * (co + 1)], ppb[:, 0:512],
                            rotkl[:, 512 * Q:512 * (Q + 1)], op=ALU.mult),
                            reads=(ppk, "rotkl"), writes=("klrT",))
                    tokd, tokk = (ktok, "ktok") if kind == 1 else (vtok, "vtok")

                    def ttr(e, slot=slot):
                        last = None
                        for jj in range(4):
                            last = e.transpose(p6b[:, 128 * jj:128 * (jj + 1)],
                                               slot[:, 128 * jj:128 * (jj + 1)], idb[:])
                        return last
                    bld.op("pe", ttr, reads=(slotk, "idb"), writes=("p6",))
                    dst = tokd.rearrange("p (j d) -> p j d", j=4)[:, :, 128 * co:128 * (co + 1)]
                    bld.op("act", lambda e, dst=dst: e.copy(
                        dst, p6b[:, 0:512].rearrange("p (j d) -> p j d", j=4)),
                        reads=("p6",), writes=(tokk,))

            bld.claim("lt_s", ["kslot", "vslot_a"])
            mmg([lambda e, Q=Q: [e.matmul(p7[0:16, 0:512],
                                          wass[:, 64 * c + 16 * Q:64 * c + 16 * Q + 16],
                                          xnTq[:, 512 * c:512 * (c + 1)],
                                          start=(c == 0), stop=(c == 7)) for c in range(8)][-1]],
                reads=("wass", "xnTq"), writes=("p7",))
            bld.op("dve", lambda e: e.tensor_copy(lt_s[:], p7[0:16, 0:512]),
                   reads=("p7",), writes=("lt_s",))
            bld.dma("Sc", lambda e: e.dma_start(out=lt_dram, in_=lt_s[:]),
                    reads=("lt_s",), writes=("lt_dram",))
            diag = _ap_custom(lt_dram, [[544, 16], [1, 32]], 0)
            bld.dma("Sc", lambda e, diag=diag: e.dma_start(out=eblk[:], in_=diag),
                    reads=("lt_dram",), writes=("eblk",))
            bld.op("act", lambda e: e.activation(eblk[:], eblk[:], AF.Exp, accum_out=wsum_c[:]),
                   reads=("eblk",), writes=("eblk", "wsum_c"))
            bld.op("dve", lambda e: e.reciprocal(wrec_c[:], wsum_c[:]),
                   reads=("wsum_c",), writes=("wrec_c",))
            bld.op("dve", lambda e: e.tensor_scalar(wblk_bf[:], eblk[:], wrec_c[:], None, op0=ALU.mult),
                   reads=("eblk", "wrec_c"), writes=("wblk_bf",))
            scat = _ap_custom(wbig_dram, [[2049, 16], [64, 32]], 32768 * Q + 16 * Q)
            bld.dma("Sc", lambda e, scat=scat: e.dma_start(
                out=scat, in_=wblk_bf[:], allow_slow_non_contiguous=True),
                reads=("wblk_bf",), writes=("wbig",))
            bld.dma("Sc", lambda e, Q=Q: e.dma_start(
                out=wbig_s[:], in_=_ap_custom(wbig_dram, [[64, 128], [8192, 4], [1, 64]], 32768 * Q)),
                reads=("wbig",), writes=("wbig_s",))

            for (tok, tk, pb, pk, dst, dstk) in [(ktok, "ktok", p01, "p01", ck_s, "ck_s"),
                                                 (vtok, "vtok", p23, "p23", cv_s, "cv_s")]:
                def sf(e, tok=tok, pb=pb):
                    last = None
                    for j in range(4):
                        for nb in range(2):
                            last = e.matmul(pb[0:64, 512 * nb:512 * (nb + 1)],
                                            wbig_s[:, 64 * j:64 * (j + 1)],
                                            tok[:, 1024 * j + 512 * nb:1024 * j + 512 * (nb + 1)],
                                            start=(j == 0), stop=(j == 3))
                    return last
                mmg([sf], reads=("wbig_s", tk), writes=(pk,))
                bld.op("dve", lambda e, pb=pb, dst=dst: e.tensor_tensor(
                    dst[:], pb[0:64, 0:1024], dst[:], op=ALU.add),
                    reads=(pk, dstk), writes=(dstk,))

            evn = lambda ap: ap.rearrange("p (a two) -> p two a", two=2)[:, 0:1, :]
            odd = lambda ap: ap.rearrange("p (a two) -> p two a", two=2)[:, 1:2, :]
            bld.op("dve", lambda e: e.tensor_tensor(evn(kcr_cm[:]), odd(ck_s[:]), evn(tkc[:]), op=ALU.mult),
                   reads=("ck_s", "tkc"), writes=("kcr_cm",))
            bld.op("dve", lambda e: e.tensor_tensor(odd(kcr_cm[:]), evn(ck_s[:]), odd(tkc[:]), op=ALU.mult),
                   reads=("ck_s", "tkc"), writes=("kcr_cm",))
            bld.op("dve", lambda e: e.tensor_copy(cv_bf[:], cv_s[:]),
                   reads=("cv_s",), writes=("cv_bf",))

            def ktrs(e):
                last = None
                for h in range(16):
                    hb = 64 * (h % 2)
                    last = e.transpose(
                        p6b[hb:hb + 64, 56 * (h // 2) + 448 * (h % 2):56 * (h // 2) + 448 * (h % 2) + 56],
                        kcr_cm[0:56, 64 * h:64 * (h + 1)],
                        idb[0:56, 0:56], tile_position=(0, hb))
                return last
            bld.op("pe", ktrs, reads=("kcr_cm", "idb"), writes=("p6",))
            bld.op("act", lambda e: e.copy(kcrT[0:64, 0:448], p6b[0:64, 0:448]),
                   reads=("p6",), writes=("kcrT",))
            bld.op("act", lambda e: e.copy(kcrT[64:128, 0:448], p6b[64:128, 448:896]),
                   reads=("p6",), writes=("kcrT",))

            bld.claim("yT", ["xnTq"])
            iters = [(gl, hp, hh, qc) for gl in range(2) for hp in range(8)
                     for hh in range(2) for qc in range(2)]

            def front(i):
                gl, hp, hh, qc = iters[i]
                ncl_g = NCPG * (2 * Q + gl)
                hb = 64 * hh
                pp = i % 2
                qend = 128 * (qc + 1)
                ncols = ncl_g + qend
                spb, spk = (p4, "p4") if pp == 0 else (p5, "p5")
                p_sm, ssum1, srec1 = p_sm_b[pp], ssum_b[pp], srec_b[pp]
                psk, ssk, srk = "p_sm%d" % pp, "ssum%d" % pp, "srec%d" % pp

                def sc_f(e, hp=hp, hb=hb, gl=gl, qc=qc, ncl_g=ncl_g, qend=qend, spb=spb):
                    last = None
                    if ncl_g > 0:
                        last = e.matmul(
                            spb[:, 0:ncl_g],
                            qrT[hb:hb + 64, 512 * hp + 256 * gl + 128 * qc:512 * hp + 256 * gl + 128 * (qc + 1)],
                            kcrT[hb:hb + 64, 56 * hp:56 * hp + ncl_g],
                            start=True, stop=True, tile_position=(hb, 0))
                    last = e.matmul(
                        spb[:, ncl_g:ncl_g + qend],
                        qrT[hb:hb + 64, 512 * hp + 256 * gl + 128 * qc:512 * hp + 256 * gl + 128 * (qc + 1)],
                        klrT[hb:hb + 64, 512 * hp + 256 * gl:512 * hp + 256 * gl + qend],
                        start=True, stop=True, tile_position=(hb, 0))
                    return last
                mmg([sc_f], reads=("qrT", "kcrT", "klrT"), writes=(spk,))
                bld.op("dve", lambda e, spb=spb, ncl_g=ncl_g, qc=qc: e.tensor_tensor(
                    spb[:, ncl_g + 128 * qc:ncl_g + 128 * (qc + 1)],
                    spb[:, ncl_g + 128 * qc:ncl_g + 128 * (qc + 1)],
                    tri[:], op=ALU.add),
                    reads=(spk, "tri"), writes=(spk,))
                bld.op("act", lambda e, spb=spb, ncols=ncols, p_sm=p_sm, ssum1=ssum1: e.activation(
                    p_sm[:, 0:ncols], spb[:, 0:ncols], AF.Exp, accum_out=ssum1[:]),
                    reads=(spk,), writes=(psk, ssk))
                bld.op("dve", lambda e, srec1=srec1, ssum1=ssum1: e.reciprocal(srec1[:], ssum1[:]),
                       reads=(ssk,), writes=(srk,))
                bld.op("dve", lambda e, ncols=ncols, p_sm=p_sm, srec1=srec1: e.tensor_scalar(
                    p_sm[:, 0:ncols], p_sm[:, 0:ncols], srec1[:], None, op0=ALU.mult),
                    reads=(psk, srk), writes=(psk,))

            def back(i):
                gl, hp, hh, qc = iters[i]
                ncl_g = NCPG * (2 * Q + gl)
                h = 2 * hp + hh
                hb = 64 * hh
                pp = i % 2
                qend = 128 * (qc + 1)
                yc0 = 256 * (hp % 2)
                p_sm, pt_sm = p_sm_b[pp], pt_sm_b[pp]
                psk, ptk = "p_sm%d" % pp, "pt_sm%d" % pp
                ptp, ptpk = (p6b, "p6") if pp == 0 else (p01b, "p01")

                def pt_f(e, ncl_g=ncl_g, qc=qc, p_sm=p_sm, ptp=ptp):
                    last = None
                    if ncl_g > 0:
                        last = e.transpose(ptp[0:ncl_g, 0:128], p_sm[:, 0:ncl_g], idb[:])
                    for kc in range(qc + 1):
                        last = e.transpose(ptp[:, 128 * (kc + 1):128 * (kc + 2)],
                                           p_sm[:, ncl_g + 128 * kc:ncl_g + 128 * (kc + 1)],
                                           idb[:])
                    return last
                bld.op("pe", pt_f, reads=(psk, "idb"), writes=(ptpk,))
                if ncl_g > 0:
                    bld.op("act", lambda e, ncl_g=ncl_g, pt_sm=pt_sm, ptp=ptp: e.copy(
                        pt_sm[0:ncl_g, 0:128], ptp[0:ncl_g, 0:128]),
                        reads=(ptpk,), writes=(ptk,))
                bld.op("act", lambda e, qc=qc, pt_sm=pt_sm, ptp=ptp: e.copy(
                    pt_sm[:, 128:128 * (qc + 2)], ptp[:, 128:128 * (qc + 2)]),
                    reads=(ptpk,), writes=(ptk,))

                def pv_f(e, h=h, hb=hb, gl=gl, qc=qc, ncl_g=ncl_g, yc0=yc0, pt_sm=pt_sm):
                    last = None
                    first = True
                    yc = yc0 + 128 * qc
                    if ncl_g > 0:
                        last = e.matmul(p7[hb:hb + 64, yc:yc + 128],
                                        cv_bf[0:ncl_g, 64 * h:64 * (h + 1)],
                                        pt_sm[0:ncl_g, 0:128],
                                        start=True, stop=False, tile_position=(0, hb))
                        first = False
                    for kc in range(qc + 1):
                        j = 2 * gl + kc
                        last = e.matmul(p7[hb:hb + 64, yc:yc + 128],
                                        vtok[:, 1024 * j + 64 * h:1024 * j + 64 * (h + 1)],
                                        pt_sm[:, 128 * (kc + 1):128 * (kc + 2)],
                                        start=first, stop=(kc == qc),
                                        tile_position=(0, hb))
                        first = False
                    return last
                mmg([pv_f], reads=(ptk, "cv_bf", "vtok"), writes=("p7",))
                if hh == 1 and qc == 1:
                    bld.op("dve", lambda e, hp=hp, gl=gl, yc0=yc0: e.tensor_copy(
                        yT[:, 512 * hp + 256 * gl:512 * hp + 256 * (gl + 1)], p7[:, yc0:yc0 + 256]),
                        reads=("p7",), writes=("yT",))

            front(0)
            for i in range(1, len(iters)):
                front(i)
                back(i - 1)
            back(len(iters) - 1)

            bld.claim("wpT", ["klrT", "ktok"])
            wpT_s = zb(A_wpT, 8192)
            bld.dma("Wg", lambda e: e.dma_start(
                out=zb(A_wpT, 8192), in_=_ap_custom(di["wpaT"], [[1024, 128], [131072, 8], [1, 1024]], 0)),
                writes=("wpT",))
            for qb in range(4):
                i = 4 * Q + qb

                def opf(e, qb=qb, wpT_s=wpT_s):
                    last = None
                    for nb in range(2):
                        for c in range(8):
                            last = e.matmul(p01[:, 512 * nb:512 * (nb + 1)],
                                            yT[:, 512 * c + 128 * qb:512 * c + 128 * (qb + 1)],
                                            wpT_s[:, 1024 * c + 512 * nb:1024 * c + 512 * nb + 512],
                                            start=(c == 0), stop=False)
                        last = e.matmul(p01[:, 512 * nb:512 * (nb + 1)], onesb[:],
                                        bprow[0:1, 512 * nb:512 * (nb + 1)],
                                        start=False, stop=True)
                    return last
                mmg([opf], reads=("yT", "wpT", "onesb", "bprow"), writes=("p01",))
                bld.op("dve", lambda e, i=i: e.tensor_tensor(
                    r_s[:, 1024 * i:1024 * (i + 1)], p01[:, 0:1024],
                    r_s[:, 1024 * i:1024 * (i + 1)], op=ALU.add),
                    reads=("p01", "r%d" % i), writes=("r%d" % i,))

    ARENA_F = ["weT0", "weT1", "xnT"]
    ARENA_B = ["svT", "wq0", "wq1", "qTb", "kTb", "P1", "P3", "vtokb",
               "vslot_b", "ptslot", "wpb", "xnT"]

    def ffn(p):
        A_weT = [0, 16384]
        A_xnT = 32768
        prev = ARENA_A if p == 0 else ARENA_B
        for k in ARENA_F:
            bld.claim(k, prev)
        xnT = zb(A_xnT, 16384)

        def xn_dst(i):
            return zb(A_xnT, 16384).rearrange("p (c t) -> p c t", c=8)[:, :, 128 * i:128 * (i + 1)]
        ln_pass(list(range(16)), xn_dst, ("xnT",), lambda i: "r%d" % i)

        def rf(e):
            last = None
            for i in range(16):
                for c in range(8):
                    last = e.matmul(p4[:, 8 * i:8 * (i + 1)],
                                    xnT[:, 2048 * c + 128 * i:2048 * c + 128 * (i + 1)],
                                    rwt[:, 8 * c:8 * (c + 1)], start=(c == 0), stop=False)
                last = e.matmul(p4[:, 8 * i:8 * (i + 1)], onesb[:], rbrow[:],
                                start=False, stop=True)
            return last
        mmg([rf], reads=("xnT", "rwt", "onesb", "rbrow"), writes=("p4",))
        bld.op("dve", lambda e: e.tensor_copy(logit_s[:], p4[:, 0:128]),
               reads=("p4",), writes=("logit_s",))
        bld.op("dve", lambda e: e.tensor_reduce(
            lmax16[:].rearrange("p (a b) -> p a b", b=1),
            logit_s[:].rearrange("p (a b) -> p a b", b=8),
            axis=AX.X, op=ALU.max, opt_input=False, opt_output=False),
            reads=("logit_s",), writes=("lmax16",))
        bld.op("act", lambda e: e.activation(er_s[:], logit_s[:], AF.Exp),
               reads=("logit_s",), writes=("er_s",))
        bld.op("act", lambda e: e.activation(elmax16[:], lmax16[:], AF.Exp),
               reads=("lmax16",), writes=("elmax16",))
        bld.op("dve", lambda e: e.tensor_reduce(
            esum16[:].rearrange("p (a b) -> p a b", b=1),
            er_s[:].rearrange("p (a b) -> p a b", b=8),
            axis=AX.X, op=ALU.add, opt_input=False, opt_output=False),
            reads=("er_s",), writes=("esum16",))
        bld.op("dve", lambda e: e.reciprocal(esum16[:], esum16[:]),
               reads=("esum16",), writes=("esum16",))
        bld.op("dve", lambda e: e.tensor_tensor(w16[:], elmax16[:], esum16[:], op=ALU.mult),
               reads=("elmax16", "esum16"), writes=("w16",))
        for i in range(16):
            bld.op("dve", lambda e, i=i: e.tensor_scalar(
                sel_s[:, 8 * i:8 * (i + 1)], logit_s[:, 8 * i:8 * (i + 1)],
                lmax16[:, i:i + 1], w16[:, i:i + 1], op0=ALU.is_equal, op1=ALU.mult),
                reads=("logit_s", "lmax16", "w16"), writes=("sel_s",))

        if dbg and p == 0:
            bld.dma("St", lambda e: e.dma_start(out=dbg_d["logit"], in_=logit_s[:]), reads=("logit_s",))
            bld.dma("St", lambda e: e.dma_start(out=dbg_d["sel"], in_=sel_s[:]), reads=("sel_s",))
            bld.dma("St", lambda e: e.dma_start(out=dbg_d["lmax"], in_=lmax16[:]), reads=("lmax16",))
            bld.dma("St", lambda e: e.dma_start(out=dbg_d["w16d"], in_=w16[:]), reads=("w16",))
            dxn = nc.dram_tensor("dbg_xnb", [128, 16384], BF16, kind="ExternalOutput").ap()
            dbg_d["xnTb"] = dxn
            bld.dma("St", lambda e: e.dma_start(out=dxn, in_=xnT), reads=("xnT",))
        for e_i in range(NE):
            buf = e_i % 2
            weT = zb(A_weT[buf], 8192)
            bld.dma("Wg", lambda e, e_i=e_i, buf=buf: e.dma_start(
                out=zb(A_weT[buf], 8192),
                in_=_ap_custom(di["weT"][e_i], [[1024, 128], [131072, 8], [1, 1024]], 0)),
                writes=("weT%d" % buf,))
            bld.dma("Wg", lambda e, e_i=e_i, buf=buf: e.dma_start(
                out=ebrow[0:1, 1024 * buf:1024 * (buf + 1)], in_=di["eb"][e_i].rearrange("(o d) -> o d", o=1)),
                writes=("ebrow%d" % buf,))
            for i in range(16):
                pb, pk = (p01, "p01") if i % 2 == 0 else (p23, "p23")

                def ef(e, i=i, pb=pb, weT=weT, buf=buf):
                    last = None
                    for nb in range(2):
                        for c in range(8):
                            last = e.matmul(pb[:, 512 * nb:512 * (nb + 1)],
                                            xnT[:, 2048 * c + 128 * i:2048 * c + 128 * (i + 1)],
                                            weT[:, 1024 * c + 512 * nb:1024 * c + 512 * nb + 512],
                                            start=(c == 0), stop=False)
                        last = e.matmul(pb[:, 512 * nb:512 * (nb + 1)], onesb[:],
                                        ebrow[0:1, 1024 * buf + 512 * nb:1024 * buf + 512 * (nb + 1)],
                                        start=False, stop=True)
                    return last
                mmg([ef], reads=("xnT", "weT%d" % buf, "onesb", "ebrow%d" % buf), writes=(pk,))
                bld.op("dve", lambda e, i=i, e_i=e_i, pb=pb: e.scalar_tensor_tensor(
                    r_s[:, 1024 * i:1024 * (i + 1)], pb[:, 0:1024],
                    sel_s[:, 8 * i + e_i:8 * i + e_i + 1],
                    r_s[:, 1024 * i:1024 * (i + 1)], op0=ALU.mult, op1=ALU.add),
                    reads=(pk, "sel_s", "r%d" % i), writes=("r%d" % i,))

    def attn_b():
        for k in ARENA_B:
            bld.claim(k, ARENA_F)
        A_xnT = 32768
        A_svT = 0
        A_wq = [65536, 71680]
        A_qT = 77824
        A_kT = 81920
        A_P1 = 86016
        A_P3 = 90112
        A_vtokb = 94208
        A_vslot = 98304
        A_ptslot = 99328
        xnT = zb(A_xnT, 16384)
        svT = zb(A_svT, 16384)

        def xn_dst(i):
            return zb(A_xnT, 16384).rearrange("p (c t) -> p c t", c=8)[:, :, 128 * i:128 * (i + 1)]
        ln_pass(list(range(16)), xn_dst, ("xnT",), lambda i: "r%d" % i)

        ptsv = sb_ptsv

        def stageA(hp):
            buf = hp % 2
            wq = zb(A_wq[buf], 3072)
            qT = zb(A_qT, 2048)
            kT = zb(A_kT, 2048)
            vtokb = zb(A_vtokb, 2048)
            vslot = zb(A_vslot, 512)
            for kind in range(3):
                src = _ap_custom(di["wqkv"], [[3072, 128], [393216, 8], [1, 128]],
                                 1024 * kind + 128 * hp)
                bld.dma("Wg", lambda e, buf=buf, src=src, kind=kind: e.dma_start(
                    out=zb(A_wq[buf], 3072).rearrange("p (c f) -> p c f", f=384)[:, :, 128 * kind:128 * (kind + 1)],
                    in_=src), writes=("wq%d" % buf,))
            for kind in range(3):
                for nb in range(4):
                    pb, pk = (p4, "p4") if nb % 2 == 0 else (p5, "p5")

                    def qf(e, kind=kind, nb=nb, pb=pb, wq=wq):
                        last = None
                        for c in range(8):
                            last = e.matmul(pb[:, 0:512],
                                            wq[:, 384 * c + 128 * kind:384 * c + 128 * (kind + 1)],
                                            xnT[:, 2048 * c + 512 * nb:2048 * c + 512 * (nb + 1)],
                                            start=(c == 0), stop=(c == 7))
                        return last
                    mmg([qf], reads=("wq%d" % buf, "xnT"), writes=(pk,))
                    if kind == 0:
                        bld.op("act", lambda e, nb=nb, pb=pb, qT=qT: e.activation(
                            qT[:, 512 * nb:512 * (nb + 1)], pb[:, 0:512], AF.Copy, scale=qsc_t[:]),
                            reads=(pk, "qsc"), writes=("qTb",))
                    elif kind == 1:
                        bld.op("act", lambda e, nb=nb, pb=pb, kT=kT: e.copy(
                            kT[:, 512 * nb:512 * (nb + 1)], pb[:, 0:512]),
                            reads=(pk,), writes=("kTb",))
                    else:
                        bld.op("act", lambda e, pb=pb, vslot=vslot: e.copy(
                            vslot[:], pb[:, 0:512]), reads=(pk,), writes=("vslot_b",))

                        def vtr(e, vslot=vslot):
                            last = None
                            for jj in range(4):
                                last = e.transpose(p6b[:, 128 * jj:128 * (jj + 1)],
                                                   vslot[:, 128 * jj:128 * (jj + 1)], idb[:])
                            return last
                        bld.op("pe", vtr, reads=("vslot_b", "idb"), writes=("p6",))
                        bld.op("act", lambda e, nb=nb, vtokb=vtokb: e.copy(
                            vtokb[:, 512 * nb:512 * (nb + 1)], p6b[:, 0:512]),
                            reads=("p6",), writes=("vtokb",))
            for (srcT, sk, dst, dk) in [(qT, "qTb", qlmT, "qlmT"), (kT, "kTb", klmT, "klmT")]:
                bld.op("dve", lambda e, srcT=srcT: e.tensor_reduce(
                    lm_f[:].rearrange("p (a b) -> p a b", b=1),
                    srcT.rearrange("p (a b) -> p a b", b=32),
                    axis=AX.X, op=ALU.add, opt_input=False, opt_output=False),
                    reads=(sk,), writes=("lm_f",))
                bld.op("dve", lambda e, dst=dst: e.tensor_scalar(
                    dst[:], lm_f[:], 1.0 / 32, None, op0=ALU.mult),
                    reads=("lm_f",), writes=(dk,))

        def pair0(dst_p, c0, lhs, rhs, lk, rk, pk, starts=None):
            def f(e, dst_p=dst_p, c0=c0, lhs=lhs, rhs=rhs):
                last = None
                for hh in range(2):
                    last = e.matmul(dst_p[0:64, c0 + 64 * hh:c0 + 64 * (hh + 1)],
                                    lhs[:, 64 * hh:64 * (hh + 1)],
                                    rhs[:, 64 * hh:64 * (hh + 1)],
                                    start=True, stop=True, tile_position=(0, 0))
                return last
            mmg([f], reads=(lk, rk), writes=(pk,))

        def inv_gen():
            mmg([lambda e: e.matmul(p4[0:64, 0:64], qlmT[0:64, :], klmT[0:64, :],
                                    start=True, stop=True, tile_position=(0, 0))],
                reads=("qlmT", "klmT"), writes=("p4",))
            mmg([lambda e: e.matmul(p5[0:64, 0:64], qlmT[64:128, :], klmT[64:128, :],
                                    start=True, stop=True, tile_position=(64, 0))],
                reads=("qlmT", "klmT"), writes=("p5",))
            yield
            for hh, pb, pk in [(0, p4, "p4"), (1, p5, "p5")]:
                bld.op("act", lambda e, hh=hh, pb=pb: e.activation(
                    er_s[0:64, 64 * hh:64 * (hh + 1)], pb[0:64, 0:64], AF.Exp,
                    accum_out=k2sum[0:64, hh:hh + 1]),
                    reads=(pk,), writes=("er_s", "k2sum"))
            bld.op("dve", lambda e: e.reciprocal(k2rec[0:64, 0:2], k2sum[0:64, 0:2]),
                   reads=("k2sum",), writes=("k2rec",))
            yield
            for hh in range(2):
                bld.op("dve", lambda e, hh=hh: e.tensor_scalar(
                    K2c[:, 64 * hh:64 * (hh + 1)], er_s[0:64, 64 * hh:64 * (hh + 1)],
                    k2rec[0:64, hh:hh + 1], None, op0=ALU.mult),
                    reads=("er_s", "k2rec"), writes=("i_K2",))
            yield
            mmg([lambda e: [e.matmul(p4[0:1, 64 + 64 * hh:128 + 64 * hh],
                                     onesc[0:64, :], K2c[:, 64 * hh:64 * (hh + 1)],
                                     start=True, stop=True, tile_position=(0, 0))
                            for hh in range(2)][-1]],
                reads=("i_K2", "onesc"), writes=("p4",))
            yield
            bld.op("dve", lambda e: e.tensor_copy(denom_s[:], p4[0:1, 64:192]),
                   reads=("p4",), writes=("denom",))
            bld.op("dve", lambda e: e.tensor_reduce(
                dmax_s[:].rearrange("p (a b) -> p a b", b=1),
                denom_s[:].rearrange("p (a b) -> p a b", b=64),
                axis=AX.X, op=ALU.max, opt_input=False, opt_output=False),
                reads=("denom",), writes=("dmax",))
            bld.op("dve", lambda e: e.reciprocal(drec_s[:], dmax_s[:]),
                   reads=("dmax",), writes=("drec",))
            yield
            mmg([lambda e: e.matmul(p4[:, 192:194], ones1f[:], drec_s[:], start=True, stop=True)],
                reads=("drec", "ones1f"), writes=("p4",))
            yield
            bld.op("dve", lambda e: e.tensor_copy(dbc_s[:], p4[:, 192:194]),
                   reads=("p4",), writes=("dbc",))
            bld.op("pe", lambda e: e.transpose(p6b[0:64, 512:576], K2c[:, 0:64], idb[0:64, 0:64],
                                               tile_position=(0, 0)),
                   reads=("i_K2", "idb"), writes=("p6",))
            bld.op("act", lambda e: e.copy(K2Tc[:, 0:64], p6b[0:64, 512:576]),
                   reads=("p6",), writes=("i_K2T",))
            bld.op("pe", lambda e: e.transpose(p7b[0:64, 512:576], K2c[:, 64:128], idb[0:64, 0:64],
                                               tile_position=(0, 0)),
                   reads=("i_K2", "idb"), writes=("p7",))
            bld.op("act", lambda e: e.copy(K2Tc[:, 64:128], p7b[0:64, 512:576]),
                   reads=("p7",), writes=("i_K2T",))
            yield
            for hh in range(2):
                bld.op("dve", lambda e, hh=hh: e.tensor_scalar(
                    Vc[:, 64 * hh:64 * (hh + 1)], K2Tc[:, 64 * hh:64 * (hh + 1)],
                    dbc_s[0:64, hh:hh + 1], None, op0=ALU.mult),
                    reads=("i_K2T", "dbc"), writes=("i_V",))
                bld.op("dve", lambda e, hh=hh: e.tensor_scalar(
                    Vtc[:, 64 * hh:64 * (hh + 1)], K2c[:, 64 * hh:64 * (hh + 1)],
                    dbc_s[0:64, hh:hh + 1], None, op0=ALU.mult),
                    reads=("i_K2", "dbc"), writes=("i_Vt",))
            yield
            pair0(p4, 0, K2Tc, Vc, "i_K2T", "i_V", "p4")
            yield
            bld.op("dve", lambda e: e.tensor_copy(Ac[:], p4[0:64, 0:128]),
                   reads=("p4",), writes=("i_A",))
            bld.op("dve", lambda e: e.scalar_tensor_tensor(
                U1c[:], p4[0:64, 0:128], -1.0, c7b2[:], op0=ALU.mult, op1=ALU.add),
                reads=("p4", "c7"), writes=("i_U1",))
            yield
            for it in range(6):
                pair0(p4, 128, Ac, U1c, "i_A", "i_U1", "p4")
                yield
                bld.op("dve", lambda e: e.scalar_tensor_tensor(
                    U3c[:], p4[0:64, 128:256], -1.0, c15b2[:], op0=ALU.mult, op1=ALU.add),
                    reads=("p4", "c15"), writes=("i_U3",))
                yield
                pair0(p4, 256, Ac, U3c, "i_A", "i_U3", "p4")
                yield
                bld.op("dve", lambda e: e.scalar_tensor_tensor(
                    U5c[:], p4[0:64, 256:384], -1.0, c13b2[:], op0=ALU.mult, op1=ALU.add),
                    reads=("p4", "c13"), writes=("i_U5",))
                yield
                pair0(p4, 384, Vtc, U5c, "i_Vt", "i_U5", "p4")
                pair0(p5, 128, U5c, Vtc, "i_U5", "i_Vt", "p5")
                pair0(p5, 256, Ac, U5c, "i_A", "i_U5", "p5")
                yield
                bld.op("dve", lambda e: e.tensor_scalar(
                    Vc[:], p4[0:64, 384:512], 0.25, None, op0=ALU.mult),
                    reads=("p4",), writes=("i_V",))
                bld.op("dve", lambda e: e.tensor_scalar(
                    Vtc[:], p5[0:64, 128:256], 0.25, None, op0=ALU.mult),
                    reads=("p5",), writes=("i_Vt",))
                if it < 5:
                    bld.op("dve", lambda e: e.tensor_scalar(
                        Ac[:], p5[0:64, 256:384], 0.25, None, op0=ALU.mult),
                        reads=("p5",), writes=("i_A",))
                    bld.op("dve", lambda e: e.scalar_tensor_tensor(
                        U1c[:], p5[0:64, 256:384], -0.25, c7b2[:], op0=ALU.mult, op1=ALU.add),
                        reads=("p5", "c7"), writes=("i_U1",))
                yield

        def stageB(hp):
            qT = zb(A_qT, 2048)
            kT = zb(A_kT, 2048)
            P1 = zb(A_P1, 2048)
            P3 = zb(A_P3, 2048)
            vtokb = zb(A_vtokb, 2048)
            ptslot = zb(A_ptslot, 512)
            g = inv_gen()

            def adv(n=2):
                for _ in range(n):
                    if next(g, "done") == "done":
                        return

            adv(1)
            for j4 in range(4):
                def k1f(e, j4=j4, hh=0, pb=p7):
                    last = None
                    for jj in range(4):
                        j = 4 * j4 + jj
                        last = e.matmul(pb[:, 64 * jj:64 * (jj + 1)],
                                        qT[64 * hh:64 * (hh + 1), 128 * j:128 * (j + 1)],
                                        klmT[64 * hh:64 * (hh + 1), :],
                                        start=True, stop=True, tile_position=(64 * hh, 0))
                    return last
                mmg([lambda e, j4=j4: k1f(e, j4, 0, p7)], reads=("qTb", "klmT"), writes=("p7",))
                mmg([lambda e, j4=j4: k1f(e, j4, 1, p6)], reads=("qTb", "klmT"), writes=("p6",))
                for hh, pb, pk in [(0, p7, "p7"), (1, p6, "p6")]:
                    dst = P1[:, 512 * j4:512 * (j4 + 1)].rearrange(
                        "p (j h m) -> p j h m", h=2, m=64)[:, :, hh, :]
                    bld.op("act", lambda e, pb=pb, dst=dst: e.activation(
                        dst, pb[:, 0:256].rearrange("p (j m) -> p j m", m=64), AF.Exp),
                        reads=(pk,), writes=("P1",))
                bld.op("dve", lambda e, j4=j4, P1=P1: e.tensor_reduce(
                    rsum32[:, 8 * j4:8 * (j4 + 1)].rearrange("p (a b) -> p a b", b=1),
                    P1[:, 512 * j4:512 * (j4 + 1)].rearrange("p (a b) -> p a b", b=64),
                    axis=AX.X, op=ALU.add, opt_input=False, opt_output=False),
                    reads=("P1",), writes=("rsum32",))
                adv()
            bld.op("dve", lambda e: e.reciprocal(rrec32[:], rsum32[:]),
                   reads=("rsum32",), writes=("rrec32",))
            adv()
            for j in range(16):
                for hh in range(2):
                    bld.op("dve", lambda e, j=j, hh=hh, P1=P1: e.tensor_scalar(
                        P1[:, 128 * j + 64 * hh:128 * j + 64 * (hh + 1)],
                        P1[:, 128 * j + 64 * hh:128 * j + 64 * (hh + 1)],
                        rrec32[:, 2 * j + hh:2 * j + hh + 1], None, op0=ALU.mult),
                        reads=("P1", "rrec32"), writes=("P1",))
                if j % 4 == 3:
                    adv()
            for half in range(2):
                for hh, pb, pk in [(0, p01, "p01"), (1, p23, "p23")]:
                    def k3f(e, half=half, hh=hh, pb=pb):
                        last = None
                        for nb in range(2):
                            last = e.matmul(pb[64 * hh:64 * (hh + 1), 512 * nb:512 * (nb + 1)],
                                            qlmT[64 * hh:64 * (hh + 1), :],
                                            kT[64 * hh:64 * (hh + 1), 1024 * half + 512 * nb:1024 * half + 512 * (nb + 1)],
                                            start=True, stop=True, tile_position=(64 * hh, 64 * hh))
                        return last
                    mmg([k3f], reads=("qlmT", "kTb"), writes=(pk,))
                    bld.op("act", lambda e, half=half, hh=hh, pb=pb, P3=P3: e.activation(
                        P3[64 * hh:64 * (hh + 1), 1024 * half:1024 * (half + 1)],
                        pb[64 * hh:64 * (hh + 1), 0:1024], AF.Exp,
                        accum_out=s3sum[64 * hh:64 * (hh + 1), half:half + 1]),
                        reads=(pk,), writes=("P3", "s3sum"))
                adv()
            bld.op("dve", lambda e: e.tensor_tensor(s3tot[:], s3sum[:, 0:1], s3sum[:, 1:2], op=ALU.add),
                   reads=("s3sum",), writes=("s3tot",))
            bld.op("dve", lambda e: e.reciprocal(s3rec[:], s3tot[:]),
                   reads=("s3tot",), writes=("s3rec",))
            bld.op("dve", lambda e, P3=P3: e.tensor_scalar(P3[:], P3[:], s3rec[:], None, op0=ALU.mult),
                   reads=("P3", "s3rec"), writes=("P3",))
            adv()
            for j4 in range(4):
                def p3t(e, j4=j4, P3=P3):
                    last = None
                    for jj in range(4):
                        last = e.transpose(p6b[:, 128 * jj:128 * (jj + 1)],
                                           P3[:, 128 * (4 * j4 + jj):128 * (4 * j4 + jj + 1)], idb[:])
                    return last
                bld.op("pe", p3t, reads=("P3", "idb"), writes=("p6",))
                bld.op("act", lambda e, ptslot=ptslot: e.copy(ptslot[:], p6b[:, 0:512]),
                       reads=("p6",), writes=("ptslot",))

                def k3vf(e, j4=j4, ptslot=ptslot, vtokb=vtokb):
                    last = None
                    for jj in range(4):
                        j = 4 * j4 + jj
                        for hh in range(2):
                            last = e.matmul(p23[0:64, 64 * hh:64 * (hh + 1)],
                                            ptslot[:, 128 * jj + 64 * hh:128 * jj + 64 * (hh + 1)],
                                            vtokb[:, 128 * j + 64 * hh:128 * j + 64 * (hh + 1)],
                                            start=(j == 0 and hh == 0), stop=(j == 15 and hh == 1),
                                            tile_position=(0, 0))
                    return last
                mmg([k3vf], reads=("ptslot", "vtokb"), writes=("p23",))
                adv()
            bld.op("dve", lambda e: e.tensor_copy(k3vc[:], p23[0:64, 0:128]),
                   reads=("p23",), writes=("i_k3v",))
            adv(99)

        def stageC(hp):
            P1 = zb(A_P1, 2048)
            pair0(p4, 0, Vtc, k3vc, "i_Vt", "i_k3v", "p4")
            bld.op("dve", lambda e: e.tensor_copy(Zc[:], p4[0:64, 0:128]),
                   reads=("p4",), writes=("i_Z",))
            for j4 in range(4):
                def p1t(e, j4=j4, P1=P1):
                    last = None
                    for jj in range(4):
                        for hh in range(2):
                            last = e.transpose(
                                p6b[0:64, 256 * jj + 128 * hh:256 * jj + 128 * (hh + 1)],
                                P1[:, 128 * (4 * j4 + jj) + 64 * hh:128 * (4 * j4 + jj) + 64 * (hh + 1)],
                                idb[:], tile_position=(0, 0))
                    return last
                bld.op("pe", p1t, reads=("P1", "idb"), writes=("p6",))
                bld.op("act", lambda e: e.copy(ptsv[0:64, 0:1024], p6b[0:64, 0:1024]),
                       reads=("p6",), writes=("ptsv",))

                def svf(e, j4=j4):
                    last = None
                    for jj in range(4):
                        for hh in range(2):
                            last = e.matmul(p23[64 * hh:64 * (hh + 1), 512 * hh + 128 * jj:512 * hh + 128 * (jj + 1)],
                                            Zc[:, 64 * hh:64 * (hh + 1)],
                                            ptsv[0:64, 256 * jj + 128 * hh:256 * jj + 128 * (hh + 1)],
                                            start=True, stop=True, tile_position=(0, 64 * hh))
                    return last
                mmg([svf], reads=("i_Z", "ptsv"), writes=("p23",))
                for hh in range(2):
                    bld.op("dve", lambda e, j4=j4, hp=hp, hh=hh: e.tensor_copy(
                        svT[64 * hh:64 * (hh + 1), 2048 * hp + 512 * j4:2048 * hp + 512 * (j4 + 1)],
                        p23[64 * hh:64 * (hh + 1), 512 * hh:512 * hh + 512]),
                        reads=("p23",), writes=("svT",))

        if b_hp > 0:
            stageA(0)
            for hp in range(b_hp):
                stageB(hp)
                if hp + 1 < b_hp:
                    stageA(hp + 1)
                stageC(hp)

        if b_stage < 11:
            return
        A_wpb = 77824
        wpb_s = zb(A_wpb, 8192)
        bld.claim("wpb", ["qTb", "kTb", "P1", "P3"])
        bld.dma("Wg", lambda e: e.dma_start(
            out=zb(A_wpb, 8192), in_=_ap_custom(di["wpb"], [[1024, 128], [131072, 8], [1, 1024]], 0)),
            writes=("wpb",))
        for i in range(16):
            pb, pk = (p01, "p01") if i % 2 == 0 else (p23, "p23")

            def bpf(e, i=i, pb=pb):
                last = None
                for nb in range(2):
                    for c in range(8):
                        last = e.matmul(pb[:, 512 * nb:512 * (nb + 1)],
                                        svT[:, 2048 * c + 128 * i:2048 * c + 128 * (i + 1)],
                                        wpb_s[:, 1024 * c + 512 * nb:1024 * c + 512 * nb + 512],
                                        start=(c == 0), stop=(c == 7))
                return last
            mmg([bpf], reads=("svT", "wpb"), writes=(pk,))
            bld.op("dve", lambda e, i=i, pb=pb: e.tensor_tensor(
                r_s[:, 1024 * i:1024 * (i + 1)], pb[:, 0:1024],
                r_s[:, 1024 * i:1024 * (i + 1)], op=ALU.add),
                reads=(pk, "r%d" % i), writes=("r%d" % i,))

    def tap(name):
        if name in tap_d:
            bld.dma("St", lambda e: e.dma_start(
                out=_ap_custom(tap_d[name], [[1024, 128], [131072, 16], [1, 1024]], 0), in_=r_s[:]),
                reads=tuple("r%d" % i for i in range(16)))

    loads()
    if 'attn_a' in phases:
        attn_a()
        tap("attn_a")
    if 'ffn0' in phases:
        ffn(0)
        tap("ffn1")
    if 'attn_b' in phases:
        attn_b()
        tap("attn_b")
    if 'ffn1' in phases:
        ffn(1)
    bld.dma("St", lambda e: e.dma_start(
        out=_ap_custom(out_d, [[1024, 128], [131072, 16], [1, 1024]], 0), in_=r_s[:]),
        reads=tuple("r%d" % i for i in range(16)))
    bld.finish()
    return nc, stack


def _in_maps(inputs, n_cores=8):
    cons = _host_consts()
    w_attn = np.asarray(inputs["L0_attn_a_w_attn"], np.float32)
    wq = w_attn[0:C].reshape(C // 2, 2, C)[:, ::-1, :].reshape(C, C)
    waT = np.ascontiguousarray(np.concatenate([wq, w_attn[C:]], 0).T)
    b_attn = np.asarray(inputs["L0_attn_a_b_attn"], np.float32).copy()
    b_attn[0:C] = b_attn[0:C].reshape(C // 2, 2)[:, ::-1].reshape(C)
    base = {
        "waT": waT,
        "b_attn": b_attn,
        "wpaT": np.ascontiguousarray(np.asarray(inputs["L0_attn_a_w_proj"], np.float32).T),
        "b_proj": np.asarray(inputs["L0_attn_a_b_proj"], np.float32),
        "wassT": np.ascontiguousarray(np.asarray(inputs["L0_attn_a_w_assign"], np.float32).T),
        "wqkv": np.asarray(inputs["L0_attn_b_w_qkv"], np.float32),
        "wpb": np.asarray(inputs["L0_attn_b_w_proj"], np.float32),
        "rwT": np.ascontiguousarray(np.asarray(inputs["L0_ffn_router_w"], np.float32).T),
        "rb": np.asarray(inputs["L0_ffn_router_b"], np.float32),
        "weT": np.ascontiguousarray(np.asarray(inputs["L0_ffn_expert_w"], np.float32).transpose(0, 2, 1)),
        "eb": np.asarray(inputs["L0_ffn_expert_b"], np.float32),
    }
    base.update(cons)
    x = np.asarray(inputs["L0_attn_a_x"], np.float32)
    maps = []
    for cid in range(n_cores):
        m = dict(base)
        m["x"] = np.ascontiguousarray(x[cid])
        maps.append(m)
    return maps


def kernel(**inputs):
    nc, stack = build()
    try:
        maps = _in_maps(inputs)
        res = run_bass_kernel_spmd(nc, maps, core_ids=list(range(8)))
    finally:
        stack.__exit__(None, None, None)
    return np.stack([np.asarray(r["out"], np.float32) for r in res.results], axis=0)


# version 06 (final)
# speedup vs baseline: 1.4365x; 1.0484x over previous
import contextlib
import math

import numpy as np
import ml_dtypes
import bass_rust
import concourse.bass as bass
import concourse.mybir as mybir
from concourse.bass_utils import run_bass_kernel_spmd

F32 = mybir.dt.float32
BF16 = mybir.dt.bfloat16
AF = mybir.ActivationFunctionType
ALU = mybir.AluOpType
AX = mybir.AxisListType
bf16 = ml_dtypes.bfloat16

B, T, C = 8, 2048, 1024
NH, HD = 16, 64
NCL, GS = 64, 256
NTPC = T // NCL
G = T // GS
NCPG = GS // NTPC
CMAX = (G - 1) * NCPG
NLM = 64
NE = 8

NCHUNK = 16
NQ = 4


def _sinusoidal(n, d):
    pos = np.arange(n, dtype=np.float32)[:, None]
    div = np.exp((np.arange(0, d, 2, dtype=np.float32) * np.float32(-math.log(10000.0) / d)).astype(np.float32))
    ang = (pos * div).astype(np.float32)
    return np.stack([np.sin(ang), np.cos(ang)], axis=-1).reshape(n, d).astype(np.float32)


def _host_consts():
    S = _sinusoidal(CMAX + GS, HD)
    sinv = S[:, 0:32].T
    cosv = S[:, 32:64].T
    rotq = np.zeros((128, 512), np.float32)
    rotkl = np.zeros((128, 2048), np.float32)
    scale = np.float32(1.0 / math.sqrt(HD))
    for r in range(128):
        i = (r % 64) // 2
        tq = np.arange(512) % 256
        tkl = 8 * (np.arange(2048) // 256) + (np.arange(2048) % 256)
        if r % 2 == 0:
            rotq[r] = -cosv[i, tq] * scale
            rotkl[r] = -cosv[i, tkl]
        else:
            rotq[r] = sinv[i, tq] * scale
            rotkl[r] = sinv[i, tkl]
    tkc = np.zeros((64, 1024), np.float32)
    for c in range(CMAX):
        v = np.zeros(64, np.float32)
        v[0::2] = -cosv[:, c]
        v[1::2] = sinv[:, c]
        tkc[c] = np.tile(v, NH)
    tri = np.triu(np.full((128, 128), -1e30, np.float32), 1)
    perm = np.zeros((128, 128), np.float32)
    for k in range(128):
        perm[k, k ^ 1] = 1.0
    idn = np.eye(128, dtype=np.float32)
    blk = np.zeros((64, 128), np.float32)
    blk[np.arange(128) % 64, np.arange(128)] = 1.0
    return {
        "ROTQ": rotq.astype(bf16), "ROTKL": rotkl.astype(bf16),
        "TKC": tkc.astype(bf16), "TRI": tri.astype(bf16),
        "PERM": perm.astype(bf16), "IDB": idn.astype(bf16),
        "C7": (7.0 * blk).astype(bf16), "C15": (15.0 * blk).astype(bf16),
        "C13": (13.0 * blk).astype(bf16),
    }


ENG2ATTR = {"pe": "tensor", "dve": "vector", "act": "scalar", "pool": "gpsimd", "sp": "sync"}


class Bld:
    def __init__(self, nc, stack):
        self.nc = nc
        self.stk = stack
        self.q = {e: [] for e in ENG2ATTR}
        self.sem = {}
        self.cnt = {e: 0 for e in ENG2ATTR}
        self.seen = {e: {} for e in ENG2ATTR}
        self.bufs = {}
        self.ch = {}
        for e in ENG2ATTR:
            self.sem[e] = stack.enter_context(nc.semaphore("sem_" + e))
        for name, eng in [("Ld", "sp"), ("Tx", "sp"), ("St", "sp"),
                          ("Wg", "pool"), ("Sc", "pool")]:
            self.sem[name] = stack.enter_context(nc.semaphore("sem_" + name))
            self.ch[name] = {"eng": eng, "issued": 0, "fenced": 0, "maxwait": 0}

    def _collect(self, reads, writes):
        waits = {}

        def add(tok):
            if tok is None:
                return
            s, v = tok
            if waits.get(s, -1) < v:
                waits[s] = v

        for k in reads:
            b = self.bufs.get(k)
            if b:
                add(b["w"])
        for k in writes:
            b = self.bufs.get(k)
            if b:
                add(b["w"])
                for s, v in b["r"].items():
                    add((s, v))
        return waits

    def _wait_tok_for(self, s):
        if s in self.ch:
            c = self.ch[s]
            v = 16 * c["issued"]
            c["maxwait"] = max(c["maxwait"], v)
            return v
        return None

    def _emit_waits(self, eng, waits):
        out = []
        for s, v in waits.items():
            if s in self.ch:
                v2 = self._wait_tok_for(s)
                v = max(v, 0) if v2 is None else v2
            if self.seen[eng].get(s, -1) >= v:
                continue
            self.seen[eng][s] = v
            out.append((self.sem[s], v))
        return out

    def _track(self, reads, writes, tok):
        s, v = tok
        for k in writes:
            self.bufs[k] = {"w": (s, v), "r": {}}
        for k in reads:
            b = self.bufs.setdefault(k, {"w": None, "r": {}})
            if b["r"].get(s, -1) < v:
                b["r"][s] = v

    def op(self, eng, fn, reads=(), writes=()):
        waits = self._collect(reads, writes)
        wl = self._emit_waits(eng, waits)
        self.cnt[eng] += 1
        n = self.cnt[eng]
        sem = self.sem[eng]

        def run(e, wl=wl, fn=fn, sem=sem, n=n):
            for s, v in wl:
                e.wait_ge(s, v)
            inst = fn(e)
            inst.then_inc(sem, 1)
            return inst

        self.q[eng].append(run)
        self._track(reads, writes, (eng, n))

    def dma(self, chname, fn, reads=(), writes=()):
        c = self.ch[chname]
        eng = c["eng"]
        if c["maxwait"] > 16 * c["fenced"]:
            tgt = 16 * c["issued"]
            c["fenced"] = c["issued"]
            sem = self.sem[chname]

            def fence(e, sem=sem, tgt=tgt):
                e.wait_ge(sem, tgt)
            self.q[eng].append(lambda e, f=fence: f(e) or _NOINST)
            self.seen[eng][chname] = tgt
        waits = self._collect(reads, writes)
        wl = self._emit_waits(eng, waits)
        c["issued"] += 1
        v = 16 * c["issued"]
        sem = self.sem[chname]

        def run(e, wl=wl, fn=fn, sem=sem):
            for s, vv in wl:
                e.wait_ge(s, vv)
            inst = fn(e)
            inst.then_inc(sem, 16)
            return inst

        self.q[eng].append(run)
        self._track(reads, writes, (chname, v))

    def claim(self, newkey, oldkeys):
        b = self.bufs.setdefault(newkey, {"w": None, "r": {}})
        for k in oldkeys:
            if k == newkey:
                continue
            ob = self.bufs.get(k)
            if not ob:
                continue
            for tok in ([ob["w"]] if ob["w"] else []) + list(ob["r"].items()):
                s_, v = tok
                if b["r"].get(s_, -1) < v:
                    b["r"][s_] = v

    def finish(self):
        for name, c in self.ch.items():
            if c["issued"] > 0:
                tgt = 16 * c["issued"]
                sem = self.sem[name]
                self.q["sp"].append(lambda e, s=sem, t=tgt: e.wait_ge(s, t) or _NOINST)
        blk = self.stk.enter_context(self.nc.Block())
        for ename, attr in ENG2ATTR.items():
            ops = self.q[ename]
            getattr(blk, attr)(lambda eng, ops=ops: [f(eng) for f in ops] and None)


class _NoInst:
    def then_inc(self, *a):
        return self


_NOINST = _NoInst()


def _ap_custom(base_ap, dims, offset):
    ap = base_ap.copy()
    ap.ap = bass_rust.VecI64Pair(dims)
    ap.offset = base_ap.offset + offset
    return ap


def build(taps=(), dbg=False, phases=('attn_a', 'ffn0', 'attn_b', 'ffn1'), b_hp=8, b_stage=99):
    nc = bass.Bass("TRN2", target_bir_lowering=False)
    stack = contextlib.ExitStack()
    stack.__enter__()
    bld = Bld(nc, stack)
    sb = lambda name, shape, dt: stack.enter_context(nc.sbuf_tensor(name, shape, dt))
    ps = lambda name, shape, dt: stack.enter_context(nc.psum_tensor(name, shape, dt))

    di = {}
    ins = [
        ("x", [T, C]), ("waT", [C, 3 * C]), ("b_attn", [3 * C]),
        ("wpaT", [C, C]), ("b_proj", [C]), ("wassT", [C, NCL]),
        ("wqkv", [C, 3 * C]), ("wpb", [C, C]),
        ("rwT", [C, NE]), ("rb", [NE]), ("weT", [NE, C, C]), ("eb", [NE, C]),
    ]
    bf_ins = [("ROTQ", [128, 512]), ("ROTKL", [128, 2048]), ("TKC", [64, 1024]),
              ("TRI", [128, 128]), ("PERM", [128, 128]), ("IDB", [128, 128]),
              ("C7", [64, 128]), ("C15", [64, 128]), ("C13", [64, 128])]
    for n, shape in ins:
        di[n] = nc.dram_tensor(n, shape, F32, kind="ExternalInput").ap()
    for n, shape in bf_ins:
        di[n] = nc.dram_tensor(n, shape, BF16, kind="ExternalInput").ap()
    out_d = nc.dram_tensor("out", [T, C], F32, kind="ExternalOutput").ap()
    tap_d = {}
    for tname in taps:
        tap_d[tname] = nc.dram_tensor("tap_" + tname, [T, C], F32, kind="ExternalOutput").ap()
    dbg_d = {}
    if dbg:
        for n, shape in [("logit", [128, 128]), ("sel", [128, 128]), ("lmax", [128, 16]),
                         ("w16d", [128, 16]), ("xnTd", [128, 16384]), ("y0", [128, 1024])]:
            dbg_d[n] = nc.dram_tensor("dbg_" + n, shape, F32, kind="ExternalOutput").ap()
    lt_dram = nc.dram_tensor("lt_dram", [16, 512], F32, kind="Internal").ap()
    wbig_dram = nc.dram_tensor("wbig_dram", [T, NCL], BF16, kind="Internal").ap()

    r_s = sb("r_s", [128, 16 * 1024], F32)
    rotq = sb("rotq", [128, 512], BF16)
    rotkl = sb("rotkl", [128, 2048], BF16)
    tkc = sb("tkc_s", [64, 1024], BF16)
    tri = sb("tri_s", [128, 128], BF16)
    perm = sb("perm_s", [128, 128], BF16)
    idb = sb("idb_s", [128, 128], BF16)
    c7 = sb("c7_s", [64, 128], BF16)
    c15 = sb("c15_s", [64, 128], BF16)
    c13 = sb("c13_s", [64, 128], BF16)
    onesb = sb("onesb", [1, 128], BF16)
    ones1f = sb("ones1f", [1, 128], F32)
    onesc = sb("onesc", [128, 1], BF16)
    eps_t = sb("eps_t", [128, 1], F32)
    qsc_t = sb("qsc_t", [128, 1], F32)
    sum16 = sb("sum16", [128, 16], F32)
    ssq16 = sb("ssq16", [128, 16], F32)
    mean16 = sb("mean16", [128, 16], F32)
    var16 = sb("var16", [128, 16], F32)
    rstd16 = sb("rstd16", [128, 16], F32)
    tmp16 = sb("tmp16", [128, 16], F32)
    xn_tok = sb("xn_tok", [128, 1024], BF16)
    battn = sb("battn", [128, 24], F32)
    bprow = sb("bprow", [1, 1024], BF16)
    ebrow = sb("ebrow", [1, 2048], BF16)
    rbrow = sb("rbrow", [1, 8], BF16)
    wass = sb("wass", [128, 8 * 64], BF16)
    rwt = sb("rwt", [128, 64], BF16)
    logit_s = sb("logit_s", [128, 128], F32)
    er_s = sb("er_s", [128, 128], F32)
    lmax16 = sb("lmax16", [128, 16], F32)
    esum16 = sb("esum16", [128, 16], F32)
    elmax16 = sb("elmax16", [128, 16], F32)
    w16 = sb("w16", [128, 16], F32)
    sel_s = sb("sel_s", [128, 128], F32)
    eblk = sb("eblk", [16, 32], F32)
    wsum_c = sb("wsum_c", [16, 1], F32)
    wrec_c = sb("wrec_c", [16, 1], F32)
    wblk_bf = sb("wblk_bf", [16, 32], BF16)
    wbig_s = sb("wbig_s", [128, 4 * 64], BF16)
    kcr_cm = sb("kcr_cm", [64, 1024], BF16)
    cv_bf = sb("cv_bf", [64, 1024], BF16)
    kcrT = sb("kcrT", [128, 8 * 56], BF16)
    p_sm_b = [sb("p_sm0", [128, 512], BF16), sb("p_sm1", [128, 512], BF16)]
    pt_sm_b = [sb("pt_sm0", [128, 384], BF16), sb("pt_sm1", [128, 384], BF16)]
    ssum_b = [sb("ssum0", [128, 1], F32), sb("ssum1", [128, 1], F32)]
    srec_b = [sb("srec0", [128, 1], F32), sb("srec1", [128, 1], F32)]
    zero64 = sb("zero64", [128, 64], BF16)
    qlmT = sb("qlmT", [128, 64], BF16)
    klmT = sb("klmT", [128, 64], BF16)
    lm_f = sb("lm_f", [128, 64], F32)
    rsum32 = sb("rsum32", [128, 32], F32)
    rrec32 = sb("rrec32", [128, 32], F32)
    s3sum = sb("s3sum", [128, 2], F32)
    s3tot = sb("s3tot", [128, 1], F32)
    s3rec = sb("s3rec", [128, 1], F32)
    k2sum = sb("k2sum", [64, 2], F32)
    k2rec = sb("k2rec", [64, 2], F32)
    denom_s = sb("denom_s", [1, 128], F32)
    dmax_s = sb("dmax_s", [1, 2], F32)
    drec_s = sb("drec_s", [1, 2], F32)
    dbc_s = sb("dbc_s", [128, 2], F32)
    sb_ptsv = sb("ptsv", [128, 1024], BF16)
    K2c = sb("K2c", [64, 128], BF16)
    K2Tc = sb("K2Tc", [64, 128], BF16)
    Ac = sb("Ac", [64, 128], BF16)
    Vc = sb("Vc", [64, 128], BF16)
    Vtc = sb("Vtc", [64, 128], BF16)
    U1c = sb("U1c", [64, 128], BF16)
    U2c = sb("U2c", [64, 128], BF16)
    U3c = sb("U3c", [64, 128], BF16)
    U4c = sb("U4c", [64, 128], BF16)
    U4Tc = sb("U4Tc", [64, 128], BF16)
    U5c = sb("U5c", [64, 128], BF16)
    k3vc = sb("k3vc", [64, 128], BF16)
    Zc = sb("Zc", [64, 128], BF16)
    c7b2, c15b2, c13b2 = c7, c15, c13
    ZB = sb("ZB", [128, 50176], BF16)
    ZF = sb("ZF", [128, 2048], F32)
    ck_s = ZF[0:64, 0:1024]
    cv_s = ZF[0:64, 1024:2048]
    lt_s = ZB[:, 45056:46080].bitcast(F32)[0:16, :]

    p01 = ps("p01", [128, 1024], F32)
    p23 = ps("p23", [128, 1024], F32)
    p4 = ps("p4", [128, 512], F32)
    p5 = ps("p5", [128, 512], F32)
    p6 = ps("p6", [128, 512], F32)
    p7 = ps("p7", [128, 512], F32)
    p6b = p6.bitcast(BF16)
    p7b = p7.bitcast(BF16)
    p01b = p01.bitcast(BF16)
    PK = {id(p01): "p01", id(p23): "p23", id(p4): "p4", id(p5): "p5",
          id(p6): "p6", id(p7): "p7"}

    assert nc.sbuf_bytes_remaining >= 0, nc.sbuf_bytes_remaining

    def zb(off_b, cols):
        c0 = off_b // 2
        return ZB[:, c0:c0 + cols]

    def zf(off_b, cols):
        c0 = off_b // 4
        return ZF[:, c0:c0 + cols]

    def memset(tile, val, key):
        bld.op("dve", lambda e, t=tile, v=val: e.memset(t, v), writes=(key,))

    def ln_stats(i):
        bld.op("dve", lambda e, i=i: e.reduce_sum(
            sum16[:, i:i + 1], r_s[:, 1024 * i:1024 * (i + 1)], axis=AX.X),
            reads=("r%d" % i,), writes=("sum16",))
        bld.op("act", lambda e, i=i: e.activation(
            sb_ptsv[:, 0:1024], r_s[:, 1024 * i:1024 * (i + 1)], AF.Square,
            accum_out=ssq16[:, i:i + 1]),
            reads=("r%d" % i,), writes=("ptsv", "ssq16"))

    def ln_stats_gen(chunks):
        for i in chunks:
            ln_stats(i)
            yield

    def ln_pass(chunks, dst_fn, dst_keys, src_key_fn, prestats=()):
        for i in chunks:
            if i not in prestats:
                ln_stats(i)
        i0, i1 = chunks[0], chunks[-1] + 1
        sl = slice(i0, i1)
        bld.op("dve", lambda e: e.tensor_scalar(mean16[:, sl], sum16[:, sl], 1.0 / C, None, op0=ALU.mult),
               reads=("sum16",), writes=("mean16",))
        bld.op("dve", lambda e: e.tensor_tensor(tmp16[:, sl], mean16[:, sl], mean16[:, sl], op=ALU.mult),
               reads=("mean16",), writes=("tmp16",))
        bld.op("dve", lambda e: e.tensor_scalar(var16[:, sl], ssq16[:, sl], 1.0 / C, None, op0=ALU.mult),
               reads=("ssq16",), writes=("var16",))
        bld.op("dve", lambda e: e.tensor_tensor(var16[:, sl], var16[:, sl], tmp16[:, sl], op=ALU.subtract),
               reads=("tmp16", "var16"), writes=("var16",))
        bld.op("act", lambda e: e.activation(rstd16[:, sl], var16[:, sl], AF.Sqrt, bias=eps_t[:]),
               reads=("var16", "eps"), writes=("rstd16",))
        bld.op("dve", lambda e: e.reciprocal(rstd16[:, sl], rstd16[:, sl]),
               reads=("rstd16",), writes=("rstd16",))
        for i in chunks:
            bld.op("dve", lambda e, i=i: e.tensor_scalar(
                xn_tok[:], r_s[:, 1024 * i:1024 * (i + 1)],
                mean16[:, i:i + 1], rstd16[:, i:i + 1], op0=ALU.subtract, op1=ALU.mult),
                reads=(src_key_fn(i), "mean16", "rstd16"), writes=("xn_tok",))

            def trs(e, i=i):
                last = None
                for b in range(8):
                    last = e.transpose(p6b[:, 128 * b:128 * (b + 1)],
                                       xn_tok[:, 128 * b:128 * (b + 1)], idb[:])
                return last
            bld.op("pe", trs, reads=("xn_tok", "idb"), writes=("p6",))
            dst = dst_fn(i)
            bld.op("act", lambda e, dst=dst: e.copy(
                dst, p6b[:, 0:1024].rearrange("p (c t) -> p c t", t=128)),
                reads=("p6",), writes=dst_keys)

    def mmg(fns, reads, writes):
        def run(e, fns=fns):
            last = None
            for f in fns:
                last = f(e)
            return last
        bld.op("pe", run, reads=reads, writes=writes)

    pre_ffn0, pre_attnb, pre_ffn1 = set(), set(), set()

    def loads():
        bld.dma("Ld", lambda e: e.dma_start(
            out=r_s[:], in_=_ap_custom(di["x"], [[1024, 128], [131072, 16], [1, 1024]], 0)),
            writes=tuple(f"r{i}" for i in range(16)))
        for n, t, key in [("ROTQ", rotq, "rotq"), ("ROTKL", rotkl, "rotkl"),
                          ("TKC", tkc, "tkc"), ("TRI", tri, "tri")]:
            bld.dma("Ld", lambda e, n=n, t=t: e.dma_start(out=t[:], in_=di[n]), writes=(key,))
        for n, t, key in [("PERM", perm, "perm"), ("IDB", idb, "idb"),
                          ("C7", c7, "c7"), ("C15", c15, "c15"), ("C13", c13, "c13")]:
            bld.dma("Ld", lambda e, n=n, t=t: e.dma_start(out=t[:], in_=di[n]), writes=(key,))
        bld.dma("Wg", lambda e: e.dma_start(
            out=battn[:], in_=_ap_custom(di["b_attn"], [[1, 128], [128, 24]], 0),
            allow_slow_non_contiguous=True), writes=("battn",))
        bld.dma("Wg", lambda e: e.dma_start(out=bprow[:], in_=di["b_proj"].rearrange("(o d) -> o d", o=1)),
                writes=("bprow",))
        bld.dma("Wg", lambda e: e.dma_start(out=rbrow[:], in_=di["rb"].rearrange("(o d) -> o d", o=1)),
                writes=("rbrow",))
        bld.dma("Wg", lambda e: e.dma_start(
            out=wass[:], in_=_ap_custom(di["wassT"], [[64, 128], [8192, 8], [1, 64]], 0)), writes=("wass",))
        bld.dma("Wg", lambda e: e.dma_start(
            out=rwt[:], in_=_ap_custom(di["rwT"], [[8, 128], [1024, 8], [1, 8]], 0)), writes=("rwt",))
        memset(onesb[:], 1.0, "onesb")
        memset(ones1f[:], 1.0, "ones1f")
        memset(onesc[:], 1.0, "onesc")
        memset(eps_t[:], 1e-5, "eps")
        memset(qsc_t[:], 0.125, "qsc")
        memset(zero64[:], 0.0, "zero64")
        memset(ck_s[:], 0.0, "ck_s")
        memset(cv_s[:], 0.0, "cv_s")
        for i in range(16):
            bld.dma("Sc", lambda e, i=i: e.dma_start(
                out=wbig_dram[128 * i:128 * (i + 1), :], in_=zero64[:]),
                reads=("zero64",), writes=("wbig",))

    ARENA_A = ["waT0", "waT1", "xnTq", "qrT", "klrT", "ktok", "vtok", "lt_s",
               "kslot0", "kslot1", "vslot_a0", "vslot_a1", "yT", "wpT"]

    def attn_a():
        A_waT = 0
        A_xnTq = 49152
        A_qrT = 57344
        A_klrT = 65536
        A_ktok = 73728
        A_vtok = 81920
        A_kslot = [90112, 92160]
        A_vslot = [91136, 93184]
        A_yT = A_xnTq
        A_wpT = A_klrT
        waT_s = zb(A_waT, 8 * 3072)
        for half in range(2):
            bld.dma("Wg", lambda e, h=half: e.dma_start(
                out=zb(A_waT, 8 * 3072).rearrange("p (c f) -> p c f", f=3072)[:, :, 1536 * h:1536 * (h + 1)],
                in_=_ap_custom(di["waT"], [[3072, 128], [393216, 8], [1, 1536]], 1536 * h)),
                writes=("waT%d" % half,))

        for Q in range(NQ):
            xnTq = zb(A_xnTq, 4096)
            qrT = zb(A_qrT, 4096)
            klrT = zb(A_klrT, 4096)
            ktok = zb(A_ktok, 4096)
            vtok = zb(A_vtok, 4096)

            yT = zb(A_yT, 4096)

            bld.claim("xnTq", ["yT"])
            bld.claim("klrT", ["wpT"])
            bld.claim("ktok", ["wpT"])


            def xnq_dst(i, Q=Q):
                ii = i - 4 * Q
                return zb(A_xnTq, 4096).rearrange("p (c t) -> p c t", c=8)[:, :, 128 * ii:128 * (ii + 1)]
            ln_pass(list(range(4 * Q, 4 * Q + 4)), xnq_dst, ("xnTq",),
                    lambda i: "r%d" % i, prestats=(set() if Q == 0 else set(range(16))))

            bld.claim("lt_s", ["kslot0", "vslot_a0"])
            mmg([lambda e, Q=Q: [e.matmul(p7[0:16, 0:512],
                                          wass[:, 64 * c + 16 * Q:64 * c + 16 * Q + 16],
                                          xnTq[:, 512 * c:512 * (c + 1)],
                                          start=(c == 0), stop=(c == 7)) for c in range(8)][-1]],
                reads=("wass", "xnTq"), writes=("p7",))
            bld.op("dve", lambda e: e.tensor_copy(lt_s[:], p7[0:16, 0:512]),
                   reads=("p7",), writes=("lt_s",))
            bld.dma("Sc", lambda e: e.dma_start(out=lt_dram, in_=lt_s[:]),
                    reads=("lt_s",), writes=("lt_dram",))
            diag = _ap_custom(lt_dram, [[544, 16], [1, 32]], 0)
            bld.dma("Sc", lambda e, diag=diag: e.dma_start(out=eblk[:], in_=diag),
                    reads=("lt_dram",), writes=("eblk",))
            bld.op("act", lambda e: e.activation(eblk[:], eblk[:], AF.Exp, accum_out=wsum_c[:]),
                   reads=("eblk",), writes=("eblk", "wsum_c"))
            bld.op("dve", lambda e: e.reciprocal(wrec_c[:], wsum_c[:]),
                   reads=("wsum_c",), writes=("wrec_c",))
            bld.op("dve", lambda e: e.tensor_scalar(wblk_bf[:], eblk[:], wrec_c[:], None, op0=ALU.mult),
                   reads=("eblk", "wrec_c"), writes=("wblk_bf",))
            scat = _ap_custom(wbig_dram, [[2049, 16], [64, 32]], 32768 * Q + 16 * Q)
            bld.dma("Sc", lambda e, scat=scat: e.dma_start(
                out=scat, in_=wblk_bf[:], allow_slow_non_contiguous=True),
                reads=("wblk_bf",), writes=("wbig",))
            bld.dma("Sc", lambda e, Q=Q: e.dma_start(
                out=wbig_s[:], in_=_ap_custom(wbig_dram, [[64, 128], [8192, 4], [1, 64]], 32768 * Q)),
                reads=("wbig",), writes=("wbig_s",))

            bld.claim("kslot0", ["lt_s"])
            bld.claim("vslot_a0", ["lt_s"])
            pending = []
            for F in range(24):
                kind, co = F // 8, F % 8
                pb, pk = (p4, "p4") if F % 2 == 0 else (p5, "p5")

                def qf(e, F=F, pb=pb):
                    last = None
                    for c in range(8):
                        last = e.matmul(pb[:, 0:512],
                                        waT_s[:, 3072 * c + 128 * F:3072 * c + 128 * (F + 1)],
                                        xnTq[:, 512 * c:512 * (c + 1)],
                                        start=(c == 0), stop=(c == 7))
                    return last
                mmg([qf], reads=("waT0", "waT1", "xnTq"), writes=(pk,))
                if kind == 0:
                    bld.op("dve", lambda e, co=co, F=F, pb=pb: e.scalar_tensor_tensor(
                        qrT[:, 512 * co:512 * (co + 1)], pb[:, 0:512],
                        battn[:, F:F + 1], rotq[:], op0=ALU.add, op1=ALU.mult),
                        reads=(pk, "battn", "rotq"), writes=("qrT",))
                    continue
                par = co % 2
                slot = zb((A_kslot if kind == 1 else A_vslot)[par], 512)
                slotk = ("kslot%d" if kind == 1 else "vslot_a%d") % par
                bld.op("act", lambda e, F=F, pb=pb, slot=slot: e.activation(
                    slot[:], pb[:, 0:512], AF.Identity, bias=battn[:, F:F + 1]),
                    reads=(pk, "battn"), writes=(slotk,))

                def derived(kind=kind, co=co, slot=slot, slotk=slotk, F=F):
                    if kind == 1:
                        ppb, ppk = (p5, "p5") if F % 2 == 0 else (p4, "p4")
                        mmg([lambda e, ppb=ppb, slot=slot: e.matmul(
                            ppb[:, 0:512], perm[:], slot[:], start=True, stop=True)],
                            reads=(slotk, "perm"), writes=(ppk,))
                        bld.op("dve", lambda e, co=co, ppb=ppb, Q=Q: e.tensor_tensor(
                            klrT[:, 512 * co:512 * (co + 1)], ppb[:, 0:512],
                            rotkl[:, 512 * Q:512 * (Q + 1)], op=ALU.mult),
                            reads=(ppk, "rotkl"), writes=("klrT",))
                    tokd, tokk = (ktok, "ktok") if kind == 1 else (vtok, "vtok")

                    def ttr(e, slot=slot):
                        last = None
                        for jj in range(4):
                            last = e.transpose(p6b[:, 128 * jj:128 * (jj + 1)],
                                               slot[:, 128 * jj:128 * (jj + 1)], idb[:])
                        return last
                    bld.op("pe", ttr, reads=(slotk, "idb"), writes=("p6",))
                    dst = tokd.rearrange("p (j d) -> p j d", j=4)[:, :, 128 * co:128 * (co + 1)]
                    bld.op("act", lambda e, dst=dst: e.copy(
                        dst, p6b[:, 0:512].rearrange("p (j d) -> p j d", j=4)),
                        reads=("p6",), writes=(tokk,))
                pending.append(derived)
                if len(pending) > 1:
                    pending.pop(0)()
            for fn in pending:
                fn()

            for (tok, tk, pb, pk, dst, dstk) in [(ktok, "ktok", p01, "p01", ck_s, "ck_s"),
                                                 (vtok, "vtok", p23, "p23", cv_s, "cv_s")]:
                def sf(e, tok=tok, pb=pb):
                    last = None
                    for j in range(4):
                        for nb in range(2):
                            last = e.matmul(pb[0:64, 512 * nb:512 * (nb + 1)],
                                            wbig_s[:, 64 * j:64 * (j + 1)],
                                            tok[:, 1024 * j + 512 * nb:1024 * j + 512 * (nb + 1)],
                                            start=(j == 0), stop=(j == 3))
                    return last
                mmg([sf], reads=("wbig_s", tk), writes=(pk,))
                bld.op("dve", lambda e, pb=pb, dst=dst: e.tensor_tensor(
                    dst[:], pb[0:64, 0:1024], dst[:], op=ALU.add),
                    reads=(pk, dstk), writes=(dstk,))

            evn = lambda ap: ap.rearrange("p (a two) -> p two a", two=2)[:, 0:1, :]
            odd = lambda ap: ap.rearrange("p (a two) -> p two a", two=2)[:, 1:2, :]
            bld.op("dve", lambda e: e.tensor_tensor(evn(kcr_cm[:]), odd(ck_s[:]), evn(tkc[:]), op=ALU.mult),
                   reads=("ck_s", "tkc"), writes=("kcr_cm",))
            bld.op("dve", lambda e: e.tensor_tensor(odd(kcr_cm[:]), evn(ck_s[:]), odd(tkc[:]), op=ALU.mult),
                   reads=("ck_s", "tkc"), writes=("kcr_cm",))
            bld.op("dve", lambda e: e.tensor_copy(cv_bf[:], cv_s[:]),
                   reads=("cv_s",), writes=("cv_bf",))

            def ktrs(e):
                last = None
                for h in range(16):
                    hb = 64 * (h % 2)
                    last = e.transpose(
                        p6b[hb:hb + 64, 56 * (h // 2) + 448 * (h % 2):56 * (h // 2) + 448 * (h % 2) + 56],
                        kcr_cm[0:56, 64 * h:64 * (h + 1)],
                        idb[0:56, 0:56], tile_position=(0, hb))
                return last
            bld.op("pe", ktrs, reads=("kcr_cm", "idb"), writes=("p6",))
            bld.op("act", lambda e: e.copy(kcrT[0:64, 0:448], p6b[0:64, 0:448]),
                   reads=("p6",), writes=("kcrT",))
            bld.op("act", lambda e: e.copy(kcrT[64:128, 0:448], p6b[64:128, 448:896]),
                   reads=("p6",), writes=("kcrT",))

            bld.claim("yT", ["xnTq"])
            iters = [(gl, hp, hh, qc) for gl in range(2) for hp in range(8)
                     for hh in range(2) for qc in range(2)]

            def front(i):
                gl, hp, hh, qc = iters[i]
                ncl_g = NCPG * (2 * Q + gl)
                hb = 64 * hh
                pp = i % 2
                qend = 128 * (qc + 1)
                ncols = ncl_g + qend
                spb, spk = (p4, "p4") if pp == 0 else (p5, "p5")
                p_sm, ssum1, srec1 = p_sm_b[pp], ssum_b[pp], srec_b[pp]
                psk, ssk, srk = "p_sm%d" % pp, "ssum%d" % pp, "srec%d" % pp

                def sc_f(e, hp=hp, hb=hb, gl=gl, qc=qc, ncl_g=ncl_g, qend=qend, spb=spb):
                    last = None
                    if ncl_g > 0:
                        last = e.matmul(
                            spb[:, 0:ncl_g],
                            qrT[hb:hb + 64, 512 * hp + 256 * gl + 128 * qc:512 * hp + 256 * gl + 128 * (qc + 1)],
                            kcrT[hb:hb + 64, 56 * hp:56 * hp + ncl_g],
                            start=True, stop=True, tile_position=(hb, 0))
                    last = e.matmul(
                        spb[:, ncl_g:ncl_g + qend],
                        qrT[hb:hb + 64, 512 * hp + 256 * gl + 128 * qc:512 * hp + 256 * gl + 128 * (qc + 1)],
                        klrT[hb:hb + 64, 512 * hp + 256 * gl:512 * hp + 256 * gl + qend],
                        start=True, stop=True, tile_position=(hb, 0))
                    return last
                mmg([sc_f], reads=("qrT", "kcrT", "klrT"), writes=(spk,))
                bld.op("dve", lambda e, spb=spb, ncl_g=ncl_g, qc=qc: e.tensor_tensor(
                    spb[:, ncl_g + 128 * qc:ncl_g + 128 * (qc + 1)],
                    spb[:, ncl_g + 128 * qc:ncl_g + 128 * (qc + 1)],
                    tri[:], op=ALU.add),
                    reads=(spk, "tri"), writes=(spk,))
                bld.op("act", lambda e, spb=spb, ncols=ncols, p_sm=p_sm, ssum1=ssum1: e.activation(
                    p_sm[:, 0:ncols], spb[:, 0:ncols], AF.Exp, accum_out=ssum1[:]),
                    reads=(spk,), writes=(psk, ssk))
                bld.op("dve", lambda e, srec1=srec1, ssum1=ssum1: e.reciprocal(srec1[:], ssum1[:]),
                       reads=(ssk,), writes=(srk,))
                bld.op("dve", lambda e, ncols=ncols, p_sm=p_sm, srec1=srec1: e.tensor_scalar(
                    p_sm[:, 0:ncols], p_sm[:, 0:ncols], srec1[:], None, op0=ALU.mult),
                    reads=(psk, srk), writes=(psk,))

            def back(i):
                gl, hp, hh, qc = iters[i]
                ncl_g = NCPG * (2 * Q + gl)
                h = 2 * hp + hh
                hb = 64 * hh
                pp = i % 2
                qend = 128 * (qc + 1)
                yc0 = 256 * (hp % 2)
                p_sm, pt_sm = p_sm_b[pp], pt_sm_b[pp]
                psk, ptk = "p_sm%d" % pp, "pt_sm%d" % pp
                ptp, ptpk = (p6b, "p6") if pp == 0 else (p01b, "p01")

                def pt_f(e, ncl_g=ncl_g, qc=qc, p_sm=p_sm, ptp=ptp):
                    last = None
                    if ncl_g > 0:
                        last = e.transpose(ptp[0:ncl_g, 0:128], p_sm[:, 0:ncl_g], idb[:])
                    for kc in range(qc + 1):
                        last = e.transpose(ptp[:, 128 * (kc + 1):128 * (kc + 2)],
                                           p_sm[:, ncl_g + 128 * kc:ncl_g + 128 * (kc + 1)],
                                           idb[:])
                    return last
                bld.op("pe", pt_f, reads=(psk, "idb"), writes=(ptpk,))
                if ncl_g > 0:
                    bld.op("act", lambda e, ncl_g=ncl_g, pt_sm=pt_sm, ptp=ptp: e.copy(
                        pt_sm[0:ncl_g, 0:128], ptp[0:ncl_g, 0:128]),
                        reads=(ptpk,), writes=(ptk,))
                bld.op("act", lambda e, qc=qc, pt_sm=pt_sm, ptp=ptp: e.copy(
                    pt_sm[:, 128:128 * (qc + 2)], ptp[:, 128:128 * (qc + 2)]),
                    reads=(ptpk,), writes=(ptk,))

                def pv_f(e, h=h, hb=hb, gl=gl, qc=qc, ncl_g=ncl_g, yc0=yc0, pt_sm=pt_sm):
                    last = None
                    first = True
                    yc = yc0 + 128 * qc
                    if ncl_g > 0:
                        last = e.matmul(p7[hb:hb + 64, yc:yc + 128],
                                        cv_bf[0:ncl_g, 64 * h:64 * (h + 1)],
                                        pt_sm[0:ncl_g, 0:128],
                                        start=True, stop=False, tile_position=(0, hb))
                        first = False
                    for kc in range(qc + 1):
                        j = 2 * gl + kc
                        last = e.matmul(p7[hb:hb + 64, yc:yc + 128],
                                        vtok[:, 1024 * j + 64 * h:1024 * j + 64 * (h + 1)],
                                        pt_sm[:, 128 * (kc + 1):128 * (kc + 2)],
                                        start=first, stop=(kc == qc),
                                        tile_position=(0, hb))
                        first = False
                    return last
                mmg([pv_f], reads=(ptk, "cv_bf", "vtok"), writes=("p7",))
                if hh == 1 and qc == 1:
                    bld.op("dve", lambda e, hp=hp, gl=gl, yc0=yc0: e.tensor_copy(
                        yT[:, 512 * hp + 256 * gl:512 * hp + 256 * (gl + 1)], p7[:, yc0:yc0 + 256]),
                        reads=("p7",), writes=("yT",))

            if Q < 3:
                fill = ln_stats_gen(list(range(4 * (Q + 1), 4 * (Q + 2))))
            else:
                fill = ln_stats_gen(list(range(12)))
                pre_ffn0.update(range(12))
            front(0)
            for i in range(1, len(iters)):
                front(i)
                back(i - 1)
                if i % 4 == 0:
                    next(fill, None)
            back(len(iters) - 1)
            for _ in range(8):
                next(fill, None)

            bld.claim("wpT", ["klrT", "ktok"])
            wpT_s = zb(A_wpT, 8192)
            bld.dma("Wg", lambda e: e.dma_start(
                out=zb(A_wpT, 8192), in_=_ap_custom(di["wpaT"], [[1024, 128], [131072, 8], [1, 1024]], 0)),
                writes=("wpT",))
            for qb in range(4):
                i = 4 * Q + qb

                def opf(e, qb=qb, wpT_s=wpT_s):
                    last = None
                    for nb in range(2):
                        for c in range(8):
                            last = e.matmul(p01[:, 512 * nb:512 * (nb + 1)],
                                            yT[:, 512 * c + 128 * qb:512 * c + 128 * (qb + 1)],
                                            wpT_s[:, 1024 * c + 512 * nb:1024 * c + 512 * nb + 512],
                                            start=(c == 0), stop=False)
                        last = e.matmul(p01[:, 512 * nb:512 * (nb + 1)], onesb[:],
                                        bprow[0:1, 512 * nb:512 * (nb + 1)],
                                        start=False, stop=True)
                    return last
                mmg([opf], reads=("yT", "wpT", "onesb", "bprow"), writes=("p01",))
                bld.op("dve", lambda e, i=i: e.tensor_tensor(
                    r_s[:, 1024 * i:1024 * (i + 1)], p01[:, 0:1024],
                    r_s[:, 1024 * i:1024 * (i + 1)], op=ALU.add),
                    reads=("p01", "r%d" % i), writes=("r%d" % i,))

    ARENA_F = ["weT0", "weT1", "xnT"]
    ARENA_B = ["svT", "wq0", "wq1", "qTb", "kTb", "P1", "P3", "vtokb",
               "vslot_b", "ptslot", "wpb", "xnT"]

    def ffn(p):
        A_weT = [0, 16384]
        A_xnT = 32768
        prev = ARENA_A if p == 0 else ARENA_B
        for k in ARENA_F:
            bld.claim(k, prev)
        xnT = zb(A_xnT, 16384)

        def xn_dst(i):
            return zb(A_xnT, 16384).rearrange("p (c t) -> p c t", c=8)[:, :, 128 * i:128 * (i + 1)]
        ln_pass(list(range(16)), xn_dst, ("xnT",), lambda i: "r%d" % i,
                prestats=(pre_ffn0 if p == 0 else pre_ffn1))

        def rf(e):
            last = None
            for i in range(16):
                for c in range(8):
                    last = e.matmul(p4[:, 8 * i:8 * (i + 1)],
                                    xnT[:, 2048 * c + 128 * i:2048 * c + 128 * (i + 1)],
                                    rwt[:, 8 * c:8 * (c + 1)], start=(c == 0), stop=False)
                last = e.matmul(p4[:, 8 * i:8 * (i + 1)], onesb[:], rbrow[:],
                                start=False, stop=True)
            return last
        mmg([rf], reads=("xnT", "rwt", "onesb", "rbrow"), writes=("p4",))
        bld.op("dve", lambda e: e.tensor_copy(logit_s[:], p4[:, 0:128]),
               reads=("p4",), writes=("logit_s",))
        bld.op("dve", lambda e: e.tensor_reduce(
            lmax16[:].rearrange("p (a b) -> p a b", b=1),
            logit_s[:].rearrange("p (a b) -> p a b", b=8),
            axis=AX.X, op=ALU.max, opt_input=False, opt_output=False),
            reads=("logit_s",), writes=("lmax16",))
        bld.op("act", lambda e: e.activation(er_s[:], logit_s[:], AF.Exp),
               reads=("logit_s",), writes=("er_s",))
        bld.op("act", lambda e: e.activation(elmax16[:], lmax16[:], AF.Exp),
               reads=("lmax16",), writes=("elmax16",))
        bld.op("dve", lambda e: e.tensor_reduce(
            esum16[:].rearrange("p (a b) -> p a b", b=1),
            er_s[:].rearrange("p (a b) -> p a b", b=8),
            axis=AX.X, op=ALU.add, opt_input=False, opt_output=False),
            reads=("er_s",), writes=("esum16",))
        bld.op("dve", lambda e: e.reciprocal(esum16[:], esum16[:]),
               reads=("esum16",), writes=("esum16",))
        bld.op("dve", lambda e: e.tensor_tensor(w16[:], elmax16[:], esum16[:], op=ALU.mult),
               reads=("elmax16", "esum16"), writes=("w16",))
        for i in range(16):
            bld.op("dve", lambda e, i=i: e.tensor_scalar(
                sel_s[:, 8 * i:8 * (i + 1)], logit_s[:, 8 * i:8 * (i + 1)],
                lmax16[:, i:i + 1], w16[:, i:i + 1], op0=ALU.is_equal, op1=ALU.mult),
                reads=("logit_s", "lmax16", "w16"), writes=("sel_s",))

        if dbg and p == 0:
            bld.dma("St", lambda e: e.dma_start(out=dbg_d["logit"], in_=logit_s[:]), reads=("logit_s",))
            bld.dma("St", lambda e: e.dma_start(out=dbg_d["sel"], in_=sel_s[:]), reads=("sel_s",))
            bld.dma("St", lambda e: e.dma_start(out=dbg_d["lmax"], in_=lmax16[:]), reads=("lmax16",))
            bld.dma("St", lambda e: e.dma_start(out=dbg_d["w16d"], in_=w16[:]), reads=("w16",))
            dxn = nc.dram_tensor("dbg_xnb", [128, 16384], BF16, kind="ExternalOutput").ap()
            dbg_d["xnTb"] = dxn
            bld.dma("St", lambda e: e.dma_start(out=dxn, in_=xnT), reads=("xnT",))
        for e_i in range(NE):
            buf = e_i % 2
            weT = zb(A_weT[buf], 8192)
            bld.dma("Wg", lambda e, e_i=e_i, buf=buf: e.dma_start(
                out=zb(A_weT[buf], 8192),
                in_=_ap_custom(di["weT"][e_i], [[1024, 128], [131072, 8], [1, 1024]], 0)),
                writes=("weT%d" % buf,))
            bld.dma("Wg", lambda e, e_i=e_i, buf=buf: e.dma_start(
                out=ebrow[0:1, 1024 * buf:1024 * (buf + 1)], in_=di["eb"][e_i].rearrange("(o d) -> o d", o=1)),
                writes=("ebrow%d" % buf,))
            for i in range(16):
                pb, pk = (p01, "p01") if i % 2 == 0 else (p23, "p23")

                def ef(e, i=i, pb=pb, weT=weT, buf=buf):
                    last = None
                    for nb in range(2):
                        for c in range(8):
                            last = e.matmul(pb[:, 512 * nb:512 * (nb + 1)],
                                            xnT[:, 2048 * c + 128 * i:2048 * c + 128 * (i + 1)],
                                            weT[:, 1024 * c + 512 * nb:1024 * c + 512 * nb + 512],
                                            start=(c == 0), stop=False)
                        last = e.matmul(pb[:, 512 * nb:512 * (nb + 1)], onesb[:],
                                        ebrow[0:1, 1024 * buf + 512 * nb:1024 * buf + 512 * (nb + 1)],
                                        start=False, stop=True)
                    return last
                mmg([ef], reads=("xnT", "weT%d" % buf, "onesb", "ebrow%d" % buf), writes=(pk,))
                bld.op("dve", lambda e, i=i, e_i=e_i, pb=pb: e.scalar_tensor_tensor(
                    r_s[:, 1024 * i:1024 * (i + 1)], pb[:, 0:1024],
                    sel_s[:, 8 * i + e_i:8 * i + e_i + 1],
                    r_s[:, 1024 * i:1024 * (i + 1)], op0=ALU.mult, op1=ALU.add),
                    reads=(pk, "sel_s", "r%d" % i), writes=("r%d" % i,))
                if p == 0 and e_i == NE - 1:
                    ln_stats(i)
                    pre_attnb.add(i)

    def attn_b():
        for k in ARENA_B:
            bld.claim(k, ARENA_F)
        A_xnT = 32768
        A_svT = 0
        A_wq = [65536, 71680]
        A_qT = 77824
        A_kT = 81920
        A_P1 = 86016
        A_P3 = 90112
        A_vtokb = 94208
        A_vslot = 98304
        A_ptslot = 99328
        xnT = zb(A_xnT, 16384)
        svT = zb(A_svT, 16384)

        def xn_dst(i):
            return zb(A_xnT, 16384).rearrange("p (c t) -> p c t", c=8)[:, :, 128 * i:128 * (i + 1)]
        ln_pass(list(range(16)), xn_dst, ("xnT",), lambda i: "r%d" % i)

        ptsv = sb_ptsv

        def stageA(hp):
            buf = hp % 2
            wq = zb(A_wq[buf], 3072)
            qT = zb(A_qT, 2048)
            kT = zb(A_kT, 2048)
            vtokb = zb(A_vtokb, 2048)
            vslot = zb(A_vslot, 512)
            for kind in range(3):
                src = _ap_custom(di["wqkv"], [[3072, 128], [393216, 8], [1, 128]],
                                 1024 * kind + 128 * hp)
                bld.dma("Wg", lambda e, buf=buf, src=src, kind=kind: e.dma_start(
                    out=zb(A_wq[buf], 3072).rearrange("p (c f) -> p c f", f=384)[:, :, 128 * kind:128 * (kind + 1)],
                    in_=src), writes=("wq%d" % buf,))
            for kind in range(3):
                for nb in range(4):
                    pb, pk = (p4, "p4") if nb % 2 == 0 else (p5, "p5")

                    def qf(e, kind=kind, nb=nb, pb=pb, wq=wq):
                        last = None
                        for c in range(8):
                            last = e.matmul(pb[:, 0:512],
                                            wq[:, 384 * c + 128 * kind:384 * c + 128 * (kind + 1)],
                                            xnT[:, 2048 * c + 512 * nb:2048 * c + 512 * (nb + 1)],
                                            start=(c == 0), stop=(c == 7))
                        return last
                    mmg([qf], reads=("wq%d" % buf, "xnT"), writes=(pk,))
                    if kind == 0:
                        bld.op("act", lambda e, nb=nb, pb=pb, qT=qT: e.activation(
                            qT[:, 512 * nb:512 * (nb + 1)], pb[:, 0:512], AF.Copy, scale=qsc_t[:]),
                            reads=(pk, "qsc"), writes=("qTb",))
                    elif kind == 1:
                        bld.op("act", lambda e, nb=nb, pb=pb, kT=kT: e.copy(
                            kT[:, 512 * nb:512 * (nb + 1)], pb[:, 0:512]),
                            reads=(pk,), writes=("kTb",))
                    else:
                        bld.op("act", lambda e, pb=pb, vslot=vslot: e.copy(
                            vslot[:], pb[:, 0:512]), reads=(pk,), writes=("vslot_b",))

                        def vtr(e, vslot=vslot):
                            last = None
                            for jj in range(4):
                                last = e.transpose(p6b[:, 128 * jj:128 * (jj + 1)],
                                                   vslot[:, 128 * jj:128 * (jj + 1)], idb[:])
                            return last
                        bld.op("pe", vtr, reads=("vslot_b", "idb"), writes=("p6",))
                        bld.op("act", lambda e, nb=nb, vtokb=vtokb: e.copy(
                            vtokb[:, 512 * nb:512 * (nb + 1)], p6b[:, 0:512]),
                            reads=("p6",), writes=("vtokb",))
            for (srcT, sk, dst, dk) in [(qT, "qTb", qlmT, "qlmT"), (kT, "kTb", klmT, "klmT")]:
                bld.op("dve", lambda e, srcT=srcT: e.tensor_reduce(
                    lm_f[:].rearrange("p (a b) -> p a b", b=1),
                    srcT.rearrange("p (a b) -> p a b", b=32),
                    axis=AX.X, op=ALU.add, opt_input=False, opt_output=False),
                    reads=(sk,), writes=("lm_f",))
                bld.op("dve", lambda e, dst=dst: e.tensor_scalar(
                    dst[:], lm_f[:], 1.0 / 32, None, op0=ALU.mult),
                    reads=("lm_f",), writes=(dk,))

        def pair0(dst_p, c0, lhs, rhs, lk, rk, pk, starts=None):
            def f(e, dst_p=dst_p, c0=c0, lhs=lhs, rhs=rhs):
                last = None
                for hh in range(2):
                    last = e.matmul(dst_p[0:64, c0 + 64 * hh:c0 + 64 * (hh + 1)],
                                    lhs[:, 64 * hh:64 * (hh + 1)],
                                    rhs[:, 64 * hh:64 * (hh + 1)],
                                    start=True, stop=True, tile_position=(0, 0))
                return last
            mmg([f], reads=(lk, rk), writes=(pk,))

        def inv_gen():
            mmg([lambda e: e.matmul(p4[0:64, 0:64], qlmT[0:64, :], klmT[0:64, :],
                                    start=True, stop=True, tile_position=(0, 0))],
                reads=("qlmT", "klmT"), writes=("p4",))
            mmg([lambda e: e.matmul(p5[0:64, 0:64], qlmT[64:128, :], klmT[64:128, :],
                                    start=True, stop=True, tile_position=(64, 0))],
                reads=("qlmT", "klmT"), writes=("p5",))
            yield
            for hh, pb, pk in [(0, p4, "p4"), (1, p5, "p5")]:
                bld.op("act", lambda e, hh=hh, pb=pb: e.activation(
                    er_s[0:64, 64 * hh:64 * (hh + 1)], pb[0:64, 0:64], AF.Exp,
                    accum_out=k2sum[0:64, hh:hh + 1]),
                    reads=(pk,), writes=("er_s", "k2sum"))
            bld.op("dve", lambda e: e.reciprocal(k2rec[0:64, 0:2], k2sum[0:64, 0:2]),
                   reads=("k2sum",), writes=("k2rec",))
            yield
            for hh in range(2):
                bld.op("dve", lambda e, hh=hh: e.tensor_scalar(
                    K2c[:, 64 * hh:64 * (hh + 1)], er_s[0:64, 64 * hh:64 * (hh + 1)],
                    k2rec[0:64, hh:hh + 1], None, op0=ALU.mult),
                    reads=("er_s", "k2rec"), writes=("i_K2",))
            yield
            mmg([lambda e: [e.matmul(p4[0:1, 64 + 64 * hh:128 + 64 * hh],
                                     onesc[0:64, :], K2c[:, 64 * hh:64 * (hh + 1)],
                                     start=True, stop=True, tile_position=(0, 0))
                            for hh in range(2)][-1]],
                reads=("i_K2", "onesc"), writes=("p4",))
            yield
            bld.op("dve", lambda e: e.tensor_copy(denom_s[:], p4[0:1, 64:192]),
                   reads=("p4",), writes=("denom",))
            bld.op("dve", lambda e: e.tensor_reduce(
                dmax_s[:].rearrange("p (a b) -> p a b", b=1),
                denom_s[:].rearrange("p (a b) -> p a b", b=64),
                axis=AX.X, op=ALU.max, opt_input=False, opt_output=False),
                reads=("denom",), writes=("dmax",))
            bld.op("dve", lambda e: e.reciprocal(drec_s[:], dmax_s[:]),
                   reads=("dmax",), writes=("drec",))
            yield
            mmg([lambda e: e.matmul(p4[:, 192:194], ones1f[:], drec_s[:], start=True, stop=True)],
                reads=("drec", "ones1f"), writes=("p4",))
            yield
            bld.op("dve", lambda e: e.tensor_copy(dbc_s[:], p4[:, 192:194]),
                   reads=("p4",), writes=("dbc",))
            bld.op("pe", lambda e: e.transpose(p6b[0:64, 512:576], K2c[:, 0:64], idb[0:64, 0:64],
                                               tile_position=(0, 0)),
                   reads=("i_K2", "idb"), writes=("p6",))
            bld.op("act", lambda e: e.copy(K2Tc[:, 0:64], p6b[0:64, 512:576]),
                   reads=("p6",), writes=("i_K2T",))
            bld.op("pe", lambda e: e.transpose(p7b[0:64, 512:576], K2c[:, 64:128], idb[0:64, 0:64],
                                               tile_position=(0, 0)),
                   reads=("i_K2", "idb"), writes=("p7",))
            bld.op("act", lambda e: e.copy(K2Tc[:, 64:128], p7b[0:64, 512:576]),
                   reads=("p7",), writes=("i_K2T",))
            yield
            for hh in range(2):
                bld.op("dve", lambda e, hh=hh: e.tensor_scalar(
                    Vc[:, 64 * hh:64 * (hh + 1)], K2Tc[:, 64 * hh:64 * (hh + 1)],
                    dbc_s[0:64, hh:hh + 1], None, op0=ALU.mult),
                    reads=("i_K2T", "dbc"), writes=("i_V",))
                bld.op("dve", lambda e, hh=hh: e.tensor_scalar(
                    Vtc[:, 64 * hh:64 * (hh + 1)], K2c[:, 64 * hh:64 * (hh + 1)],
                    dbc_s[0:64, hh:hh + 1], None, op0=ALU.mult),
                    reads=("i_K2", "dbc"), writes=("i_Vt",))
            yield
            pair0(p4, 0, K2Tc, Vc, "i_K2T", "i_V", "p4")
            yield
            bld.op("dve", lambda e: e.tensor_copy(Ac[:], p4[0:64, 0:128]),
                   reads=("p4",), writes=("i_A",))
            bld.op("dve", lambda e: e.scalar_tensor_tensor(
                U1c[:], p4[0:64, 0:128], -1.0, c7b2[:], op0=ALU.mult, op1=ALU.add),
                reads=("p4", "c7"), writes=("i_U1",))
            yield
            for it in range(6):
                pair0(p4, 128, Ac, U1c, "i_A", "i_U1", "p4")
                yield
                bld.op("dve", lambda e: e.scalar_tensor_tensor(
                    U3c[:], p4[0:64, 128:256], -1.0, c15b2[:], op0=ALU.mult, op1=ALU.add),
                    reads=("p4", "c15"), writes=("i_U3",))
                yield
                pair0(p4, 256, Ac, U3c, "i_A", "i_U3", "p4")
                yield
                bld.op("dve", lambda e: e.scalar_tensor_tensor(
                    U5c[:], p4[0:64, 256:384], -1.0, c13b2[:], op0=ALU.mult, op1=ALU.add),
                    reads=("p4", "c13"), writes=("i_U5",))
                yield
                pair0(p4, 384, Vtc, U5c, "i_Vt", "i_U5", "p4")
                pair0(p5, 128, U5c, Vtc, "i_U5", "i_Vt", "p5")
                pair0(p5, 256, Ac, U5c, "i_A", "i_U5", "p5")
                yield
                bld.op("dve", lambda e: e.tensor_scalar(
                    Vc[:], p4[0:64, 384:512], 0.25, None, op0=ALU.mult),
                    reads=("p4",), writes=("i_V",))
                bld.op("dve", lambda e: e.tensor_scalar(
                    Vtc[:], p5[0:64, 128:256], 0.25, None, op0=ALU.mult),
                    reads=("p5",), writes=("i_Vt",))
                if it < 5:
                    bld.op("dve", lambda e: e.tensor_scalar(
                        Ac[:], p5[0:64, 256:384], 0.25, None, op0=ALU.mult),
                        reads=("p5",), writes=("i_A",))
                    bld.op("dve", lambda e: e.scalar_tensor_tensor(
                        U1c[:], p5[0:64, 256:384], -0.25, c7b2[:], op0=ALU.mult, op1=ALU.add),
                        reads=("p5", "c7"), writes=("i_U1",))
                yield

        def stageB(hp):
            qT = zb(A_qT, 2048)
            kT = zb(A_kT, 2048)
            P1 = zb(A_P1, 2048)
            P3 = zb(A_P3, 2048)
            vtokb = zb(A_vtokb, 2048)
            ptslot = zb(A_ptslot, 512)
            g = inv_gen()

            def adv(n=2):
                for _ in range(n):
                    if next(g, "done") == "done":
                        return

            adv(1)
            for j4 in range(4):
                def k1f(e, j4=j4, hh=0, pb=p7):
                    last = None
                    for jj in range(4):
                        j = 4 * j4 + jj
                        last = e.matmul(pb[:, 64 * jj:64 * (jj + 1)],
                                        qT[64 * hh:64 * (hh + 1), 128 * j:128 * (j + 1)],
                                        klmT[64 * hh:64 * (hh + 1), :],
                                        start=True, stop=True, tile_position=(64 * hh, 0))
                    return last
                mmg([lambda e, j4=j4: k1f(e, j4, 0, p7)], reads=("qTb", "klmT"), writes=("p7",))
                mmg([lambda e, j4=j4: k1f(e, j4, 1, p6)], reads=("qTb", "klmT"), writes=("p6",))
                for hh, pb, pk in [(0, p7, "p7"), (1, p6, "p6")]:
                    dst = P1[:, 512 * j4:512 * (j4 + 1)].rearrange(
                        "p (j h m) -> p j h m", h=2, m=64)[:, :, hh, :]
                    bld.op("act", lambda e, pb=pb, dst=dst: e.activation(
                        dst, pb[:, 0:256].rearrange("p (j m) -> p j m", m=64), AF.Exp),
                        reads=(pk,), writes=("P1",))
                bld.op("dve", lambda e, j4=j4, P1=P1: e.tensor_reduce(
                    rsum32[:, 8 * j4:8 * (j4 + 1)].rearrange("p (a b) -> p a b", b=1),
                    P1[:, 512 * j4:512 * (j4 + 1)].rearrange("p (a b) -> p a b", b=64),
                    axis=AX.X, op=ALU.add, opt_input=False, opt_output=False),
                    reads=("P1",), writes=("rsum32",))
                adv()
            bld.op("dve", lambda e: e.reciprocal(rrec32[:], rsum32[:]),
                   reads=("rsum32",), writes=("rrec32",))
            adv()
            for j in range(16):
                for hh in range(2):
                    bld.op("dve", lambda e, j=j, hh=hh, P1=P1: e.tensor_scalar(
                        P1[:, 128 * j + 64 * hh:128 * j + 64 * (hh + 1)],
                        P1[:, 128 * j + 64 * hh:128 * j + 64 * (hh + 1)],
                        rrec32[:, 2 * j + hh:2 * j + hh + 1], None, op0=ALU.mult),
                        reads=("P1", "rrec32"), writes=("P1",))
                if j % 4 == 3:
                    adv()
            for half in range(2):
                for hh, pb, pk in [(0, p01, "p01"), (1, p23, "p23")]:
                    def k3f(e, half=half, hh=hh, pb=pb):
                        last = None
                        for nb in range(2):
                            last = e.matmul(pb[64 * hh:64 * (hh + 1), 512 * nb:512 * (nb + 1)],
                                            qlmT[64 * hh:64 * (hh + 1), :],
                                            kT[64 * hh:64 * (hh + 1), 1024 * half + 512 * nb:1024 * half + 512 * (nb + 1)],
                                            start=True, stop=True, tile_position=(64 * hh, 64 * hh))
                        return last
                    mmg([k3f], reads=("qlmT", "kTb"), writes=(pk,))
                    bld.op("act", lambda e, half=half, hh=hh, pb=pb, P3=P3: e.activation(
                        P3[64 * hh:64 * (hh + 1), 1024 * half:1024 * (half + 1)],
                        pb[64 * hh:64 * (hh + 1), 0:1024], AF.Exp,
                        accum_out=s3sum[64 * hh:64 * (hh + 1), half:half + 1]),
                        reads=(pk,), writes=("P3", "s3sum"))
                adv()
            bld.op("dve", lambda e: e.tensor_tensor(s3tot[:], s3sum[:, 0:1], s3sum[:, 1:2], op=ALU.add),
                   reads=("s3sum",), writes=("s3tot",))
            bld.op("dve", lambda e: e.reciprocal(s3rec[:], s3tot[:]),
                   reads=("s3tot",), writes=("s3rec",))
            bld.op("dve", lambda e, P3=P3: e.tensor_scalar(P3[:], P3[:], s3rec[:], None, op0=ALU.mult),
                   reads=("P3", "s3rec"), writes=("P3",))
            adv()
            for j4 in range(4):
                def p3t(e, j4=j4, P3=P3):
                    last = None
                    for jj in range(4):
                        last = e.transpose(p6b[:, 128 * jj:128 * (jj + 1)],
                                           P3[:, 128 * (4 * j4 + jj):128 * (4 * j4 + jj + 1)], idb[:])
                    return last
                bld.op("pe", p3t, reads=("P3", "idb"), writes=("p6",))
                bld.op("act", lambda e, ptslot=ptslot: e.copy(ptslot[:], p6b[:, 0:512]),
                       reads=("p6",), writes=("ptslot",))

                def k3vf(e, j4=j4, ptslot=ptslot, vtokb=vtokb):
                    last = None
                    for jj in range(4):
                        j = 4 * j4 + jj
                        for hh in range(2):
                            last = e.matmul(p23[0:64, 64 * hh:64 * (hh + 1)],
                                            ptslot[:, 128 * jj + 64 * hh:128 * jj + 64 * (hh + 1)],
                                            vtokb[:, 128 * j + 64 * hh:128 * j + 64 * (hh + 1)],
                                            start=(j == 0 and hh == 0), stop=(j == 15 and hh == 1),
                                            tile_position=(0, 0))
                    return last
                mmg([k3vf], reads=("ptslot", "vtokb"), writes=("p23",))
                adv()
            bld.op("dve", lambda e: e.tensor_copy(k3vc[:], p23[0:64, 0:128]),
                   reads=("p23",), writes=("i_k3v",))
            adv(99)

        def stageC(hp):
            P1 = zb(A_P1, 2048)
            pair0(p4, 0, Vtc, k3vc, "i_Vt", "i_k3v", "p4")
            bld.op("dve", lambda e: e.tensor_copy(Zc[:], p4[0:64, 0:128]),
                   reads=("p4",), writes=("i_Z",))
            for j4 in range(4):
                def p1t(e, j4=j4, P1=P1):
                    last = None
                    for jj in range(4):
                        for hh in range(2):
                            last = e.transpose(
                                p6b[0:64, 256 * jj + 128 * hh:256 * jj + 128 * (hh + 1)],
                                P1[:, 128 * (4 * j4 + jj) + 64 * hh:128 * (4 * j4 + jj) + 64 * (hh + 1)],
                                idb[:], tile_position=(0, 0))
                    return last
                bld.op("pe", p1t, reads=("P1", "idb"), writes=("p6",))
                bld.op("act", lambda e: e.copy(ptsv[0:64, 0:1024], p6b[0:64, 0:1024]),
                       reads=("p6",), writes=("ptsv",))

                def svf(e, j4=j4):
                    last = None
                    for jj in range(4):
                        for hh in range(2):
                            last = e.matmul(p23[64 * hh:64 * (hh + 1), 512 * hh + 128 * jj:512 * hh + 128 * (jj + 1)],
                                            Zc[:, 64 * hh:64 * (hh + 1)],
                                            ptsv[0:64, 256 * jj + 128 * hh:256 * jj + 128 * (hh + 1)],
                                            start=True, stop=True, tile_position=(0, 64 * hh))
                    return last
                mmg([svf], reads=("i_Z", "ptsv"), writes=("p23",))
                for hh in range(2):
                    bld.op("dve", lambda e, j4=j4, hp=hp, hh=hh: e.tensor_copy(
                        svT[64 * hh:64 * (hh + 1), 2048 * hp + 512 * j4:2048 * hp + 512 * (j4 + 1)],
                        p23[64 * hh:64 * (hh + 1), 512 * hh:512 * hh + 512]),
                        reads=("p23",), writes=("svT",))

        if b_hp > 0:
            stageA(0)
            for hp in range(b_hp):
                stageB(hp)
                if hp + 1 < b_hp:
                    stageA(hp + 1)
                stageC(hp)

        if b_stage < 11:
            return
        A_wpb = 77824
        wpb_s = zb(A_wpb, 8192)
        bld.claim("wpb", ["qTb", "kTb", "P1", "P3"])
        bld.dma("Wg", lambda e: e.dma_start(
            out=zb(A_wpb, 8192), in_=_ap_custom(di["wpb"], [[1024, 128], [131072, 8], [1, 1024]], 0)),
            writes=("wpb",))
        for i in range(16):
            pb, pk = (p01, "p01") if i % 2 == 0 else (p23, "p23")

            def bpf(e, i=i, pb=pb):
                last = None
                for nb in range(2):
                    for c in range(8):
                        last = e.matmul(pb[:, 512 * nb:512 * (nb + 1)],
                                        svT[:, 2048 * c + 128 * i:2048 * c + 128 * (i + 1)],
                                        wpb_s[:, 1024 * c + 512 * nb:1024 * c + 512 * nb + 512],
                                        start=(c == 0), stop=(c == 7))
                return last
            mmg([bpf], reads=("svT", "wpb"), writes=(pk,))
            bld.op("dve", lambda e, i=i, pb=pb: e.tensor_tensor(
                r_s[:, 1024 * i:1024 * (i + 1)], pb[:, 0:1024],
                r_s[:, 1024 * i:1024 * (i + 1)], op=ALU.add),
                reads=(pk, "r%d" % i), writes=("r%d" % i,))
            ln_stats(i)
            pre_ffn1.add(i)

    def tap(name):
        if name in tap_d:
            bld.dma("St", lambda e: e.dma_start(
                out=_ap_custom(tap_d[name], [[1024, 128], [131072, 16], [1, 1024]], 0), in_=r_s[:]),
                reads=tuple("r%d" % i for i in range(16)))

    loads()
    if 'attn_a' in phases:
        attn_a()
        tap("attn_a")
    if 'ffn0' in phases:
        ffn(0)
        tap("ffn1")
    if 'attn_b' in phases:
        attn_b()
        tap("attn_b")
    if 'ffn1' in phases:
        ffn(1)
    bld.dma("St", lambda e: e.dma_start(
        out=_ap_custom(out_d, [[1024, 128], [131072, 16], [1, 1024]], 0), in_=r_s[:]),
        reads=tuple("r%d" % i for i in range(16)))
    bld.finish()
    return nc, stack


def _in_maps(inputs, n_cores=8):
    cons = _host_consts()
    w_attn = np.asarray(inputs["L0_attn_a_w_attn"], np.float32)
    wq = w_attn[0:C].reshape(C // 2, 2, C)[:, ::-1, :].reshape(C, C)
    waT = np.ascontiguousarray(np.concatenate([wq, w_attn[C:]], 0).T)
    b_attn = np.asarray(inputs["L0_attn_a_b_attn"], np.float32).copy()
    b_attn[0:C] = b_attn[0:C].reshape(C // 2, 2)[:, ::-1].reshape(C)
    base = {
        "waT": waT,
        "b_attn": b_attn,
        "wpaT": np.ascontiguousarray(np.asarray(inputs["L0_attn_a_w_proj"], np.float32).T),
        "b_proj": np.asarray(inputs["L0_attn_a_b_proj"], np.float32),
        "wassT": np.ascontiguousarray(np.asarray(inputs["L0_attn_a_w_assign"], np.float32).T),
        "wqkv": np.asarray(inputs["L0_attn_b_w_qkv"], np.float32),
        "wpb": np.asarray(inputs["L0_attn_b_w_proj"], np.float32),
        "rwT": np.ascontiguousarray(np.asarray(inputs["L0_ffn_router_w"], np.float32).T),
        "rb": np.asarray(inputs["L0_ffn_router_b"], np.float32),
        "weT": np.ascontiguousarray(np.asarray(inputs["L0_ffn_expert_w"], np.float32).transpose(0, 2, 1)),
        "eb": np.asarray(inputs["L0_ffn_expert_b"], np.float32),
    }
    base.update(cons)
    x = np.asarray(inputs["L0_attn_a_x"], np.float32)
    maps = []
    for cid in range(n_cores):
        m = dict(base)
        m["x"] = np.ascontiguousarray(x[cid])
        maps.append(m)
    return maps


def kernel(**inputs):
    nc, stack = build()
    try:
        maps = _in_maps(inputs)
        res = run_bass_kernel_spmd(nc, maps, core_ids=list(range(8)))
    finally:
        stack.__exit__(None, None, None)
    return np.stack([np.asarray(r["out"], np.float32) for r in res.results], axis=0)
